# version 19 (final)
# baseline (speedup 1.0000x reference)
import numpy as np
from contextlib import ExitStack

import concourse.bass as bass
import concourse.mybir as mybir
from concourse.bass_utils import run_bass_kernel_spmd

F32 = mybir.dt.float32
BF16 = mybir.dt.bfloat16
F16 = mybir.dt.float16
I8 = mybir.dt.int8
U8 = mybir.dt.uint8
AF = mybir.ActivationFunctionType
ALU = mybir.AluOpType
AXX = mybir.AxisListType.X

H = 2048
T = 2048
NH = 16
NTC = 16
NHC = 16
II = 5632
NIC = 44
CORES = 8
GU_COLS = 11264 // CORES
GU_L = GU_COLS // 128
DN_COLS = 2048 // CORES
DN_U = 2816 // 128
LNEPS = 1e-5
SQH = (1.0 / 2048.0) ** 0.5
GEPS = 1e-5
FEPS = 1e-6


def _perm(nchunks):
    p = np.empty(nchunks * 128, dtype=np.int64)
    for k in range(nchunks):
        u, j = k // 2, k % 2
        p[k * 128:(k + 1) * 128] = 256 * u + 2 * np.arange(128) + j
    return p

PERM = _perm(NHC)
PERM2 = _perm(NIC)


class Sem:
    def __init__(self, h):
        self.h = h
        self.n = 0

    def inc(self, k=1):
        self.n += k
        return self.n


class Lanes:

    def __init__(self, mk, n, name):
        self.sems = [mk(f"{name}{i}") for i in range(n)]
        self.n = n
        self.k = 0

    def nxt(self):
        lane = self.sems[self.k % self.n]
        self.k += 1
        return lane


class Plan:
    ENGINES = ("sync", "scalar", "vector", "tensor", "gpsimd")

    def __init__(self):
        self.ops = {e: [] for e in self.ENGINES}

    def on(self, eng, fn):
        self.ops[eng].append(fn)

    def dma(self, eng, sem, out, in_, **kw):
        if isinstance(sem, Lanes):
            sem = sem.nxt()
        t = sem.inc(16)
        self.on(eng, lambda e, out=out, in_=in_, kw=kw, s=sem.h: e.dma_start(out=out, in_=in_, **kw).then_inc(s, 16))
        return (sem, t)

    def waitt(self, eng, tok):
        sem, t = tok
        self.on(eng, lambda e, s=sem.h, t=t: e.wait_ge(s, t))

    def wait(self, eng, sem, tgt):
        if isinstance(tgt, tuple):
            sem, tgt = tgt
        if tgt and tgt > 0:
            self.on(eng, lambda e, s=sem.h, t=tgt: e.wait_ge(s, t))

    def waitl(self, eng, sem, holder, idx):
        def go(e, s=sem.h, holder=holder, idx=idx):
            t = holder[idx]
            if isinstance(t, tuple):
                return e.wait_ge(t[0].h, t[1])
            if t and t > 0:
                return e.wait_ge(s, t)
            return e.nop()
        self.on(eng, go)

    def run(self, eng, fn, sem=None, k=1):
        if sem is None:
            self.on(eng, fn)
            return None
        t = sem.inc(k)
        self.on(eng, lambda e, fn=fn, s=sem.h, k=k: fn(e).then_inc(s, k))
        return t


def build(es, debug=()):
    nc = bass.Bass("TRN2", target_bir_lowering=False)
    dbg = set(debug)

    d_x = nc.dram_tensor("x", [T, H], F32, kind="ExternalInput")
    d_wq = nc.dram_tensor("wq", [H, H], F32, kind="ExternalInput")
    d_wk = nc.dram_tensor("wk", [H, H], F32, kind="ExternalInput")
    d_wv = nc.dram_tensor("wv", [H, H], F32, kind="ExternalInput")
    d_wo = nc.dram_tensor("wo", [H, H], F32, kind="ExternalInput")
    d_wg1 = nc.dram_tensor("wg1", [H, 128], F32, kind="ExternalInput")
    d_wg2 = nc.dram_tensor("wg2", [128, H], F32, kind="ExternalInput")
    d_nw = nc.dram_tensor("nw", [H, 1], F32, kind="ExternalInput")
    d_ld = nc.dram_tensor("ldec", [1, NH], F32, kind="ExternalInput")
    d_gap = nc.dram_tensor("gap", [1, CORES], F32, kind="ExternalInput")
    d_pkgu = nc.dram_tensor("pkgu", [GU_L, 128, 1024], U8, kind="ExternalInput")
    d_scgu = nc.dram_tensor("scgu", [GU_L, 128, 1024], F16, kind="ExternalInput")
    d_pkdn = nc.dram_tensor("pkdn", [DN_U, 128, DN_COLS], U8, kind="ExternalInput")
    d_scdn = nc.dram_tensor("scdn", [DN_U, 128, DN_COLS], F16, kind="ExternalInput")
    d_out = nc.dram_tensor("out", [T, H], F32, kind="ExternalOutput")

    d_ln1g = nc.dram_tensor("ln1g", [NTC * NHC * 128, 128], BF16)
    d_q = nc.dram_tensor("qd", [NTC * NHC * 128, 128], BF16)
    d_oi = nc.dram_tensor("oid", [NTC, 128, H], BF16)
    d_a = nc.dram_tensor("ad", [NTC, 128, H], BF16)
    d_mid = nc.dram_tensor("midd", [128, H], BF16)
    d_y1 = nc.dram_tensor("y1d", [T, H], F32)
    d_ln2 = nc.dram_tensor("ln2d", [T, H], BF16)
    d_agt_i = nc.dram_tensor("agti", [128, H], F32)
    d_agt_o = nc.dram_tensor("agto", [CORES * 128, H], F32, addr_space="Shared")
    d_aggu_i = nc.dram_tensor("aggui", [128, GU_L * 2048], BF16)
    d_aggu_o = nc.dram_tensor("agguo", [CORES * 128, GU_L * 2048], BF16, addr_space="Shared")
    d_agdn_i = nc.dram_tensor("agdni", [128, NIC * DN_COLS], BF16)
    d_agdn_o = nc.dram_tensor("agdno", [CORES * 128, NIC * DN_COLS], BF16, addr_space="Shared")

    d_dbg = {}
    def dbgout(name, shape, dt=F32):
        if name in dbg:
            if name not in d_dbg:
                d_dbg[name] = nc.dram_tensor("dbg_" + name, shape, dt, kind="ExternalOutput")
            return d_dbg[name]
        return None

    sbt = lambda n_, sh, dt: es.enter_context(nc.sbuf_tensor(n_, sh, dt))
    ident = sbt("ident", [128, 128], BF16)
    iorow = sbt("iorow", [128, 128], F32)
    iocol = sbt("iocol", [128, 1], F32)
    ld_rep = sbt("ldrep", [128, NH], F32)
    ldneg = sbt("ldneg", [128, NH], F32)
    qdecs = sbt("qdecs", [128, NH], F32)
    kdecs = sbt("kdecs", [128, NH], F32)
    cds = sbt("cds", [128, NH], F32)
    gapr = sbt("gapr", [128, CORES], F32)
    coefr = sbt("coefr", [128, NH], F32)
    tmask = sbt("tmask", [128, 128], F32)
    dargm = sbt("dargm", [128, 128], F32)
    dtmp = sbt("dtmp", [128, 128], F32)
    stat = sbt("stat", [128, 24], F32)
    bstatr = [sbt("bstat0", [128, 16], F32), sbt("bstat1", [128, 16], F32)]
    systat = sbt("systat", [128, 24], F32)
    nwcol = sbt("nwcol", [128, NHC], F32)
    dmsk = sbt("dmsk", [128, NH * 128], BF16)
    cepsA = sbt("cepsA", [128, 1], F32)
    dumf = sbt("dumf", [128, 2], F32)
    kdk8 = sbt("kdk8", [128, 1024], BF16)
    kfm8 = sbt("kfm8", [128, 1024], BF16)
    sdb8 = sbt("sdb8", [128, 1024], BF16)
    sdt8 = sbt("sdt8", [128, 1024], BF16)
    ktcB = sbt("ktcB", [128, H], BF16)[:, :]
    vtcB = sbt("vtcB", [128, H], BF16)[:, :]

    KB = 1024
    R1, R2, R3 = 0, 64 * KB, 128 * KB
    ARN = 182 * KB
    arena = sbt("arena", [128, ARN], U8)

    def carve(off, nelem, dt):
        sz = {F32: 4, BF16: 2, F16: 2, I8: 1, U8: 1}[dt]
        ap = arena[:, off:off + nelem * sz].bitcast(dt)
        return ap, off + nelem * sz

    def carve_list(off, n, nelem, dt):
        out = []
        for _ in range(n):
            ap, off = carve(off, nelem, dt)
            out.append(ap)
        return out, off

    ln1f, o1 = carve_list(R1, NHC, T, BF16)
    assert o1 <= R2
    wv_sb, _ = carve_list(R1, NHC, H, BF16)
    wo_sb = wv_sb
    ysb, o = carve_list(R1, 8, H, BF16)
    y1rb, o = carve(o, H, F32)
    wdnb, o = carve_list(o, 2, 11 * DN_COLS, BF16)
    glb, o = carve(o, 1024, BF16)
    glf = glb.bitcast(F32)
    hact_r1_base = o
    assert hact_r1_base + 5 * 2 * KB <= R2, hact_r1_base

    o = R2
    gub, o = carve_list(o, GU_L, 1024, U8)
    gscr, o = carve_list(o, GU_L, 1024, F16)
    gdec, o = carve_list(o, 4, 2048, BF16)
    dvi, o = carve_list(o, 3, 1024, I8)
    dvf, o = carve_list(o, 2, 1024, F32)
    assert o <= R3, o
    dnbb = gub[0].tensor if False else None
    dnbb, _ = carve(R2, DN_U * DN_COLS, U8)
    dnsr2, _ = carve(R2 + GU_L * KB, DN_U * DN_COLS, F16)
    dnstg = [gdec[0], gdec[1]]
    wk_sb, _ = carve_list(R2, NHC, H, BF16)
    o = R2
    sinb, o = carve(o, H, F32)
    sbfb, o = carve(o, H, BF16)
    oitcr, o = carve_list(o, 2, H, BF16)
    a2tcr, o = carve_list(o, 2, H, BF16)
    onbr, o = carve_list(o, 2, H, BF16)
    ontbb, o = carve(o, H, BF16)
    aof0, o = carve(o, H, F32)
    bln20, o = carve(o, H, BF16)
    gsigb, o = carve(o, H, BF16)
    peerb, o = carve(o, H, BF16)
    wg2, o = carve(o, H, BF16)
    assert o <= R3, o
    o = R2
    ln2f, o = carve_list(o, NHC, 1024, BF16)
    wgub, o = carve_list(o, 2, 2048, BF16)
    hact_r2_base = o
    assert hact_r2_base + 12 * 2 * KB <= R3

    o = R3
    xb2, o = carve_list(o, 2, H, F32)
    ln1b, o = carve(o, H, BF16)
    o = R3
    wqcolw, o = carve_list(o, 2, H, BF16)
    wg1, o = carve(o, H, BF16)
    qout, o = carve_list(o, 4, 512, BF16)
    wqf32, o = carve_list(o, 2, H, F32)
    o = R3
    ln1tc, o = carve(o, H, BF16)
    ktc, o = carve(o, H, BF16)
    vtc, o = carve(o, H, BF16)
    qtc2, o = carve(o, 2 * H, BF16)
    otc, o = carve(o, H, BF16)
    atc, o = carve(o, H, BF16)
    tstate, o = carve(o, H, F32)
    midb, o = carve(o, H, BF16)
    assert o <= R3 + 48 * KB, o
    xtc2r = [carve(R3 + 28 * KB, H, F32)[0], carve(R3, H, F32)[0]]
    aofr = [aof0, carve(R3 + 40 * KB, H, F32)[0]]
    bln2r = [bln20, carve(R3 + 48 * KB, H, BF16)[0]]
    qtc = [qtc2[:, 0:H], qtc2[:, H:2 * H]]
    hact = []
    for i in range(27):
        apx, _ = carve(R3 + i * 2 * KB, 1024, BF16)
        hact.append(apx)
    for i in range(12):
        apx, _ = carve(hact_r2_base + i * 2 * KB, 1024, BF16)
        hact.append(apx)
    for i in range(5):
        apx, _ = carve(hact_r1_base + i * 2 * KB, 1024, BF16)
        hact.append(apx)

    psb = [es.enter_context(nc.psum_tensor(f"pb{i}", [128, 512], F32)) for i in range(8)]

    mksem = lambda n_: Sem(es.enter_context(nc.semaphore(n_)))
    qS = mksem("qS")
    qG = mksem("qG")
    qA = mksem("qA")
    pe = mksem("pe")
    dv = mksem("dv")
    ac = mksem("ac")
    gp = mksem("gp")
    tokfull = lambda sm: (sm, sm.n)
    qLD = mksem("qLD"); qNW = mksem("qNW")
    qX = Lanes(mksem, 2, "qX")
    qQSP = Lanes(mksem, 4, "qQSP")
    qLNF = mksem("qLNF")
    qL1 = mksem("qL1")
    qQT = Lanes(mksem, 2, "qQT")
    qOSP = mksem("qOSP"); qASP = mksem("qASP"); qTSPs = mksem("qTSPs")
    qOI = Lanes(mksem, 2, "qOI"); qA2 = Lanes(mksem, 2, "qA2"); qX2 = Lanes(mksem, 2, "qX2")
    qY1S = mksem("qY1S"); qL2Sa = mksem("qL2Sa"); qL2Sb = mksem("qL2Sb")
    qL2F = mksem("qL2F"); qY1R = mksem("qY1R"); qPEER = mksem("qPEER")
    qWQ = Lanes(mksem, 2, "qWQ"); qWG1 = mksem("qWG1"); qWG2 = mksem("qWG2")
    qWK = mksem("qWK"); qWV = mksem("qWV"); qWO = mksem("qWO")
    qGUB = mksem("qGUB"); qDNB = mksem("qDNB")
    qGSC = Lanes(mksem, 4, "qGSC"); qDNS = Lanes(mksem, 4, "qDNS")
    qGSP = Lanes(mksem, 4, "qGSP"); qDSP = Lanes(mksem, 2, "qDSP")
    qWGU = Lanes(mksem, 2, "qWGU"); qWDN = Lanes(mksem, 2, "qWDN")

    P = Plan()
    bf = lambda ap: ap.bitcast(BF16)

    for l in range(GU_L):
        P.dma("scalar", qGUB, gub[l], d_pkgu[l])
    for l in range(GU_L):
        P.dma("scalar", qGUB, gscr[l], d_scgu[l])
    t_gu_in = tokfull(qGUB)

    P.run("gpsimd", lambda g: g.iota(iocol[:, :], [[1, 1]], channel_multiplier=1, allow_small_or_imprecise_dtypes=True))
    P.run("gpsimd", lambda g: g.iota(iorow[:, :], [[1, 128]], channel_multiplier=0, allow_small_or_imprecise_dtypes=True), gp)
    t_iota = gp.n
    t_ln1tm = [0] * NTC
    t_qgrp_pe = [0] * NHC

    P.dma("sync", qLD, ld_rep[:, :], bass.AP(d_ld, 0, [[0, 128], [1, NH]]))
    P.dma("sync", qLD, gapr[:, :], bass.AP(d_gap, 0, [[0, 128], [1, CORES]]))
    t_ldrep = tokfull(qLD)
    for ec in range(NHC):
        P.dma("sync", qNW, nwcol[:, ec:ec + 1], d_nw[ec * 128:(ec + 1) * 128, :])
    t_nw = tokfull(qNW)

    P.wait("vector", gp, t_iota)
    P.run("vector", lambda v: v.tensor_scalar(out=ident[:, :], in0=iorow[:, :], scalar1=iocol[:, 0:1], scalar2=None, op0=ALU.is_equal), dv)
    t_ident = dv.n
    P.run("vector", lambda v: v.memset(cepsA[:, :], LNEPS), dv)
    t_ceps = dv.n
    P.wait("vector", qS, t_ldrep)
    P.run("vector", lambda v: v.tensor_scalar_mul(out=ldneg[:, :], in0=ld_rep[:, :], scalar1=-1.0))
    P.run("vector", lambda v: v.tensor_scalar(out=tmask[:, :], in0=iorow[:, :], scalar1=iocol[:, 0:1], scalar2=None, op0=ALU.subtract))
    P.run("vector", lambda v: v.tensor_scalar_min(out=dargm[:, :], in0=tmask[:, :], scalar1=0.0))
    P.run("vector", lambda v: v.tensor_scalar(out=tmask[:, :], in0=tmask[:, :], scalar1=0.0, scalar2=None, op0=ALU.is_le), dv)
    t_dargs = dv.n

    P.wait("scalar", qS, t_ldrep)
    P.wait("scalar", gp, t_iota)
    P.run("scalar", lambda s: s.activation(out=stat[:, 0:1], in_=iocol[:, 0:1], func=AF.Copy, bias=1.0))
    P.run("scalar", lambda s: s.activation(out=stat[:, 1:2], in_=iocol[:, 0:1], func=AF.Copy, scale=-1.0, bias=127.0))
    for h in range(NH):
        P.run("scalar", lambda s, h=h: s.activation(out=qdecs[:, h:h + 1], in_=stat[:, 0:1], func=AF.Exp, scale=ld_rep[:, h:h + 1]))
        P.run("scalar", lambda s, h=h: s.activation(out=kdecs[:, h:h + 1], in_=stat[:, 1:2], func=AF.Exp, scale=ld_rep[:, h:h + 1]))
        P.run("scalar", lambda s, h=h: s.activation(out=cds[:, h:h + 1], in_=ld_rep[:, h:h + 1], func=AF.Exp, scale=128.0))
    P.run("scalar", lambda s: s.nop(), ac)
    t_decay = ac.n
    P.wait("scalar", dv, t_dargs)
    t_dmh = 0
    for h in range(NH):
        if t_dmh:
            P.wait("scalar", dv, t_dmh)
        P.run("scalar", lambda s, h=h: s.activation(out=dtmp[:, :], in_=dargm[:, :], func=AF.Exp, scale=ldneg[:, h:h + 1]), ac)
        th = ac.n
        P.wait("vector", ac, th)
        P.run("vector", lambda v, h=h: v.tensor_mul(out=dmsk[:, h * 128:(h + 1) * 128], in0=dtmp[:, :], in1=tmask[:, :]), dv)
        t_dmh = dv.n
    t_dmask = dv.n

    def dec_ops(e, src3, n8, m8, v8, f1, f2, scl3, dst3, hi):
        if hi:
            e.tensor_scalar(out=m8, in0=src3, scalar1=4, scalar2=7, op0=ALU.logical_shift_right, op1=ALU.bitwise_and)
            e.tensor_scalar(out=n8, in0=src3, scalar1=4, scalar2=8, op0=ALU.logical_shift_right, op1=ALU.bitwise_and)
        else:
            e.tensor_scalar(out=m8, in0=src3, scalar1=7, scalar2=None, op0=ALU.bitwise_and)
            e.tensor_scalar(out=n8, in0=src3, scalar1=8, scalar2=None, op0=ALU.bitwise_and)
        e.tensor_scalar(out=v8, in0=m8, scalar1=2, scalar2=-4, op0=ALU.mult, op1=ALU.add)
        e.tensor_tensor(out=v8, in0=m8, in1=v8, op=ALU.max)
        e.tensor_scalar(out=m8, in0=m8, scalar1=4, scalar2=-16, op0=ALU.mult, op1=ALU.add)
        e.tensor_tensor(out=v8, in0=v8, in1=m8, op=ALU.max)
        e.tensor_scalar(out=f1, in0=n8, scalar1=-0.125, scalar2=0.5, op0=ALU.mult, op1=ALU.add)
        e.tensor_tensor(out=f2, in0=v8, in1=f1, op=ALU.mult)
        return e.tensor_tensor(out=dst3, in0=f2, in1=scl3, op=ALU.mult)

    def r3(ap, c):
        return ap.rearrange("p (u c) -> p u c", c=c)

    t_glo = [0] * GU_L
    t_ghi = [0] * GU_L
    t_gsp = [0] * (GU_L + 1)

    def emit_gu_l(l):
        src = r3(gub[l].bitcast(I8), 128)
        scl3 = r3(gscr[l], 128)
        ob = gdec[l % 4]
        dlo = r3(ob, 256)[:, :, 0:128]
        dhi = r3(ob, 256)[:, :, 128:256]
        P.wait("vector", qA, t_gu_in)
        if l >= 4:
            P.wait("vector", qG, t_gsp[l - 4])
        P.run("vector", lambda v, src=src, scl3=scl3, dlo=dlo: dec_ops(
            v, src, r3(dvi[0], 128), r3(dvi[1], 128), r3(dvi[2], 128), r3(dvf[0], 128), r3(dvf[1], 128), scl3, dlo, False), dv)
        t_glo[l] = dv.n
        P.run("vector", lambda v, src=src, scl3=scl3, dhi=dhi: dec_ops(
            v, src, r3(dvi[0], 128), r3(dvi[1], 128), r3(dvi[2], 128), r3(dvf[0], 128), r3(dvf[1], 128), scl3, dhi, True), dv)
        t_ghi[l] = dv.n

    d_ln1dbg = dbgout("ln1", [T, H], BF16)
    t_x = [0] * NTC
    t_tr_pe = [0] * NTC
    t_sqL = [0] * NTC
    prev2 = [0, 0]
    for tc in range(NTC):
        if tc >= 2:
            P.wait("sync", dv, t_ln1tm[tc - 2])
            P.wait("sync", ac, t_sqL[tc - 2])
        t_x[tc] = P.dma("sync", qX, xb2[tc % 2], d_x[tc * 128:(tc + 1) * 128, :])
        xb = xb2[tc % 2]
        P.wait("vector", qS, t_x[tc])
        P.run("vector", lambda v, xb=xb: v.tensor_reduce(out=stat[:, 2:3], in_=xb, axis=AXX, op=ALU.add), dv)
        P.wait("scalar", qS, t_x[tc])
        if tc >= 1:
            P.wait("scalar", pe, t_tr_pe[tc - 1])
        P.run("scalar", lambda s, xb=xb: s.activation(out=ln1b[:, 0:1024], in_=xb[:, 0:1024], func=AF.Square, accum_out=stat[:, 3:4]))
        P.run("scalar", lambda s, xb=xb: s.activation(out=ln1b[:, 1024:2048], in_=xb[:, 1024:2048], func=AF.Square, accum_out=stat[:, 10:11]), ac)
        t_sq = ac.n
        t_sqL[tc] = t_sq
        P.wait("vector", ac, t_sq)
        P.run("vector", lambda v: v.tensor_add(out=stat[:, 3:4], in0=stat[:, 3:4], in1=stat[:, 10:11]))
        P.run("vector", lambda v: v.tensor_scalar_mul(out=stat[:, 4:5], in0=stat[:, 2:3], scalar1=1.0 / H))
        P.run("vector", lambda v: v.tensor_scalar_mul(out=stat[:, 5:6], in0=stat[:, 3:4], scalar1=1.0 / H))
        P.run("vector", lambda v: v.tensor_mul(out=stat[:, 6:7], in0=stat[:, 4:5], in1=stat[:, 4:5]))
        P.run("vector", lambda v: v.tensor_copy(out=dumf[:, 0:1], in_=dumf[:, 1:2]))
        P.run("vector", lambda v: v.tensor_sub(out=stat[:, 6:7], in0=stat[:, 5:6], in1=stat[:, 6:7]), dv)
        t_var = dv.n
        P.wait("scalar", dv, t_var)
        P.run("scalar", lambda s: s.activation(out=stat[:, 7:8], in_=stat[:, 6:7], func=AF.Ln, bias=cepsA[:, 0:1]))
        P.run("scalar", lambda s: s.activation(out=stat[:, 8:9], in_=stat[:, 7:8], func=AF.Exp, scale=-0.5), ac)
        t_rstd = ac.n
        P.wait("vector", ac, t_rstd)
        P.run("vector", lambda v: v.tensor_scalar(out=stat[:, 9:10], in0=stat[:, 4:5], scalar1=stat[:, 8:9], scalar2=-1.0, op0=ALU.mult, op1=ALU.mult))
        P.run("vector", lambda v: v.tensor_copy(out=dumf[:, 0:1], in_=dumf[:, 1:2]))
        if tc >= 1:
            P.wait("vector", pe, t_tr_pe[tc - 1])
        P.run("vector", lambda v, xb=xb: v.tensor_scalar(out=ln1b, in0=xb, scalar1=stat[:, 8:9], scalar2=stat[:, 9:10], op0=ALU.mult, op1=ALU.add), dv)
        t_ln1tm[tc] = dv.n
        if d_ln1dbg is not None:
            P.wait("sync", dv, t_ln1tm[tc])
            P.dma("sync", qS, d_ln1dbg[tc * 128:(tc + 1) * 128, :], ln1b)
        P.wait("tensor", dv, t_ln1tm[tc])
        if tc == 0:
            P.wait("tensor", dv, t_ident)
        for hc in range(NHC):
            bank = 6 + (hc % 2)
            P.wait("tensor", dv, prev2[hc % 2])
            P.run("tensor", lambda t, hc=hc, bank=bank: t.transpose(bf(psb[bank][:, 0:64]), ln1b[:, hc * 128:(hc + 1) * 128], ident[:, :]), pe)
            tp = pe.n
            P.wait("vector", pe, tp)
            P.run("vector", lambda v, hc=hc, tc=tc, bank=bank: v.tensor_copy(out=ln1f[hc][:, tc * 128:(tc + 1) * 128], in_=bf(psb[bank][:, 0:64])), dv)
            prev2[hc % 2] = dv.n
        t_tr_pe[tc] = pe.n
    t_ln1_all = dv.n
    for l in range(GU_L):
        emit_gu_l(l)

    def dec_ops(e, src3, n8, m8, v8, f1, f2, scl3, dst3, hi):
        if hi:
            e.tensor_scalar(out=m8, in0=src3, scalar1=4, scalar2=7, op0=ALU.logical_shift_right, op1=ALU.bitwise_and)
            e.tensor_scalar(out=n8, in0=src3, scalar1=4, scalar2=8, op0=ALU.logical_shift_right, op1=ALU.bitwise_and)
        else:
            e.tensor_scalar(out=m8, in0=src3, scalar1=7, scalar2=None, op0=ALU.bitwise_and)
            e.tensor_scalar(out=n8, in0=src3, scalar1=8, scalar2=None, op0=ALU.bitwise_and)
        e.tensor_scalar(out=v8, in0=m8, scalar1=2, scalar2=-4, op0=ALU.mult, op1=ALU.add)
        e.tensor_tensor(out=v8, in0=m8, in1=v8, op=ALU.max)
        e.tensor_scalar(out=m8, in0=m8, scalar1=4, scalar2=-16, op0=ALU.mult, op1=ALU.add)
        e.tensor_tensor(out=v8, in0=v8, in1=m8, op=ALU.max)
        e.tensor_scalar(out=f1, in0=n8, scalar1=-0.125, scalar2=0.5, op0=ALU.mult, op1=ALU.add)
        e.tensor_tensor(out=f2, in0=v8, in1=f1, op=ALU.mult)
        return e.tensor_tensor(out=dst3, in0=f2, in1=scl3, op=ALU.mult)

    def r3(ap, c):
        return ap.rearrange("p (u c) -> p u c", c=c)

    for l in range(GU_L):
        P.wait("gpsimd", dv, t_ghi[l])
        t_gsp[l] = P.dma("gpsimd", qGSP, d_aggu_i[:, l * 2048:(l + 1) * 2048], gdec[l % 4])
    P.wait("gpsimd", dv, t_ghi[GU_L - 1])
    t_dnb = P.dma("gpsimd", qDNB, dnbb, bass.AP(d_pkdn, 0, [[DN_COLS, 128], [128 * DN_COLS, DN_U], [1, DN_COLS]]))
    t_dns = P.dma("gpsimd", qDNB, dnsr2, bass.AP(d_scdn, 0, [[DN_COLS, 128], [128 * DN_COLS, DN_U], [1, DN_COLS]]))
    NG = DN_U // 2
    t_dgrp = [0] * NG
    t_dsp = [0] * NG
    P.wait("vector", qG, t_gsp[8])
    P.wait("vector", qG, t_gsp[9])
    P.wait("vector", qG, t_dnb)
    P.wait("vector", qG, t_dns)
    for g_ in range(NG):
        u0 = 2 * g_
        src = r3(dnbb[:, u0 * DN_COLS:(u0 + 2) * DN_COLS].bitcast(I8), DN_COLS)
        scl3 = r3(dnsr2[:, u0 * DN_COLS:(u0 + 2) * DN_COLS], DN_COLS)
        stg = dnstg[g_ % 2][:, 0:1024]
        s4 = stg.rearrange("p (u two c) -> p u two c", two=2, c=DN_COLS)
        dlo = s4[:, :, 0:1, :].squeeze(2)
        dhi = s4[:, :, 1:2, :].squeeze(2)
        if g_ >= 2:
            P.wait("vector", qG, t_dsp[g_ - 2])
        si = [r3(dvi[i][:, 0:512], DN_COLS) for i in range(3)]
        sf = [r3(dvf[i][:, 0:512], DN_COLS) for i in range(2)]
        P.run("vector", lambda v, src=src, scl3=scl3, dlo=dlo, si=si, sf=sf: dec_ops(v, src, si[0], si[1], si[2], sf[0], sf[1], scl3, dlo, False), dv)
        P.run("vector", lambda v, src=src, scl3=scl3, dhi=dhi, si=si, sf=sf: dec_ops(v, src, si[0], si[1], si[2], sf[0], sf[1], scl3, dhi, True), dv)
        t_dgrp[g_] = dv.n
        P.wait("gpsimd", dv, t_dgrp[g_])
        t_dsp[g_] = P.dma("gpsimd", qDSP, d_agdn_i[:, g_ * 1024:(g_ + 1) * 1024], stg)
    t_dec_dv = max(t_ghi[-1], t_dgrp[-1])

    P.wait("gpsimd", dv, t_dec_dv)
    P.wait("gpsimd", qG, t_dsp[NG - 1])
    P.wait("gpsimd", qG, t_dsp[NG - 2])
    t_wk = 0
    for hc in range(NHC):
        t_wk = P.dma("gpsimd", qWK, wk_sb[hc], d_wk[hc * 128:(hc + 1) * 128, :])
    P.waitl("gpsimd", dv, t_ln1tm, NTC - 1)
    for hc in range(NHC):
        P.dma("gpsimd", qWG1, wg1[:, hc * 128:(hc + 1) * 128], d_wg1[hc * 128:(hc + 1) * 128, :])
    t_wg = tokfull(qWG1)
    h_ln1free = [0, 0]
    P.waitl("gpsimd", qS, h_ln1free, 0)
    P.waitl("gpsimd", pe, h_ln1free, 1)
    t_wv = 0
    for hc in range(NHC):
        t_wv = P.dma("gpsimd", qWV, wv_sb[hc], d_wv[hc * 128:(hc + 1) * 128, :])

    for l in range(GU_L - 4, GU_L):
        P.wait("gpsimd", qG, t_gsp[l])
    P.run("gpsimd", lambda g: g.collective_compute(
        "AllGather", ALU.bypass, replica_groups=[list(range(CORES))],
        ins=[d_aggu_i.ap().opt()], outs=[d_aggu_o.ap().opt()]), gp)
    t_aggu = gp.n
    P.wait("gpsimd", qG, t_dsp[NG - 1])
    P.wait("gpsimd", qG, t_dsp[NG - 2])
    P.run("gpsimd", lambda g: g.collective_compute(
        "AllGather", ALU.bypass, replica_groups=[list(range(CORES))],
        ins=[d_agdn_i.ap().opt()], outs=[d_agdn_o.ap().opt()]), gp)
    t_agdn = gp.n

    t_qev = []
    t_qspill = []
    t_wqf = [0] * NHC
    t_wqc = [0] * NHC

    def load_wqf(ec):
        if ec >= 2:
            P.wait("sync", ac, t_wqc[ec - 2])
        else:
            P.wait("sync", dv, t_ln1tm[NTC - 1])
            P.wait("sync", pe, t_tr_pe[NTC - 1])
        t_wqf[ec] = P.dma("sync", qWQ, wqf32[ec % 2][:, :], bass.AP(
            d_wq, ec * 128, [[H, 128], [128 * H, NHC], [1, 128]]))

    def cast_wq(ec):
        P.wait("scalar", qS, t_wqf[ec])
        if ec >= 2:
            P.wait("scalar", pe, t_qgrp_pe[ec - 2])
        if ec == 0:
            P.waitl("scalar", dv, t_ln1tm, NTC - 1)
        P.run("scalar", lambda s, ec=ec: s.activation(out=wqcolw[ec % 2][:, :], in_=wqf32[ec % 2][:, :], func=AF.Copy), ac)
        t_wqc[ec] = ac.n

    load_wqf(0)
    load_wqf(1)
    cast_wq(0)
    cast_wq(1)
    for ec in range(NHC):
        if ec + 2 < NHC:
            load_wqf(ec + 2)
        P.wait("tensor", ac, t_wqc[ec])
        if ec == 0:
            P.wait("tensor", dv, t_ln1_all)
        for es4 in range(4):
            bank = (ec * 4 + es4) % 2
            if len(t_qev) >= 2:
                P.wait("tensor", ac, t_qev[-2])
            for hc in range(NHC):
                P.run("tensor", lambda t, ec=ec, hc=hc, es4=es4, bank=bank: t.matmul(
                    psb[bank][:, :], wqcolw[ec % 2][:, hc * 128:(hc + 1) * 128], ln1f[hc][:, es4 * 512:(es4 + 1) * 512],
                    start=(hc == 0), stop=(hc == NHC - 1)), pe if hc == NHC - 1 else None)
            t_mm = pe.n
            qo = qout[(ec * 4 + es4) % 4]
            P.wait("scalar", pe, t_mm)
            if len(t_qspill) >= 4:
                P.wait("scalar", qS, t_qspill[-4])
            P.run("scalar", lambda s, qo=qo, bank=bank: s.activation(out=qo[:, :], in_=psb[bank][:, :], func=AF.Silu), ac)
            t_qev.append(ac.n)
            P.wait("sync", ac, t_qev[-1])
            t_qspill.append(P.dma("sync", qQSP, bass.AP(
                d_q, (es4 * NHC + ec) * 128 * 512, [[512, 128], [1, 512]]), qo[:, :]))
        t_qgrp_pe[ec] = pe.n
        if ec + 2 < NHC:
            cast_wq(ec + 2)
    t_midev = [0] * 4
    for es4 in range(4):
        P.wait("tensor", qG, t_wg)
        if es4 >= 1:
            P.wait("tensor", ac, t_midev[es4 - 1])
        for hc in range(NHC):
            P.run("tensor", lambda t, hc=hc, es4=es4: t.matmul(
                psb[2][:, :], wg1[:, hc * 128:(hc + 1) * 128], ln1f[hc][:, es4 * 512:(es4 + 1) * 512],
                start=(hc == 0), stop=(hc == NHC - 1)), pe if hc == NHC - 1 else None)
        t_mm = pe.n
        P.wait("scalar", pe, t_mm)
        P.run("scalar", lambda s, es4=es4: s.activation(out=midb[:, es4 * 512:(es4 + 1) * 512], in_=psb[2][:, :], func=AF.Copy), ac)
        t_midev[es4] = ac.n
    t_mid = ac.n
    t_mid_pe = pe.n
    t_l1sp = [0] * NHC
    for hc in range(NHC):
        P.wait("sync", pe, t_mid_pe)
        t_l1sp[hc] = P.dma("sync", qLNF, bass.AP(
            d_ln1g, hc * 128 * 128, [[128, 128], [NHC * 128 * 128, NTC], [1, 128]]),
            ln1f[hc][:, :].rearrange("p (t c) -> p t c", c=128))
    t_ln1_spill = tokfull(qLNF)
    h_ln1free[0] = t_ln1_spill
    h_ln1free[1] = t_mid_pe

    d_kdbg = dbgout("ktm", [NTC, 128, H], BF16)
    d_vdbg = dbgout("vtm", [NTC, 128, H], BF16)
    d_odbg = dbgout("ointra", [NTC, 128, H], BF16)
    P.wait("vector", pe, t_mid_pe)
    P.wait("vector", qS, t_ln1_spill)
    P.run("vector", lambda v: v.memset(tstate, 0.0), dv)
    t_tzero = dv.n
    P.wait("vector", ac, t_decay)
    ktcr = [ktc, ktcB]
    vtcr = [vtc, vtcB]
    t_qtc = [0] * NTC
    t_l1tc = [0] * NTC
    t_qlastuse = [0] * NTC
    t_scan_dv = [0] * NTC
    t_scan_ac = [0] * NTC
    t_kv_pe = [0] * NTC
    t_kv_ac = [0] * NTC
    t_kev = [0] * NTC
    t_vev = [0] * NTC
    t_kread_dv = [0] * NTC
    t_ospill = [None] * NTC
    t_aspill = [None] * NTC
    sc = {}

    def kv_group(tc, gi):
        isv = gi >= 4
        es4 = gi % 4
        bank = (2 if isv else 0) + es4 % 2
        wsb = wv_sb if isv else wk_sb
        dst = vtcr[tc % 2] if isv else ktcr[tc % 2]
        if gi == 0:
            if tc == 0:
                P.wait("tensor", ac, ac.n)
            P.wait("tensor", qG, t_wk)
            P.wait("tensor", qS, t_l1tc[tc])
        if gi == 4:
            P.wait("tensor", qG, t_wv)
        if es4 >= 2:
            P.wait("tensor", ac, sc[("kvev", tc, gi - 2)])
        elif gi == 0 and tc >= 1:
            P.wait("tensor", ac, t_kv_ac[tc - 1])
        for hc in range(NHC):
            P.run("tensor", lambda t, hc=hc, es4=es4, wsb=wsb: t.matmul(
                psb[(2 if isv else 0) + es4 % 2][:, :] if False else psb[bank][:, :],
                ln1tc[:, hc * 128:(hc + 1) * 128], wsb[hc][:, es4 * 512:(es4 + 1) * 512],
                start=(hc == 0), stop=(hc == NHC - 1)), pe if hc == NHC - 1 else None)
        t_mm = pe.n
        sc[("kvmm", tc, gi)] = t_mm

    def kv_evac(tc, gi):
        isv = gi >= 4
        es4 = gi % 4
        bank = (2 if isv else 0) + es4 % 2
        dst = vtcr[tc % 2] if isv else ktcr[tc % 2]
        P.wait("scalar", pe, sc[("kvmm", tc, gi)])
        if tc >= 2:
            P.wait("scalar", dv, t_kread_dv[tc - 2] if not isv else t_scan_dv[tc - 2])
            P.wait("scalar", pe, t_qlastuse[tc - 2])
        P.run("scalar", lambda s, dst=dst, es4=es4, bank=bank: s.activation(
            out=dst[:, es4 * 512:(es4 + 1) * 512], in_=psb[bank][:, :], func=AF.Silu), ac)
        sc[("kvev", tc, gi)] = ac.n
        if gi == 3:
            t_kev[tc] = ac.n
        if gi == 7:
            t_vev[tc] = ac.n
            t_kv_ac[tc] = ac.n

    def scan_dv_batch(tc, which, hb):
        hs = hb * 8
        kt = ktcr[tc % 2]
        vt = vtcr[tc % 2]
        if which == 0:
            P.wait("vector", ac, t_kev[tc])
            if tc == 0 and hb == 0:
                P.wait("vector", dv, t_tzero)
                P.wait("vector", dv, t_dmask)
            for j in range(8):
                h = hs + j
                P.run("vector", lambda v, j=j, h=h, kt=kt: v.tensor_scalar_mul(
                    out=kdk8[:, j * 128:(j + 1) * 128], in0=kt[:, h * 128:(h + 1) * 128], scalar1=kdecs[:, h:h + 1]),
                    dv if j == 7 else None)
            sc[("D0", hb)] = dv.n
            if hb == 1:
                t_kread_dv[tc] = dv.n
        elif which == 1:
            P.wait("vector", pe, sc[("SA", hb)])
            if t_aspill[tc - 1] is not None and hb == 0:
                P.wait("vector", qS, t_aspill[tc - 1])
            for j in range(8):
                h = hs + j
                bk = 4 + j // 4
                off = (j % 4) * 128
                P.run("vector", lambda v, h=h, bk=bk, off=off: v.tensor_copy(
                    out=atc[:, h * 128:(h + 1) * 128], in_=psb[bk][:, off:off + 128]), dv)
                P.run("vector", lambda v, h=h: v.scalar_tensor_tensor(
                    out=tstate[:, h * 128:(h + 1) * 128], in0=tstate[:, h * 128:(h + 1) * 128], scalar=cds[:, h:h + 1],
                    in1=atc[:, h * 128:(h + 1) * 128], op0=ALU.mult, op1=ALU.add), dv if j == 7 else None)
            sc[("D1", hb)] = dv.n
        elif which == 2:
            P.wait("vector", pe, sc[("SS", hb)])
            for j in range(8):
                h = hs + j
                bk = 4 + j // 4
                off = (j % 4) * 128
                P.run("vector", lambda v, j=j, h=h, bk=bk, off=off: v.tensor_mul(
                    out=sdb8[:, j * 128:(j + 1) * 128], in0=psb[bk][:, off:off + 128], in1=dmsk[:, h * 128:(h + 1) * 128]),
                    dv if j == 7 else None)
            sc[("D2", hb)] = dv.n
        elif which == 3:
            P.wait("vector", pe, sc[("SO", hb)])
            if t_ospill[tc - 1] is not None and hb == 0:
                P.wait("vector", qS, t_ospill[tc - 1])
            for j in range(8):
                h = hs + j
                bk = 4 + j // 4
                off = (j % 4) * 128
                P.run("vector", lambda v, h=h, bk=bk, off=off: v.tensor_copy(
                    out=otc[:, h * 128:(h + 1) * 128], in_=psb[bk][:, off:off + 128]), dv)
            sc[("D3", hb)] = dv.n
            if hb == 1:
                t_scan_dv[tc] = dv.n

    def scan_ac_batch(tc, which, hb):
        if which == 0:
            P.wait("scalar", pe, sc[("ST", hb)])
            for j in range(8):
                P.run("scalar", lambda s, j=j: s.activation(
                    out=kfm8[:, j * 128:(j + 1) * 128], in_=bf(psb[6][:, j * 64:(j + 1) * 64]), func=AF.Copy),
                    ac if j == 7 else None)
            sc[("A0", hb)] = ac.n
        else:
            P.wait("scalar", pe, sc[("SD", hb)])
            for j in range(8):
                P.run("scalar", lambda s, j=j: s.activation(
                    out=sdt8[:, j * 128:(j + 1) * 128], in_=bf(psb[7][:, j * 64:(j + 1) * 64]), func=AF.Copy),
                    ac if j == 7 else None)
            sc[("A1", hb)] = ac.n
            if hb == 1:
                t_scan_ac[tc] = ac.n

    def scan_pe_sub(tc, si):
        hb, st = si // 5, si % 5
        hs = hb * 8
        kt = ktcr[tc % 2]
        vt = vtcr[tc % 2]
        if st == 0:
            scan_dv_batch(tc, 0, hb)
            P.wait("tensor", dv, sc[("D0", hb)])
            if hb == 0:
                if tc >= 1:
                    P.wait("tensor", dv, t_scan_dv[tc - 1])
            else:
                P.wait("tensor", dv, sc[("D3", 0)])
            P.wait("tensor", ac, t_vev[tc])
            for j in range(8):
                h = hs + j
                bk = 4 + j // 4
                off = (j % 4) * 128
                P.run("tensor", lambda t, j=j, h=h, bk=bk, off=off, vt=vt: t.matmul(
                    psb[bk][:, off:off + 128], kdk8[:, j * 128:(j + 1) * 128], vt[:, h * 128:(h + 1) * 128],
                    start=True, stop=True), pe if j == 7 else None)
            sc[("SA", hb)] = pe.n
            scan_dv_batch(tc, 1, hb)
        elif st == 1:
            if hb == 1:
                P.wait("tensor", ac, sc[("A0", 0)])
            elif tc >= 1:
                P.wait("tensor", ac, sc[("A0p", 1)] if ("A0p", 1) in sc else 0)
            for j in range(8):
                h = hs + j
                P.run("tensor", lambda t, j=j, h=h, kt=kt: t.transpose(
                    bf(psb[6][:, j * 64:(j + 1) * 64]), kt[:, h * 128:(h + 1) * 128], ident[:, :]), pe if j == 7 else None)
            sc[("ST", hb)] = pe.n
            scan_ac_batch(tc, 0, hb)
        elif st == 2:
            P.wait("tensor", dv, sc[("D1", hb)])
            P.wait("tensor", ac, sc[("A0", hb)])
            P.wait("tensor", qS, t_qtc[tc])
            for j in range(8):
                h = hs + j
                bk = 4 + j // 4
                off = (j % 4) * 128
                P.run("tensor", lambda t, j=j, h=h, bk=bk, off=off, tc=tc: t.matmul(
                    psb[bk][:, off:off + 128], qtc[tc % 2][:, h * 128:(h + 1) * 128], kfm8[:, j * 128:(j + 1) * 128],
                    start=True, stop=True), pe if j == 7 else None)
            sc[("SS", hb)] = pe.n
            scan_dv_batch(tc, 2, hb)
        elif st == 3:
            P.wait("tensor", dv, sc[("D2", hb)])
            if hb == 1:
                P.wait("tensor", ac, sc[("A1", 0)])
            for j in range(8):
                P.run("tensor", lambda t, j=j: t.transpose(
                    bf(psb[7][:, j * 64:(j + 1) * 64]), sdb8[:, j * 128:(j + 1) * 128], ident[:, :]), pe if j == 7 else None)
            sc[("SD", hb)] = pe.n
            scan_ac_batch(tc, 1, hb)
        else:
            P.wait("tensor", ac, sc[("A1", hb)])
            for j in range(8):
                h = hs + j
                bk = 4 + j // 4
                off = (j % 4) * 128
                P.run("tensor", lambda t, j=j, h=h, bk=bk, off=off, vt=vt: t.matmul(
                    psb[bk][:, off:off + 128], sdt8[:, j * 128:(j + 1) * 128], vt[:, h * 128:(h + 1) * 128],
                    start=True, stop=True), pe if j == 7 else None)
            sc[("SO", hb)] = pe.n
            scan_dv_batch(tc, 3, hb)
            if hb == 1:
                t_qlastuse[tc] = pe.n
                sc[("A0p", 1)] = sc[("A0", 1)]
                P.wait("sync", dv, t_scan_dv[tc])
                t_ospill[tc] = P.dma("sync", qOSP, d_oi[tc], otc)
                t_aspill[tc] = P.dma("sync", qASP, d_a[tc], atc)
                if d_odbg is not None:
                    P.dma("sync", qS, d_odbg[tc], otc)

    for tc in range(NTC):
        if tc >= 1:
            P.wait("sync", pe, t_kv_pe[tc - 1])
        if tc == 0:
            P.wait("sync", pe, t_mid_pe)
            P.wait("sync", qS, t_ln1_spill)
            for tok in t_qspill[-4:]:
                P.wait("sync", qS, tok)
            P.wait("sync", ac, t_wqc[NHC - 1])
        t_l1tc[tc] = P.dma("sync", qL1, ln1tc[:, :].rearrange("p (h c) -> p h c", c=128), bass.AP(
            d_ln1g, tc * NHC * 128 * 128, [[128, 128], [128 * 128, NHC], [1, 128]]))
        if tc >= 2:
            P.wait("sync", pe, t_qlastuse[tc - 2])
        t_qtc[tc] = P.dma("sync", qQT, qtc[tc % 2][:, :], bass.AP(
            d_q, (tc // 4) * NHC * 128 * 512 + (tc % 4) * 128, [[512, 128], [128 * 512, NHC], [1, 128]]))
        for gi in range(8):
            kv_group(tc, gi)
            kv_evac(tc, gi)
            if tc >= 1 and gi < 8:
                si = gi
                scan_pe_sub(tc - 1, si)
        if tc >= 1:
            scan_pe_sub(tc - 1, 8)
            scan_pe_sub(tc - 1, 9)
        t_kv_pe[tc] = sc[("kvmm", tc, 7)]
        if d_kdbg is not None:
            P.wait("sync", ac, t_kev[tc])
            P.dma("sync", qS, d_kdbg[tc], ktcr[tc % 2])
        if d_vdbg is not None:
            P.wait("sync", ac, t_vev[tc])
            P.dma("sync", qS, d_vdbg[tc], vtcr[tc % 2])
    for si in range(10):
        scan_pe_sub(NTC - 1, si)

    t_wo = [0] * NHC
    P.wait("gpsimd", dv, t_scan_dv[NTC - 1])
    P.wait("gpsimd", pe, t_qlastuse[NTC - 1])
    t_wg2l = P.dma("gpsimd", qWG2, wg2[:, :], d_wg2[:, :])
    for ec in range(NHC):
        t_wo[ec] = P.dma("gpsimd", qWO, wo_sb[ec], d_wo[ec * 128:(ec + 1) * 128, :])

    t_tspill = P.dma("sync", qTSPs, d_agt_i[:, :], tstate)
    P.wait("gpsimd", qS, t_tspill)
    P.run("gpsimd", lambda g: g.collective_compute(
        "AllGather", ALU.bypass, replica_groups=[list(range(CORES))],
        ins=[d_agt_i.ap().opt()], outs=[d_agt_o.ap().opt()]), gp)
    t_agt = gp.n

    d_tdbg = dbgout("tst", [CORES * 128, H])
    if d_tdbg is not None:
        P.wait("sync", gp, t_agt)
        P.dma("sync", qS, d_tdbg[:, :], d_agt_o[:, :])
    P.wait("vector", pe, t_kv_pe[NTC - 1])
    P.run("vector", lambda v: v.memset(sinb, 0.0), dv)
    t_sin0 = dv.n
    t_peer = [0] * CORES
    t_sinr = [0] * CORES
    for r in range(CORES):
        P.wait("gpsimd", gp, t_agt)
        if r == 0:
            P.wait("gpsimd", pe, t_kv_pe[NTC - 1])
        if r >= 2:
            P.waitl("gpsimd", dv, t_sinr, r - 2)
        t_peer[r] = P.dma("gpsimd", qPEER, onbr[r % 2], d_agt_o[r * 128:(r + 1) * 128, :])
    for r in range(CORES):
        P.run("vector", lambda v, r=r: v.tensor_scalar_mul(out=coefr[:, :], in0=ld_rep[:, :], scalar1=gapr[:, r:r + 1]), dv)
        tcf = dv.n
        P.wait("scalar", dv, tcf)
        P.run("scalar", lambda s: s.activation(out=coefr[:, :], in_=coefr[:, :], func=AF.Exp, scale=2048.0), ac)
        tce = ac.n
        P.wait("vector", ac, tce)
        P.wait("vector", qG, t_peer[r])
        if r == 0:
            P.wait("vector", dv, t_sin0)
        for h in range(NH):
            P.run("vector", lambda v, r=r, h=h: v.scalar_tensor_tensor(
                out=sinb[:, h * 128:(h + 1) * 128], in0=onbr[r % 2][:, h * 128:(h + 1) * 128], scalar=coefr[:, h:h + 1],
                in1=sinb[:, h * 128:(h + 1) * 128], op0=ALU.mult, op1=ALU.add), dv if h == NH - 1 else None)
        t_sinr[r] = dv.n
    t_sin = dv.n
    P.wait("scalar", dv, t_sin)
    P.run("scalar", lambda s: s.activation(out=sbfb, in_=sinb, func=AF.Copy), ac)
    t_sbf = ac.n

    t_wofold = [0] * NHC
    for ec in range(NHC):
        P.wait("vector", qG, t_wo[NHC - 1])
        P.wait("vector", qS, t_nw)
        P.run("vector", lambda v, ec=ec: v.tensor_scalar_mul(out=wo_sb[ec][:, :], in0=wo_sb[ec][:, :], scalar1=nwcol[:, ec:ec + 1]), dv)
        t_wofold[ec] = dv.n

    d_y1dbg = dbgout("y1", [T, H])
    d_aodbg = dbgout("attnout", [T, H])
    t_ln2s = [0] * NTC
    t_y1sp = [0] * NTC
    t_sbf_tc = t_sbf
    t_supd = [0] * NTC
    t_oint = [0] * NTC
    t_gate = [0] * NTC
    t_trs_pe = [0] * NTC
    t_on = [0] * NTC
    t_gp_res = [0] * NTC
    onb3 = [onbr[0], onbr[1], peerb]
    t_msqL = [0] * NTC
    t_mg = {}
    t_gse = {}
    t_ont = {}
    t_aoev = {}
    t_bdv = [0] * NTC
    t_bac = [0] * NTC
    t_q2 = [0] * NTC
    t_oi2 = [0] * NTC
    t_a2 = [0] * NTC
    t_x2 = [0] * NTC

    def p2_loads(tc):
        if tc >= 2:
            P.wait("gpsimd", pe, t_oint[tc - 2])
        t_q2[tc] = P.dma("gpsimd", qQT, qtc[tc % 2][:, :], bass.AP(
            d_q, (tc // 4) * NHC * 128 * 512 + (tc % 4) * 128, [[512, 128], [128 * 512, NHC], [1, 128]]))
        if tc >= 2:
            P.wait("gpsimd", dv, t_mg[(tc - 2, 3)])
        t_oi2[tc] = P.dma("gpsimd", qOI, oitcr[tc % 2], d_oi[tc])
        if tc >= 2:
            P.wait("sync", dv, t_supd[tc - 2])
        t_a2[tc] = P.dma("sync", qA2, a2tcr[tc % 2], d_a[tc])

    def p2_xload(tc):
        if tc >= 2:
            P.wait("scalar", gp, t_gp_res[tc - 2])
        t_x2[tc] = P.dma("scalar", qX2, xtc2r[tc % 2], d_x[tc * 128:(tc + 1) * 128, :])

    def emit_A(tc):
        nonlocal t_sbf_tc
        sbf_prev = t_sbf_tc
        P.wait("vector", ac, sbf_prev)
        P.waitt("vector", t_a2[tc])
        for h in range(NH):
            P.run("vector", lambda v, tc=tc, h=h: v.scalar_tensor_tensor(
                out=sinb[:, h * 128:(h + 1) * 128], in0=sinb[:, h * 128:(h + 1) * 128], scalar=cds[:, h:h + 1],
                in1=a2tcr[tc % 2][:, h * 128:(h + 1) * 128], op0=ALU.mult, op1=ALU.add), dv if h == NH - 1 else None)
        t_supd[tc] = dv.n
        P.wait("tensor", ac, sbf_prev)
        P.waitt("tensor", t_q2[tc])
        if tc >= 1:
            P.wait("tensor", dv, t_mg[(tc - 1, 3)])
            P.wait("tensor", ac, t_gse[(tc - 1, 3)])
        for h in range(NH):
            P.run("tensor", lambda t, tc=tc, h=h: t.matmul(
                psb[h // 4][:, (h % 4) * 128:(h % 4 + 1) * 128], qtc[tc % 2][:, h * 128:(h + 1) * 128],
                sbfb[:, h * 128:(h + 1) * 128], start=True, stop=True), pe if h == NH - 1 else None)
        t_oint[tc] = pe.n
        P.wait("vector", pe, t_oint[tc])
        P.waitt("vector", t_oi2[tc])
        for h in range(NH):
            P.run("vector", lambda v, tc=tc, h=h: v.scalar_tensor_tensor(
                out=otc[:, h * 128:(h + 1) * 128], in0=psb[h // 4][:, (h % 4) * 128:(h % 4 + 1) * 128],
                scalar=qdecs[:, h:h + 1], in1=oitcr[tc % 2][:, h * 128:(h + 1) * 128],
                op0=ALU.mult, op1=ALU.add), dv if h == NH - 1 else None)
        t_mg[(tc, 3)] = dv.n
        P.wait("scalar", dv, t_supd[tc])
        P.wait("scalar", pe, t_oint[tc])
        P.run("scalar", lambda s: s.activation(out=sbfb, in_=sinb, func=AF.Copy), ac)
        t_sbf_tc = ac.n
        for pr in range(2):
            if pr == 0:
                P.wait("tensor", dv, t_mg[(tc, 3)])
                if tc == 0:
                    P.wait("tensor", qG, t_wg2l)
            else:
                P.wait("tensor", ac, t_gse[(tc, 1)])
            for es4 in (2 * pr, 2 * pr + 1):
                P.run("tensor", lambda t, tc=tc, es4=es4: t.matmul(
                    psb[2 + es4 % 2][:, :], midb[:, tc * 128:(tc + 1) * 128], wg2[:, es4 * 512:(es4 + 1) * 512],
                    start=True, stop=True), pe if es4 % 2 == 1 else None)
            tp = pe.n
            P.wait("scalar", pe, tp)
            for es4 in (2 * pr, 2 * pr + 1):
                P.run("scalar", lambda s, es4=es4: s.activation(
                    out=gsigb[:, es4 * 512:(es4 + 1) * 512], in_=psb[2 + es4 % 2][:, :], func=AF.Sigmoid), ac)
                t_gse[(tc, es4)] = ac.n
        t_gate[tc] = pe.n
        P.wait("vector", ac, t_gse[(tc, 3)])
        P.run("vector", lambda v: v.tensor_mul(out=otc[:, :], in0=otc[:, :], in1=gsigb[:, :]))
        if tc >= 3:
            P.wait("vector", pe, t_trs_pe[tc - 3])
        P.run("vector", lambda v, tc=tc: v.tensor_mul(out=onb3[tc % 3][:, :], in0=otc[:, :], in1=otc[:, :]))
        P.run("vector", lambda v, tc=tc: v.tensor_reduce(
            out=stat[:, 0:16], in_=onb3[tc % 3][:, :].rearrange("p (g c) -> p g c", c=128), axis=AXX, op=ALU.add), dv)
        t_msqL[tc] = dv.n

    def emit_A2(tc):
        P.wait("scalar", dv, t_msqL[tc])
        P.run("scalar", lambda s: s.activation(out=stat[:, 0:16], in_=stat[:, 0:16], func=AF.Ln, scale=1.0 / 128, bias=cepsA[:, 0:1]))
        P.run("scalar", lambda s: s.activation(out=stat[:, 0:16], in_=stat[:, 0:16], func=AF.Exp, scale=-0.5), ac)
        t_grs = ac.n
        P.wait("vector", ac, t_grs)
        for h in range(NH):
            P.run("vector", lambda v, tc=tc, h=h: v.tensor_scalar_mul(
                out=onb3[tc % 3][:, h * 128:(h + 1) * 128], in0=otc[:, h * 128:(h + 1) * 128],
                scalar1=stat[:, h:h + 1]), dv if h == NH - 1 else None)
        t_on[tc] = dv.n

    def emit_B(tc):
        aof = aofr[tc % 2]
        bln2 = bln2r[tc % 2]
        bstat = bstatr[tc % 2]
        P.wait("tensor", dv, t_on[tc])
        for hf in range(2):
            bank = 6 + hf
            if tc >= 1:
                P.wait("tensor", ac, t_ont[(tc - 1, hf)])
            for k in range(8):
                ec = hf * 8 + k
                P.run("tensor", lambda t, tc=tc, ec=ec, k=k, bank=bank: t.transpose(
                    bf(psb[bank][:, k * 64:(k + 1) * 64]), onb3[tc % 3][:, ec * 128:(ec + 1) * 128], ident[:, :]),
                    pe if k == 7 else None)
            tp = pe.n
            P.wait("scalar", pe, tp)
            P.run("scalar", lambda s, hf=hf, bank=bank: s.activation(
                out=ontbb[:, hf * 1024:(hf + 1) * 1024], in_=bf(psb[bank][:, 0:512]), func=AF.Copy), ac)
            t_ont[(tc, hf)] = ac.n
        t_trs_pe[tc] = pe.n
        P.wait("tensor", ac, t_ont[(tc, 1)])
        if tc == 0:
            P.wait("tensor", dv, t_scan_dv[NTC - 1])
            P.wait("tensor", dv, t_wofold[NHC - 1])
        bsc = bln2.bitcast(F32)
        for pr in range(2):
            if pr == 0 and tc >= 1:
                P.wait("tensor", ac, t_aoev[(tc - 1, 1)])
            elif pr == 1:
                P.wait("tensor", ac, t_aoev[(tc, 0)])
            for es4 in (2 * pr, 2 * pr + 1):
                for ec in range(NHC):
                    P.run("tensor", lambda t, ec=ec, es4=es4: t.matmul(
                        psb[4 + es4 % 2][:, :], ontbb[:, ec * 128:(ec + 1) * 128], wo_sb[ec][:, es4 * 512:(es4 + 1) * 512],
                        start=(ec == 0), stop=(ec == NHC - 1)), pe if (ec == NHC - 1 and es4 % 2 == 1) else None)
            t_mm = pe.n
            P.wait("scalar", pe, t_mm)
            if pr == 0 and tc >= 2:
                P.wait("scalar", dv, t_bdv[tc - 2])
                P.waitt("scalar", t_y1sp[tc - 2])
                P.waitt("scalar", t_ln2s[tc - 2])
            for es4 in (2 * pr, 2 * pr + 1):
                P.run("scalar", lambda s, es4=es4: s.activation(
                    out=aof[:, es4 * 512:(es4 + 1) * 512], in_=psb[4 + es4 % 2][:, :], func=AF.Copy, accum_out=bstat[:, es4:es4 + 1]))
                P.run("scalar", lambda s, es4=es4, bsc=bsc: s.activation(
                    out=bsc[:, 0:512], in_=psb[4 + es4 % 2][:, :], func=AF.Square, accum_out=bstat[:, 4 + es4:5 + es4]), ac)
            t_aoev[(tc, pr)] = ac.n
        if d_aodbg is not None:
            P.wait("sync", ac, t_aoev[(tc, 1)])
            tdbg = P.dma("sync", qS, d_aodbg[tc * 128:(tc + 1) * 128, :], aof)
            P.wait("scalar", qS, tdbg)
        P.run("scalar", lambda s, bsc=bsc: s.activation(out=bsc[:, 512:516], in_=bstat[:, 0:4], func=AF.Copy, scale=1.0 / H, accum_out=bstat[:, 10:11]))
        P.run("scalar", lambda s, bsc=bsc: s.activation(out=bsc[:, 516:520], in_=bstat[:, 4:8], func=AF.Copy, scale=1.0 / H, accum_out=bstat[:, 11:12]))
        P.run("scalar", lambda s: s.activation(out=bstat[:, 12:13], in_=bstat[:, 10:11], func=AF.Square), ac)
        t_mv = ac.n
        P.wait("vector", ac, t_mv)
        P.run("vector", lambda v: v.tensor_copy(out=dumf[:, 0:1], in_=dumf[:, 1:2]))
        P.run("vector", lambda v: v.tensor_scalar(out=bstat[:, 13:14], in0=bstat[:, 11:12], scalar1=LNEPS, scalar2=bstat[:, 12:13], op0=ALU.add, op1=ALU.subtract), dv)
        t_var = dv.n
        P.wait("scalar", dv, t_var)
        P.run("scalar", lambda s: s.activation(out=bstat[:, 14:15], in_=bstat[:, 13:14], func=AF.Ln))
        P.run("scalar", lambda s: s.activation(out=bstat[:, 15:16], in_=bstat[:, 14:15], func=AF.Exp, scale=-0.5), ac)
        t_r2 = ac.n
        P.wait("vector", ac, t_r2)
        P.run("vector", lambda v: v.tensor_scalar(out=bstat[:, 8:9], in0=bstat[:, 10:11], scalar1=bstat[:, 15:16], scalar2=-1.0, op0=ALU.mult, op1=ALU.mult))
        P.run("vector", lambda v: v.tensor_copy(out=dumf[:, 0:1], in_=dumf[:, 1:2]))
        P.run("vector", lambda v: v.tensor_scalar(out=aof[:, :], in0=aof[:, :], scalar1=bstat[:, 15:16], scalar2=bstat[:, 8:9], op0=ALU.mult, op1=ALU.add), dv)
        t_aff = dv.n
        P.wait("gpsimd", dv, t_aff)
        P.waitt("gpsimd", t_x2[tc])
        P.run("gpsimd", lambda g, tc=tc: g.tensor_add(out=aof, in0=aof, in1=xtc2r[tc % 2]), gp)
        t_gp_res[tc] = gp.n
        P.wait("scalar", gp, t_gp_res[tc])
        t_y1sp[tc] = P.dma("scalar", qY1S, d_y1[tc * 128:(tc + 1) * 128, :], aof)
        if d_y1dbg is not None:
            P.dma("sync", qS, d_y1dbg[tc * 128:(tc + 1) * 128, :], aof)
        P.wait("scalar", gp, t_gp_res[tc])
        P.run("scalar", lambda s: s.activation(out=ontbb, in_=aof, func=AF.Copy, scale=1.0 / H, accum_out=bstat[:, 0:1]))
        P.run("scalar", lambda s: s.activation(out=ontbb, in_=aof, func=AF.Square, scale=SQH, accum_out=bstat[:, 1:2]))
        P.run("scalar", lambda s: s.activation(out=bstat[:, 2:3], in_=bstat[:, 0:1], func=AF.Square), ac)
        t_s2 = ac.n
        P.wait("vector", ac, t_s2)
        P.run("vector", lambda v: v.tensor_copy(out=dumf[:, 0:1], in_=dumf[:, 1:2]))
        P.run("vector", lambda v: v.tensor_scalar(out=bstat[:, 3:4], in0=bstat[:, 1:2], scalar1=LNEPS, scalar2=bstat[:, 2:3], op0=ALU.add, op1=ALU.subtract), dv)
        t_v2 = dv.n
        P.wait("scalar", dv, t_v2)
        P.run("scalar", lambda s: s.activation(out=bstat[:, 5:6], in_=bstat[:, 3:4], func=AF.Ln))
        P.run("scalar", lambda s: s.activation(out=bstat[:, 6:7], in_=bstat[:, 5:6], func=AF.Exp, scale=-0.5), ac)
        t_r3 = ac.n
        P.wait("vector", ac, t_r3)
        P.run("vector", lambda v: v.tensor_scalar(out=bstat[:, 7:8], in0=bstat[:, 0:1], scalar1=bstat[:, 6:7], scalar2=-1.0, op0=ALU.mult, op1=ALU.mult))
        P.run("vector", lambda v: v.tensor_copy(out=dumf[:, 0:1], in_=dumf[:, 1:2]))
        P.run("vector", lambda v: v.tensor_scalar(out=bln2, in0=aof, scalar1=bstat[:, 6:7], scalar2=bstat[:, 7:8], op0=ALU.mult, op1=ALU.add), dv)
        t_bdv[tc] = dv.n
        t_bac[tc] = ac.n
        P.wait("sync", dv, t_bdv[tc])
        t_ln2s[tc] = P.dma("sync", qL2Sa if tc < 8 else qL2Sb, d_ln2[tc * 128:(tc + 1) * 128, :], bln2)
        if tc + 2 < NTC:
            p2_xload(tc + 2)

    P.wait("sync", dv, t_scan_dv[NTC - 1])
    P.wait("sync", pe, t_qlastuse[NTC - 1])
    P.wait("sync", qS, tokfull(qOSP))
    P.wait("sync", qS, tokfull(qASP))
    P.wait("sync", qS, t_tspill)
    P.wait("gpsimd", qS, tokfull(qOSP))
    p2_loads(0)
    p2_loads(1)
    p2_xload(0)
    p2_xload(1)
    for tc in range(NTC):
        emit_A(tc)
        if tc + 1 < NTC:
            p2_loads(tc + 1)
        if tc >= 2:
            emit_B(tc - 2)
        emit_A2(tc)
    emit_B(NTC - 2)
    emit_B(NTC - 1)

    t_half_dv = 0
    t_yev_prev_half = 0
    t_out_dv_half = 0
    t_out_fin = 0
    t_gu_last_half = 0
    for half in range(2):
        toff = half * 1024
        t_l2 = [0] * NHC
        P.wait("sync", qS, t_ln2s[half * 8 + 7])
        if half == 0:
            P.wait("sync", dv, t_bdv[NTC - 1])
            P.wait("sync", ac, t_bac[NTC - 1])
        else:
            P.wait("sync", pe, t_gu_last_half)
        for hc in range(NHC):
            t_l2[hc] = P.dma("sync", qL2F, ln2f[hc], d_ln2[toff:toff + 1024, hc * 128:(hc + 1) * 128], transpose=True)
        P.wait("gpsimd", gp, t_aggu)
        t_wgu = {}
        t_gu_pe = {}
        t_ha = {}

        def load_gu(ch):
            r, l = ch // GU_L, ch % GU_L
            if ch >= 2:
                P.wait("gpsimd", pe, t_gu_pe[ch - 2])
            elif half == 0:
                P.wait("gpsimd", dv, t_bdv[NTC - 1])
                P.wait("gpsimd", ac, t_bac[NTC - 1])
            else:
                P.wait("gpsimd", pe, t_gu_last_half)
            t_wgu[ch] = P.dma("gpsimd", qWGU, wgub[ch % 2][:, :], bass.AP(
                d_aggu_o, (r * 128) * (GU_L * 2048) + l * 2048, [[GU_L * 2048, 128], [1, 2048]]))

        load_gu(0)
        load_gu(1)
        first = True
        for ic in range(NIC):
            for j in range(2):
                ch = 2 * ic + j
                quad = (ic % 2) * 4 + j * 2
                P.wait("tensor", qG, t_wgu[ch])
                if first:
                    if half == 0:
                        P.wait("tensor", dv, t_bdv[NTC - 1])
                        P.wait("tensor", ac, t_bac[NTC - 1])
                    else:
                        P.wait("tensor", ac, t_yev_prev_half)
                    for hc in range(NHC):
                        P.wait("tensor", qS, t_l2[hc])
                    first = False
                if ic >= 2 and j == 0:
                    P.wait("tensor", dv, t_ha.get(ic - 2, 0))
                for es2 in range(2):
                    bank = quad + es2
                    for hc in range(NHC):
                        P.run("tensor", lambda t, ch=ch, hc=hc, es2=es2, bank=bank: t.matmul(
                            psb[bank][:, :], wgub[ch % 2][:, hc * 128:(hc + 1) * 128], ln2f[hc][:, es2 * 512:(es2 + 1) * 512],
                            start=(hc == 0), stop=(hc == NHC - 1)), pe if hc == NHC - 1 else None)
                t_gu_pe[ch] = pe.n
                t_gu_last_half = pe.n
                if ch + 2 < 2 * NIC:
                    load_gu(ch + 2)
            quad = (ic % 2) * 4
            P.wait("scalar", pe, t_gu_pe[2 * ic + 1])
            for es2 in range(2):
                P.run("scalar", lambda s, es2=es2, quad=quad: s.activation(out=glb[:, es2 * 512:(es2 + 1) * 512], in_=psb[quad + es2][:, :], func=AF.Silu), ac)
            t_sil = ac.n
            P.wait("vector", ac, t_sil)
            if half == 0 and ic == 0:
                P.wait("vector", gp, t_gp_res[NTC - 1])
                P.wait("vector", ac, t_bac[NTC - 1])
                P.wait("vector", qS, tokfull(qY1S))
                P.wait("vector", qS, tokfull(qL2Sa))
                P.wait("vector", qS, tokfull(qL2Sb))
            for es2 in range(2):
                P.run("vector", lambda v, ic=ic, es2=es2, quad=quad: v.tensor_mul(
                    out=hact[ic][:, es2 * 512:(es2 + 1) * 512], in0=glb[:, es2 * 512:(es2 + 1) * 512], in1=psb[quad + 2 + es2][:, :]), dv)
            t_ha[ic] = dv.n
        P.wait("gpsimd", gp, t_agdn)
        t_dn_pe = {}
        t_yev = {}

        def load_dn(idx):
            r8, icgq = idx // 4, idx % 4
            if idx >= 2:
                P.wait("gpsimd", pe, t_dn_pe[idx - 2])
            elif half == 0:
                P.wait("gpsimd", ac, t_bac[NTC - 1])
            else:
                P.wait("gpsimd", pe, t_dn_last_half)
            return P.dma("gpsimd", qWDN, wdnb[idx % 2][:, :], bass.AP(
                d_agdn_o, (r8 * 128) * (NIC * DN_COLS) + icgq * 11 * DN_COLS, [[NIC * DN_COLS, 128], [1, 11 * DN_COLS]]))

        t_wdn = {0: load_dn(0), 1: load_dn(1)}
        for r8 in range(8):
            for icgq in range(4):
                idx = r8 * 4 + icgq
                P.wait("tensor", qG, t_wdn[idx])
                if idx == 0:
                    P.wait("tensor", dv, t_ha[NIC - 1])
                    if half == 1:
                        P.wait("tensor", ac, t_yev_prev_half)
                if icgq == 0 and r8 > 0:
                    P.wait("tensor", ac, t_yev[r8 - 1])
                for ic2 in range(11):
                    ic = icgq * 11 + ic2
                    for tc8 in range(8):
                        P.run("tensor", lambda t, ic=ic, ic2=ic2, idx=idx, tc8=tc8: t.matmul(
                            psb[tc8][:, 0:256], hact[ic][:, tc8 * 128:(tc8 + 1) * 128], wdnb[idx % 2][:, ic2 * 256:(ic2 + 1) * 256],
                            start=(ic == 0), stop=(ic == NIC - 1), skip_group_check=True), pe if (ic2 == 10 and tc8 == 7) else None)
                t_dn_pe[idx] = pe.n
                if idx + 2 < 32:
                    t_wdn[idx + 2] = load_dn(idx + 2)
            t_dmm = pe.n
            P.wait("scalar", pe, t_dmm)
            if half == 1 and r8 == 0:
                P.wait("scalar", dv, t_out_dv_half)
            for tc8 in range(8):
                P.run("scalar", lambda s, tc8=tc8, r8=r8: s.activation(out=ysb[tc8][:, r8 * 256:(r8 + 1) * 256], in_=psb[tc8][:, 0:256], func=AF.Copy, accum_out=systat[:, tc8:tc8 + 1]))
                P.run("scalar", lambda s, tc8=tc8: s.activation(out=glf[:, 0:256], in_=psb[tc8][:, 0:256], func=AF.Square, accum_out=systat[:, 8 + tc8:9 + tc8]), ac)
            t_yev[r8] = ac.n
            P.wait("vector", ac, t_yev[r8])
            for tc8 in range(8):
                if r8 == 0:
                    P.run("vector", lambda v, tc8=tc8: v.tensor_copy(out=stat[:, tc8:tc8 + 1], in_=systat[:, tc8:tc8 + 1]))
                    P.run("vector", lambda v, tc8=tc8: v.tensor_copy(out=stat[:, 8 + tc8:9 + tc8], in_=systat[:, 8 + tc8:9 + tc8]))
                else:
                    P.run("vector", lambda v, tc8=tc8: v.tensor_add(out=stat[:, tc8:tc8 + 1], in0=stat[:, tc8:tc8 + 1], in1=systat[:, tc8:tc8 + 1]))
                    P.run("vector", lambda v, tc8=tc8: v.tensor_add(out=stat[:, 8 + tc8:9 + tc8], in0=stat[:, 8 + tc8:9 + tc8], in1=systat[:, 8 + tc8:9 + tc8]))
            P.run("vector", lambda v: v.engine_nop(), dv)
        t_dstat = dv.n
        t_yev_prev_half = t_yev[7]
        t_dn_last_half = t_dn_pe[31]
        d_yfdbg = dbgout("yffn", [NTC, 128, H], BF16)
        t_yfdump = [0] * 8
        if d_yfdbg is not None:
            for tc8 in range(8):
                P.wait("sync", ac, t_yev[7])
                t_yfdump[tc8] = P.dma("sync", qS, d_yfdbg[half * 8 + tc8], ysb[tc8][:, :])
        t_out_dv = 0
        for tc8 in range(8):
            tc = half * 8 + tc8
            if tc8 >= 1:
                P.wait("sync", dv, t_out_dv)
            elif half == 0:
                P.wait("sync", ac, t_bac[NTC - 1])
                P.wait("sync", qS, tokfull(qY1S))
            t_y1r = P.dma("sync", qY1R, y1rb, d_y1[tc * 128:(tc + 1) * 128, :])
            P.wait("vector", dv, t_dstat)
            P.run("vector", lambda v, tc8=tc8: v.tensor_scalar_mul(out=systat[:, 16:17], in0=stat[:, tc8:tc8 + 1], scalar1=1.0 / H))
            P.run("vector", lambda v, tc8=tc8: v.tensor_scalar_mul(out=systat[:, 17:18], in0=stat[:, 8 + tc8:9 + tc8], scalar1=1.0 / H))
            P.run("vector", lambda v: v.tensor_mul(out=systat[:, 18:19], in0=systat[:, 16:17], in1=systat[:, 16:17]))
            P.run("vector", lambda v: v.tensor_scalar(out=systat[:, 19:20], in0=systat[:, 17:18], scalar1=LNEPS, scalar2=FEPS * LNEPS, op0=ALU.mult, op1=ALU.add))
            P.run("vector", lambda v: v.tensor_sub(out=systat[:, 18:19], in0=systat[:, 17:18], in1=systat[:, 18:19]))
            P.run("vector", lambda v: v.tensor_copy(out=dumf[:, 0:1], in_=dumf[:, 1:2]))
            P.run("vector", lambda v: v.tensor_add(out=systat[:, 19:20], in0=systat[:, 18:19], in1=systat[:, 19:20]), dv)
            t_arg = dv.n
            P.wait("scalar", dv, t_arg)
            P.run("scalar", lambda s: s.activation(out=systat[:, 20:21], in_=systat[:, 19:20], func=AF.Ln))
            P.run("scalar", lambda s: s.activation(out=systat[:, 20:21], in_=systat[:, 20:21], func=AF.Exp, scale=-0.5), ac)
            t_rr = ac.n
            P.wait("vector", ac, t_rr)
            P.run("vector", lambda v: v.tensor_scalar(out=systat[:, 21:22], in0=systat[:, 16:17], scalar1=systat[:, 20:21], scalar2=-1.0, op0=ALU.mult, op1=ALU.mult))
            P.run("vector", lambda v: v.tensor_copy(out=dumf[:, 0:1], in_=dumf[:, 1:2]))
            if d_yfdbg is not None:
                P.wait("vector", qS, t_yfdump[tc8])
            P.run("vector", lambda v, tc8=tc8: v.tensor_scalar(out=ysb[tc8][:, :], in0=ysb[tc8][:, :], scalar1=systat[:, 20:21], scalar2=systat[:, 21:22], op0=ALU.mult, op1=ALU.add), dv)
            P.wait("vector", qS, t_y1r)
            P.run("vector", lambda v, tc8=tc8: v.tensor_add(out=y1rb, in0=ysb[tc8][:, :], in1=y1rb), dv)
            t_out_dv = dv.n
            P.wait("sync", dv, t_out_dv)
            t_out_fin = P.dma("sync", qS, d_out[tc * 128:(tc + 1) * 128, :], y1rb)
        t_out_dv_half = t_out_dv

    d_qdbg = dbgout("qq", [NTC * NHC * 128, 128], BF16)
    if d_qdbg is not None:
        P.wait("gpsimd", qS, qS.n)
        P.dma("gpsimd", qG, d_qdbg[:, :], d_q[:, :])
    d_ln2dbg = dbgout("ln2x", [T, H], BF16)
    if d_ln2dbg is not None:
        P.wait("gpsimd", qS, qS.n)
        P.dma("gpsimd", qG, d_ln2dbg[:, :], d_ln2[:, :])
    P.wait("gpsimd", qS, qS.n)
    P.wait("gpsimd", qG, qG.n)
    P.wait("gpsimd", qA, qA.n)

    with nc.Block() as block:
        @block.sync
        def _(e):
            for fn in P.ops["sync"]:
                fn(e)

        @block.scalar
        def _(e):
            for fn in P.ops["scalar"]:
                fn(e)

        @block.vector
        def _(e):
            for fn in P.ops["vector"]:
                fn(e)

        @block.tensor
        def _(e):
            for fn in P.ops["tensor"]:
                fn(e)

        @block.gpsimd
        def _(e):
            for fn in P.ops["gpsimd"]:
                fn(e)

    return nc


def _colperm_gu():
    cp = np.empty(2 * II, dtype=np.int64)
    for ic in range(NIC):
        blk = PERM2[ic * 128:(ic + 1) * 128]
        cp[(2 * ic) * 128:(2 * ic + 1) * 128] = blk
        cp[(2 * ic + 1) * 128:(2 * ic + 2) * 128] = II + blk
    return cp


def prep_inputs(L0_attn_x, L0_attn_log_decay, L0_attn_Wq, L0_attn_Wk, L0_attn_Wv,
                L0_attn_Wo, L0_attn_Wg1, L0_attn_Wg2, L0_attn_norm_w,
                L0_ffn_gate_up_packed, L0_ffn_gate_up_scales, L0_ffn_down_packed,
                L0_ffn_down_scales):
    perm = PERM
    cpgu = _colperm_gu()
    wq = np.ascontiguousarray(np.asarray(L0_attn_Wq, dtype=np.float32)[perm, :])
    wk = np.ascontiguousarray(np.asarray(L0_attn_Wk, dtype=np.float32)[perm, :])
    wv = np.ascontiguousarray(np.asarray(L0_attn_Wv, dtype=np.float32)[perm, :])
    wo = np.ascontiguousarray(np.asarray(L0_attn_Wo, dtype=np.float32)[:, perm])
    wg1 = np.ascontiguousarray(np.asarray(L0_attn_Wg1, dtype=np.float32)[perm, :])
    wg2 = np.ascontiguousarray(np.asarray(L0_attn_Wg2, dtype=np.float32))
    nw = np.ascontiguousarray(np.asarray(L0_attn_norm_w, dtype=np.float32).reshape(H, 1))
    ldrow = np.ascontiguousarray(np.asarray(L0_attn_log_decay, dtype=np.float32).reshape(1, NH))

    gu_b = np.asarray(L0_ffn_gate_up_packed, dtype=np.int32).view(np.uint8).reshape(2 * II, 1024)
    gu_bt = gu_b.T[:, cpgu]
    gu_sc = np.asarray(L0_ffn_gate_up_scales)[:, cpgu]

    dn_b = np.asarray(L0_ffn_down_packed, dtype=np.int32).view(np.uint8).reshape(H, 2816)
    dn_bt = dn_b[perm, :].T
    dn_sc = np.asarray(L0_ffn_down_scales)[:, perm]

    grow = (np.arange(128)[None, :] >= 64).astype(np.int64)
    x = np.asarray(L0_attn_x, dtype=np.float32)
    in_maps = []
    for c in range(CORES):
        b, qt = c // 4, c % 4
        xs = np.ascontiguousarray(x[b, qt * T:(qt + 1) * T, :][:, perm])
        sl = gu_bt[:, c * GU_COLS:(c + 1) * GU_COLS]
        pk = np.ascontiguousarray(sl.reshape(8, 128, GU_L, 128).transpose(2, 1, 0, 3).reshape(GU_L, 128, 1024))
        g_idx = (2 * np.arange(8)[:, None] + grow).astype(np.int64)
        scs = gu_sc[:, c * GU_COLS:(c + 1) * GU_COLS]
        sc_lpuc = np.ascontiguousarray(
            scs[g_idx].reshape(8, 128, GU_L, 128).transpose(2, 1, 0, 3).reshape(GU_L, 128, 1024).astype(np.float16))
        dsl = dn_bt[:, c * DN_COLS:(c + 1) * DN_COLS]
        pkd = np.ascontiguousarray(dsl.reshape(DN_U, 128, DN_COLS))
        g2 = (2 * np.arange(DN_U)[:, None] + grow).astype(np.int64)
        dscs = dn_sc[:, c * DN_COLS:(c + 1) * DN_COLS]
        sc_d = np.ascontiguousarray(dscs[g2].astype(np.float16))
        gap = np.full((1, CORES), 1e6, dtype=np.float32)
        for r in range(CORES):
            br, qr = r // 4, r % 4
            if br == b and qr < qt:
                gap[0, r] = float(qt - 1 - qr)
        in_maps.append({
            "x": xs, "wq": wq, "wk": wk, "wv": wv, "wo": wo, "wg1": wg1, "wg2": wg2,
            "nw": nw, "ldec": ldrow, "gap": gap,
            "pkgu": pk, "scgu": sc_lpuc, "pkdn": pkd, "scdn": sc_d,
        })
    return in_maps


def kernel(**inputs):
    in_maps = prep_inputs(**inputs)
    es = ExitStack()
    try:
        nc = build(es)
        res = run_bass_kernel_spmd(nc, in_maps, core_ids=list(range(CORES)))
    finally:
        es.close()
    inv = np.argsort(PERM)
    out = np.empty((2, 8192, H), dtype=np.float32)
    for c in range(CORES):
        b, qt = c // 4, c % 4
        out[b, qt * T:(qt + 1) * T, :] = res.results[c]["out"][:, inv]
    return out


# version 20
# speedup vs baseline: 1.0016x; 1.0016x over previous
import numpy as np
from contextlib import ExitStack

import concourse.bass as bass
import concourse.mybir as mybir
from concourse.bass_utils import run_bass_kernel_spmd

F32 = mybir.dt.float32
BF16 = mybir.dt.bfloat16
F16 = mybir.dt.float16
I8 = mybir.dt.int8
U8 = mybir.dt.uint8
AF = mybir.ActivationFunctionType
ALU = mybir.AluOpType
AXX = mybir.AxisListType.X

H = 2048
T = 2048
NH = 16
NTC = 16
NHC = 16
II = 5632
NIC = 44
CORES = 8
GU_COLS = 11264 // CORES
GU_L = GU_COLS // 128
DN_COLS = 2048 // CORES
DN_U = 2816 // 128
LNEPS = 1e-5
SQH = (1.0 / 2048.0) ** 0.5
GEPS = 1e-5
FEPS = 1e-6


def _perm(nchunks):
    p = np.empty(nchunks * 128, dtype=np.int64)
    for k in range(nchunks):
        u, j = k // 2, k % 2
        p[k * 128:(k + 1) * 128] = 256 * u + 2 * np.arange(128) + j
    return p

PERM = _perm(NHC)
PERM2 = _perm(NIC)


class Sem:
    def __init__(self, h):
        self.h = h
        self.n = 0

    def inc(self, k=1):
        self.n += k
        return self.n


class Lanes:

    def __init__(self, mk, n, name):
        self.sems = [mk(f"{name}{i}") for i in range(n)]
        self.n = n
        self.k = 0

    def nxt(self):
        lane = self.sems[self.k % self.n]
        self.k += 1
        return lane


class Plan:
    ENGINES = ("sync", "scalar", "vector", "tensor", "gpsimd")

    def __init__(self):
        self.ops = {e: [] for e in self.ENGINES}

    def on(self, eng, fn):
        self.ops[eng].append(fn)

    def dma(self, eng, sem, out, in_, **kw):
        if isinstance(sem, Lanes):
            sem = sem.nxt()
        t = sem.inc(16)
        self.on(eng, lambda e, out=out, in_=in_, kw=kw, s=sem.h: e.dma_start(out=out, in_=in_, **kw).then_inc(s, 16))
        return (sem, t)

    def waitt(self, eng, tok):
        sem, t = tok
        self.on(eng, lambda e, s=sem.h, t=t: e.wait_ge(s, t))

    def wait(self, eng, sem, tgt):
        if isinstance(tgt, tuple):
            sem, tgt = tgt
        if tgt and tgt > 0:
            self.on(eng, lambda e, s=sem.h, t=tgt: e.wait_ge(s, t))

    def waitl(self, eng, sem, holder, idx):
        def go(e, s=sem.h, holder=holder, idx=idx):
            t = holder[idx]
            if isinstance(t, tuple):
                return e.wait_ge(t[0].h, t[1])
            if t and t > 0:
                return e.wait_ge(s, t)
            return e.nop()
        self.on(eng, go)

    def run(self, eng, fn, sem=None, k=1):
        if sem is None:
            self.on(eng, fn)
            return None
        t = sem.inc(k)
        self.on(eng, lambda e, fn=fn, s=sem.h, k=k: fn(e).then_inc(s, k))
        return t


def build(es, debug=()):
    nc = bass.Bass("TRN2", target_bir_lowering=False)
    dbg = set(debug)

    d_x = nc.dram_tensor("x", [T, H], F32, kind="ExternalInput")
    d_wq = nc.dram_tensor("wq", [H, H], F32, kind="ExternalInput")
    d_wk = nc.dram_tensor("wk", [H, H], F32, kind="ExternalInput")
    d_wv = nc.dram_tensor("wv", [H, H], F32, kind="ExternalInput")
    d_wo = nc.dram_tensor("wo", [H, H], F32, kind="ExternalInput")
    d_wg1 = nc.dram_tensor("wg1", [H, 128], F32, kind="ExternalInput")
    d_wg2 = nc.dram_tensor("wg2", [128, H], F32, kind="ExternalInput")
    d_nw = nc.dram_tensor("nw", [H, 1], F32, kind="ExternalInput")
    d_ld = nc.dram_tensor("ldec", [1, NH], F32, kind="ExternalInput")
    d_gap = nc.dram_tensor("gap", [1, CORES], F32, kind="ExternalInput")
    d_pkgu = nc.dram_tensor("pkgu", [GU_L, 128, 1024], U8, kind="ExternalInput")
    d_scgu = nc.dram_tensor("scgu", [GU_L, 128, 1024], F16, kind="ExternalInput")
    d_pkdn = nc.dram_tensor("pkdn", [DN_U, 128, DN_COLS], U8, kind="ExternalInput")
    d_scdn = nc.dram_tensor("scdn", [DN_U, 128, DN_COLS], F16, kind="ExternalInput")
    d_out = nc.dram_tensor("out", [T, H], F32, kind="ExternalOutput")

    d_ln1g = nc.dram_tensor("ln1g", [NTC * NHC * 128, 128], BF16)
    d_q = nc.dram_tensor("qd", [NTC * NHC * 128, 128], BF16)
    d_oi = nc.dram_tensor("oid", [NTC, 128, H], BF16)
    d_a = nc.dram_tensor("ad", [NTC, 128, H], BF16)
    d_mid = nc.dram_tensor("midd", [128, H], BF16)
    d_y1 = nc.dram_tensor("y1d", [T, H], F32)
    d_ln2 = nc.dram_tensor("ln2d", [T, H], BF16)
    d_agt_i = nc.dram_tensor("agti", [128, H], F32)
    d_agt_o = nc.dram_tensor("agto", [CORES * 128, H], F32, addr_space="Shared")
    d_aggu_i = nc.dram_tensor("aggui", [128, GU_L * 2048], BF16)
    d_aggu_o = nc.dram_tensor("agguo", [CORES * 128, GU_L * 2048], BF16, addr_space="Shared")
    d_agdn_i = nc.dram_tensor("agdni", [128, NIC * DN_COLS], BF16)
    d_agdn_o = nc.dram_tensor("agdno", [CORES * 128, NIC * DN_COLS], BF16, addr_space="Shared")

    d_dbg = {}
    def dbgout(name, shape, dt=F32):
        if name in dbg:
            if name not in d_dbg:
                d_dbg[name] = nc.dram_tensor("dbg_" + name, shape, dt, kind="ExternalOutput")
            return d_dbg[name]
        return None

    sbt = lambda n_, sh, dt: es.enter_context(nc.sbuf_tensor(n_, sh, dt))
    ident = sbt("ident", [128, 128], BF16)
    iorow = sbt("iorow", [128, 128], F32)
    iocol = sbt("iocol", [128, 1], F32)
    ld_rep = sbt("ldrep", [128, NH], F32)
    ldneg = sbt("ldneg", [128, NH], F32)
    qdecs = sbt("qdecs", [128, NH], F32)
    kdecs = sbt("kdecs", [128, NH], F32)
    cds = sbt("cds", [128, NH], F32)
    gapr = sbt("gapr", [128, CORES], F32)
    coefr = sbt("coefr", [128, NH], F32)
    tmask = sbt("tmask", [128, 128], F32)
    dargm = sbt("dargm", [128, 128], F32)
    dtmp = sbt("dtmp", [128, 128], F32)
    stat = sbt("stat", [128, 24], F32)
    bstatr = [sbt("bstat0", [128, 16], F32), sbt("bstat1", [128, 16], F32)]
    systat = sbt("systat", [128, 24], F32)
    nwcol = sbt("nwcol", [128, NHC], F32)
    dmsk = sbt("dmsk", [128, NH * 128], BF16)
    cepsA = sbt("cepsA", [128, 1], F32)
    dumf = sbt("dumf", [128, 2], F32)
    kdk8 = sbt("kdk8", [128, 1024], BF16)
    kfm8 = sbt("kfm8", [128, 1024], BF16)
    sdb8 = sbt("sdb8", [128, 1024], BF16)
    sdt8 = sbt("sdt8", [128, 1024], BF16)
    ktcB = sbt("ktcB", [128, H], BF16)[:, :]
    vtcB = sbt("vtcB", [128, H], BF16)[:, :]

    KB = 1024
    R1, R2, R3 = 0, 64 * KB, 128 * KB
    ARN = 182 * KB
    arena = sbt("arena", [128, ARN], U8)

    def carve(off, nelem, dt):
        sz = {F32: 4, BF16: 2, F16: 2, I8: 1, U8: 1}[dt]
        ap = arena[:, off:off + nelem * sz].bitcast(dt)
        return ap, off + nelem * sz

    def carve_list(off, n, nelem, dt):
        out = []
        for _ in range(n):
            ap, off = carve(off, nelem, dt)
            out.append(ap)
        return out, off

    ln1f, o1 = carve_list(R1, NHC, T, BF16)
    assert o1 <= R2
    wv_sb, _ = carve_list(R1, NHC, H, BF16)
    wo_sb = wv_sb
    ysb, o = carve_list(R1, 8, H, BF16)
    y1rb, o = carve(o, H, F32)
    wdnb, o = carve_list(o, 2, 11 * DN_COLS, BF16)
    glb, o = carve(o, 1024, BF16)
    glf = glb.bitcast(F32)
    hact_r1_base = o
    assert hact_r1_base + 5 * 2 * KB <= R2, hact_r1_base

    o = R2
    gub, o = carve_list(o, GU_L, 1024, U8)
    gscr, o = carve_list(o, GU_L, 1024, F16)
    gdec, o = carve_list(o, 4, 2048, BF16)
    dvi, o = carve_list(o, 3, 1024, I8)
    dvf, o = carve_list(o, 2, 1024, F32)
    assert o <= R3, o
    dnbb = gub[0].tensor if False else None
    dnbb, _ = carve(R2, DN_U * DN_COLS, U8)
    dnsr2, _ = carve(R2 + GU_L * KB, DN_U * DN_COLS, F16)
    dnstg = [gdec[0], gdec[1]]
    wk_sb, _ = carve_list(R2, NHC, H, BF16)
    o = R2
    sinb, o = carve(o, H, F32)
    sbfb, o = carve(o, H, BF16)
    oitcr, o = carve_list(o, 2, H, BF16)
    a2tcr, o = carve_list(o, 2, H, BF16)
    onbr, o = carve_list(o, 2, H, BF16)
    ontbb, o = carve(o, H, BF16)
    aof0, o = carve(o, H, F32)
    bln20, o = carve(o, H, BF16)
    gsigb, o = carve(o, H, BF16)
    peerb, o = carve(o, H, BF16)
    wg2, o = carve(o, H, BF16)
    assert o <= R3, o
    o = R2
    ln2f, o = carve_list(o, NHC, 1024, BF16)
    wgub, o = carve_list(o, 2, 2048, BF16)
    hact_r2_base = o
    assert hact_r2_base + 12 * 2 * KB <= R3

    o = R3
    xb2, o = carve_list(o, 2, H, F32)
    ln1b, o = carve(o, H, BF16)
    o = R3
    wqcolw, o = carve_list(o, 2, H, BF16)
    wg1, o = carve(o, H, BF16)
    qout, o = carve_list(o, 4, 512, BF16)
    wqf32, o = carve_list(o, 2, H, F32)
    o = R3
    ln1tc, o = carve(o, H, BF16)
    ktc, o = carve(o, H, BF16)
    vtc, o = carve(o, H, BF16)
    qtc2, o = carve(o, 2 * H, BF16)
    otc, o = carve(o, H, BF16)
    atc, o = carve(o, H, BF16)
    tstate, o = carve(o, H, F32)
    midb, o = carve(o, H, BF16)
    assert o <= R3 + 48 * KB, o
    xtc2r = [carve(R3 + 28 * KB, H, F32)[0], carve(R3, H, F32)[0]]
    aofr = [aof0, carve(R3 + 40 * KB, H, F32)[0]]
    bln2r = [bln20, carve(R3 + 48 * KB, H, BF16)[0]]
    qtc = [qtc2[:, 0:H], qtc2[:, H:2 * H]]
    hact = []
    for i in range(27):
        apx, _ = carve(R3 + i * 2 * KB, 1024, BF16)
        hact.append(apx)
    for i in range(12):
        apx, _ = carve(hact_r2_base + i * 2 * KB, 1024, BF16)
        hact.append(apx)
    for i in range(5):
        apx, _ = carve(hact_r1_base + i * 2 * KB, 1024, BF16)
        hact.append(apx)

    psb = [es.enter_context(nc.psum_tensor(f"pb{i}", [128, 512], F32)) for i in range(8)]

    mksem = lambda n_: Sem(es.enter_context(nc.semaphore(n_)))
    qS = mksem("qS")
    qG = mksem("qG")
    qA = mksem("qA")
    pe = mksem("pe")
    dv = mksem("dv")
    ac = mksem("ac")
    gp = mksem("gp")
    tokfull = lambda sm: (sm, sm.n)
    qLD = mksem("qLD"); qNW = mksem("qNW")
    qX = Lanes(mksem, 2, "qX")
    qQSP = Lanes(mksem, 4, "qQSP")
    qLNF = mksem("qLNF")
    qL1 = mksem("qL1")
    qQT = Lanes(mksem, 2, "qQT")
    qOSP = mksem("qOSP"); qASP = mksem("qASP"); qTSPs = mksem("qTSPs")
    qOI = Lanes(mksem, 2, "qOI"); qA2 = Lanes(mksem, 2, "qA2"); qX2 = Lanes(mksem, 2, "qX2")
    qY1S = mksem("qY1S"); qL2Sa = mksem("qL2Sa"); qL2Sb = mksem("qL2Sb")
    qL2F = mksem("qL2F"); qY1R = mksem("qY1R"); qPEER = mksem("qPEER")
    qWQ = Lanes(mksem, 2, "qWQ"); qWG1 = mksem("qWG1"); qWG2 = mksem("qWG2")
    qWK = mksem("qWK"); qWV = mksem("qWV"); qWO = mksem("qWO")
    qGUB = mksem("qGUB"); qDNB = mksem("qDNB")
    qGSC = Lanes(mksem, 4, "qGSC"); qDNS = Lanes(mksem, 4, "qDNS")
    qGSP = Lanes(mksem, 4, "qGSP"); qDSP = Lanes(mksem, 2, "qDSP")
    qWGU = Lanes(mksem, 2, "qWGU"); qWDN = Lanes(mksem, 2, "qWDN")

    P = Plan()
    bf = lambda ap: ap.bitcast(BF16)

    for l in range(GU_L):
        P.dma("scalar", qGUB, gub[l], d_pkgu[l])
    for l in range(GU_L):
        P.dma("scalar", qGUB, gscr[l], d_scgu[l])
    t_gu_in = tokfull(qGUB)

    P.run("gpsimd", lambda g: g.iota(iocol[:, :], [[1, 1]], channel_multiplier=1, allow_small_or_imprecise_dtypes=True))
    P.run("gpsimd", lambda g: g.iota(iorow[:, :], [[1, 128]], channel_multiplier=0, allow_small_or_imprecise_dtypes=True), gp)
    t_iota = gp.n
    t_ln1tm = [0] * NTC
    t_qgrp_pe = [0] * NHC

    P.dma("sync", qLD, ld_rep[:, :], bass.AP(d_ld, 0, [[0, 128], [1, NH]]))
    P.dma("sync", qLD, gapr[:, :], bass.AP(d_gap, 0, [[0, 128], [1, CORES]]))
    t_ldrep = tokfull(qLD)
    for ec in range(NHC):
        P.dma("sync", qNW, nwcol[:, ec:ec + 1], d_nw[ec * 128:(ec + 1) * 128, :])
    t_nw = tokfull(qNW)

    P.wait("vector", gp, t_iota)
    P.run("vector", lambda v: v.tensor_scalar(out=ident[:, :], in0=iorow[:, :], scalar1=iocol[:, 0:1], scalar2=None, op0=ALU.is_equal), dv)
    t_ident = dv.n
    P.run("vector", lambda v: v.memset(cepsA[:, :], LNEPS), dv)
    t_ceps = dv.n
    P.wait("vector", qS, t_ldrep)
    P.run("vector", lambda v: v.tensor_scalar_mul(out=ldneg[:, :], in0=ld_rep[:, :], scalar1=-1.0))
    P.run("vector", lambda v: v.tensor_scalar(out=tmask[:, :], in0=iorow[:, :], scalar1=iocol[:, 0:1], scalar2=None, op0=ALU.subtract))
    P.run("vector", lambda v: v.tensor_scalar_min(out=dargm[:, :], in0=tmask[:, :], scalar1=0.0))
    P.run("vector", lambda v: v.tensor_scalar(out=tmask[:, :], in0=tmask[:, :], scalar1=0.0, scalar2=None, op0=ALU.is_le), dv)
    t_dargs = dv.n

    P.wait("scalar", qS, t_ldrep)
    P.wait("scalar", gp, t_iota)
    P.run("scalar", lambda s: s.activation(out=stat[:, 0:1], in_=iocol[:, 0:1], func=AF.Copy, bias=1.0))
    P.run("scalar", lambda s: s.activation(out=stat[:, 1:2], in_=iocol[:, 0:1], func=AF.Copy, scale=-1.0, bias=127.0))
    for h in range(NH):
        P.run("scalar", lambda s, h=h: s.activation(out=qdecs[:, h:h + 1], in_=stat[:, 0:1], func=AF.Exp, scale=ld_rep[:, h:h + 1]))
        P.run("scalar", lambda s, h=h: s.activation(out=kdecs[:, h:h + 1], in_=stat[:, 1:2], func=AF.Exp, scale=ld_rep[:, h:h + 1]))
        P.run("scalar", lambda s, h=h: s.activation(out=cds[:, h:h + 1], in_=ld_rep[:, h:h + 1], func=AF.Exp, scale=128.0))
    P.run("scalar", lambda s: s.nop(), ac)
    t_decay = ac.n
    P.wait("scalar", dv, t_dargs)
    t_dmh = 0
    for h in range(NH):
        if t_dmh:
            P.wait("scalar", dv, t_dmh)
        P.run("scalar", lambda s, h=h: s.activation(out=dtmp[:, :], in_=dargm[:, :], func=AF.Exp, scale=ldneg[:, h:h + 1]), ac)
        th = ac.n
        P.wait("vector", ac, th)
        P.run("vector", lambda v, h=h: v.tensor_mul(out=dmsk[:, h * 128:(h + 1) * 128], in0=dtmp[:, :], in1=tmask[:, :]), dv)
        t_dmh = dv.n
    t_dmask = dv.n

    def dec_ops(e, src3, n8, m8, v8, f1, f2, scl3, dst3, hi):
        if hi:
            e.tensor_scalar(out=m8, in0=src3, scalar1=4, scalar2=7, op0=ALU.logical_shift_right, op1=ALU.bitwise_and)
            e.tensor_scalar(out=n8, in0=src3, scalar1=4, scalar2=8, op0=ALU.logical_shift_right, op1=ALU.bitwise_and)
        else:
            e.tensor_scalar(out=m8, in0=src3, scalar1=7, scalar2=None, op0=ALU.bitwise_and)
            e.tensor_scalar(out=n8, in0=src3, scalar1=8, scalar2=None, op0=ALU.bitwise_and)
        e.tensor_scalar(out=v8, in0=m8, scalar1=2, scalar2=-4, op0=ALU.mult, op1=ALU.add)
        e.tensor_tensor(out=v8, in0=m8, in1=v8, op=ALU.max)
        e.tensor_scalar(out=m8, in0=m8, scalar1=4, scalar2=-16, op0=ALU.mult, op1=ALU.add)
        e.tensor_tensor(out=v8, in0=v8, in1=m8, op=ALU.max)
        e.tensor_scalar(out=f1, in0=n8, scalar1=-0.125, scalar2=0.5, op0=ALU.mult, op1=ALU.add)
        e.tensor_tensor(out=f2, in0=v8, in1=f1, op=ALU.mult)
        return e.tensor_tensor(out=dst3, in0=f2, in1=scl3, op=ALU.mult)

    def r3(ap, c):
        return ap.rearrange("p (u c) -> p u c", c=c)

    t_glo = [0] * GU_L
    t_ghi = [0] * GU_L
    t_gsp = [0] * (GU_L + 1)

    def emit_gu_l(l):
        src = r3(gub[l].bitcast(I8), 128)
        scl3 = r3(gscr[l], 128)
        ob = gdec[l % 4]
        dlo = r3(ob, 256)[:, :, 0:128]
        dhi = r3(ob, 256)[:, :, 128:256]
        P.wait("vector", qA, t_gu_in)
        if l >= 4:
            P.wait("vector", qG, t_gsp[l - 4])
        P.run("vector", lambda v, src=src, scl3=scl3, dlo=dlo: dec_ops(
            v, src, r3(dvi[0], 128), r3(dvi[1], 128), r3(dvi[2], 128), r3(dvf[0], 128), r3(dvf[1], 128), scl3, dlo, False), dv)
        t_glo[l] = dv.n
        P.run("vector", lambda v, src=src, scl3=scl3, dhi=dhi: dec_ops(
            v, src, r3(dvi[0], 128), r3(dvi[1], 128), r3(dvi[2], 128), r3(dvf[0], 128), r3(dvf[1], 128), scl3, dhi, True), dv)
        t_ghi[l] = dv.n

    d_ln1dbg = dbgout("ln1", [T, H], BF16)
    t_x = [0] * NTC
    t_tr_pe = [0] * NTC
    t_sqL = [0] * NTC
    prev2 = [0, 0]
    for tc in range(NTC):
        if tc >= 2:
            P.wait("sync", dv, t_ln1tm[tc - 2])
            P.wait("sync", ac, t_sqL[tc - 2])
        t_x[tc] = P.dma("sync", qX, xb2[tc % 2], d_x[tc * 128:(tc + 1) * 128, :])
        xb = xb2[tc % 2]
        P.wait("vector", qS, t_x[tc])
        P.run("vector", lambda v, xb=xb: v.tensor_reduce(out=stat[:, 2:3], in_=xb, axis=AXX, op=ALU.add), dv)
        P.wait("scalar", qS, t_x[tc])
        if tc >= 1:
            P.wait("scalar", pe, t_tr_pe[tc - 1])
        P.run("scalar", lambda s, xb=xb: s.activation(out=ln1b[:, 0:1024], in_=xb[:, 0:1024], func=AF.Square, accum_out=stat[:, 3:4]))
        P.run("scalar", lambda s, xb=xb: s.activation(out=ln1b[:, 1024:2048], in_=xb[:, 1024:2048], func=AF.Square, accum_out=stat[:, 10:11]), ac)
        t_sq = ac.n
        t_sqL[tc] = t_sq
        P.wait("vector", ac, t_sq)
        P.run("vector", lambda v: v.tensor_add(out=stat[:, 3:4], in0=stat[:, 3:4], in1=stat[:, 10:11]))
        P.run("vector", lambda v: v.tensor_scalar_mul(out=stat[:, 4:5], in0=stat[:, 2:3], scalar1=1.0 / H))
        P.run("vector", lambda v: v.tensor_scalar_mul(out=stat[:, 5:6], in0=stat[:, 3:4], scalar1=1.0 / H))
        P.run("vector", lambda v: v.tensor_mul(out=stat[:, 6:7], in0=stat[:, 4:5], in1=stat[:, 4:5]))
        P.run("vector", lambda v: v.tensor_copy(out=dumf[:, 0:1], in_=dumf[:, 1:2]))
        P.run("vector", lambda v: v.tensor_sub(out=stat[:, 6:7], in0=stat[:, 5:6], in1=stat[:, 6:7]), dv)
        t_var = dv.n
        P.wait("scalar", dv, t_var)
        P.run("scalar", lambda s: s.activation(out=stat[:, 7:8], in_=stat[:, 6:7], func=AF.Ln, bias=cepsA[:, 0:1]))
        P.run("scalar", lambda s: s.activation(out=stat[:, 8:9], in_=stat[:, 7:8], func=AF.Exp, scale=-0.5), ac)
        t_rstd = ac.n
        P.wait("vector", ac, t_rstd)
        P.run("vector", lambda v: v.tensor_scalar(out=stat[:, 9:10], in0=stat[:, 4:5], scalar1=stat[:, 8:9], scalar2=-1.0, op0=ALU.mult, op1=ALU.mult))
        P.run("vector", lambda v: v.tensor_copy(out=dumf[:, 0:1], in_=dumf[:, 1:2]))
        if tc >= 1:
            P.wait("vector", pe, t_tr_pe[tc - 1])
        P.run("vector", lambda v, xb=xb: v.tensor_scalar(out=ln1b, in0=xb, scalar1=stat[:, 8:9], scalar2=stat[:, 9:10], op0=ALU.mult, op1=ALU.add), dv)
        t_ln1tm[tc] = dv.n
        if d_ln1dbg is not None:
            P.wait("sync", dv, t_ln1tm[tc])
            P.dma("sync", qS, d_ln1dbg[tc * 128:(tc + 1) * 128, :], ln1b)
        P.wait("tensor", dv, t_ln1tm[tc])
        if tc == 0:
            P.wait("tensor", dv, t_ident)
        for hc in range(NHC):
            bank = 6 + (hc % 2)
            P.wait("tensor", dv, prev2[hc % 2])
            P.run("tensor", lambda t, hc=hc, bank=bank: t.transpose(bf(psb[bank][:, 0:64]), ln1b[:, hc * 128:(hc + 1) * 128], ident[:, :]), pe)
            tp = pe.n
            P.wait("vector", pe, tp)
            P.run("vector", lambda v, hc=hc, tc=tc, bank=bank: v.tensor_copy(out=ln1f[hc][:, tc * 128:(tc + 1) * 128], in_=bf(psb[bank][:, 0:64])), dv)
            prev2[hc % 2] = dv.n
        t_tr_pe[tc] = pe.n
    t_ln1_all = dv.n
    for l in range(GU_L):
        emit_gu_l(l)

    def dec_ops(e, src3, n8, m8, v8, f1, f2, scl3, dst3, hi):
        if hi:
            e.tensor_scalar(out=m8, in0=src3, scalar1=4, scalar2=7, op0=ALU.logical_shift_right, op1=ALU.bitwise_and)
            e.tensor_scalar(out=n8, in0=src3, scalar1=4, scalar2=8, op0=ALU.logical_shift_right, op1=ALU.bitwise_and)
        else:
            e.tensor_scalar(out=m8, in0=src3, scalar1=7, scalar2=None, op0=ALU.bitwise_and)
            e.tensor_scalar(out=n8, in0=src3, scalar1=8, scalar2=None, op0=ALU.bitwise_and)
        e.tensor_scalar(out=v8, in0=m8, scalar1=2, scalar2=-4, op0=ALU.mult, op1=ALU.add)
        e.tensor_tensor(out=v8, in0=m8, in1=v8, op=ALU.max)
        e.tensor_scalar(out=m8, in0=m8, scalar1=4, scalar2=-16, op0=ALU.mult, op1=ALU.add)
        e.tensor_tensor(out=v8, in0=v8, in1=m8, op=ALU.max)
        e.tensor_scalar(out=f1, in0=n8, scalar1=-0.125, scalar2=0.5, op0=ALU.mult, op1=ALU.add)
        e.tensor_tensor(out=f2, in0=v8, in1=f1, op=ALU.mult)
        return e.tensor_tensor(out=dst3, in0=f2, in1=scl3, op=ALU.mult)

    def r3(ap, c):
        return ap.rearrange("p (u c) -> p u c", c=c)

    for l in range(GU_L):
        P.wait("gpsimd", dv, t_ghi[l])
        t_gsp[l] = P.dma("gpsimd", qGSP, d_aggu_i[:, l * 2048:(l + 1) * 2048], gdec[l % 4])
    P.wait("gpsimd", dv, t_ghi[GU_L - 1])
    t_dnb = P.dma("gpsimd", qDNB, dnbb, bass.AP(d_pkdn, 0, [[DN_COLS, 128], [128 * DN_COLS, DN_U], [1, DN_COLS]]))
    t_dns = P.dma("gpsimd", qDNB, dnsr2, bass.AP(d_scdn, 0, [[DN_COLS, 128], [128 * DN_COLS, DN_U], [1, DN_COLS]]))
    NG = DN_U // 2
    t_dgrp = [0] * NG
    t_dsp = [0] * NG
    P.wait("vector", qG, t_gsp[8])
    P.wait("vector", qG, t_gsp[9])
    P.wait("vector", qG, t_dnb)
    P.wait("vector", qG, t_dns)
    for g_ in range(NG):
        u0 = 2 * g_
        src = r3(dnbb[:, u0 * DN_COLS:(u0 + 2) * DN_COLS].bitcast(I8), DN_COLS)
        scl3 = r3(dnsr2[:, u0 * DN_COLS:(u0 + 2) * DN_COLS], DN_COLS)
        stg = dnstg[g_ % 2][:, 0:1024]
        s4 = stg.rearrange("p (u two c) -> p u two c", two=2, c=DN_COLS)
        dlo = s4[:, :, 0:1, :].squeeze(2)
        dhi = s4[:, :, 1:2, :].squeeze(2)
        if g_ >= 2:
            P.wait("vector", qG, t_dsp[g_ - 2])
        si = [r3(dvi[i][:, 0:512], DN_COLS) for i in range(3)]
        sf = [r3(dvf[i][:, 0:512], DN_COLS) for i in range(2)]
        P.run("vector", lambda v, src=src, scl3=scl3, dlo=dlo, si=si, sf=sf: dec_ops(v, src, si[0], si[1], si[2], sf[0], sf[1], scl3, dlo, False), dv)
        P.run("vector", lambda v, src=src, scl3=scl3, dhi=dhi, si=si, sf=sf: dec_ops(v, src, si[0], si[1], si[2], sf[0], sf[1], scl3, dhi, True), dv)
        t_dgrp[g_] = dv.n
        P.wait("gpsimd", dv, t_dgrp[g_])
        t_dsp[g_] = P.dma("gpsimd", qDSP, d_agdn_i[:, g_ * 1024:(g_ + 1) * 1024], stg)
    t_dec_dv = max(t_ghi[-1], t_dgrp[-1])

    P.wait("gpsimd", dv, t_dec_dv)
    P.wait("gpsimd", qG, t_dsp[NG - 1])
    P.wait("gpsimd", qG, t_dsp[NG - 2])
    t_wk = 0
    for hc in range(NHC):
        t_wk = P.dma("gpsimd", qWK, wk_sb[hc], d_wk[hc * 128:(hc + 1) * 128, :])
    P.waitl("gpsimd", dv, t_ln1tm, NTC - 1)
    for hc in range(NHC):
        P.dma("gpsimd", qWG1, wg1[:, hc * 128:(hc + 1) * 128], d_wg1[hc * 128:(hc + 1) * 128, :])
    t_wg = tokfull(qWG1)
    h_ln1free = [0, 0]
    P.waitl("gpsimd", qS, h_ln1free, 0)
    P.waitl("gpsimd", pe, h_ln1free, 1)
    t_wv = 0
    for hc in range(NHC):
        t_wv = P.dma("gpsimd", qWV, wv_sb[hc], d_wv[hc * 128:(hc + 1) * 128, :])

    for l in range(GU_L - 4, GU_L):
        P.wait("gpsimd", qG, t_gsp[l])
    P.run("gpsimd", lambda g: g.collective_compute(
        "AllGather", ALU.bypass, replica_groups=[list(range(CORES))],
        ins=[d_aggu_i.ap().opt()], outs=[d_aggu_o.ap().opt()]), gp)
    t_aggu = gp.n
    P.wait("gpsimd", qG, t_dsp[NG - 1])
    P.wait("gpsimd", qG, t_dsp[NG - 2])
    P.run("gpsimd", lambda g: g.collective_compute(
        "AllGather", ALU.bypass, replica_groups=[list(range(CORES))],
        ins=[d_agdn_i.ap().opt()], outs=[d_agdn_o.ap().opt()]), gp)
    t_agdn = gp.n

    t_qev = []
    t_qspill = []
    t_wqf = [0] * NHC
    t_wqc = [0] * NHC

    def load_wqf(ec):
        if ec >= 2:
            P.wait("sync", ac, t_wqc[ec - 2])
        else:
            P.wait("sync", dv, t_ln1tm[NTC - 1])
            P.wait("sync", pe, t_tr_pe[NTC - 1])
        t_wqf[ec] = P.dma("sync", qWQ, wqf32[ec % 2][:, :], bass.AP(
            d_wq, ec * 128, [[H, 128], [128 * H, NHC], [1, 128]]))

    def cast_wq(ec):
        P.wait("scalar", qS, t_wqf[ec])
        if ec >= 2:
            P.wait("scalar", pe, t_qgrp_pe[ec - 2])
        if ec == 0:
            P.waitl("scalar", dv, t_ln1tm, NTC - 1)
        P.run("scalar", lambda s, ec=ec: s.activation(out=wqcolw[ec % 2][:, :], in_=wqf32[ec % 2][:, :], func=AF.Copy), ac)
        t_wqc[ec] = ac.n

    load_wqf(0)
    load_wqf(1)
    cast_wq(0)
    cast_wq(1)
    for ec in range(NHC):
        if ec + 2 < NHC:
            load_wqf(ec + 2)
        P.wait("tensor", ac, t_wqc[ec])
        if ec == 0:
            P.wait("tensor", dv, t_ln1_all)
        for es4 in range(4):
            bank = (ec * 4 + es4) % 2
            if len(t_qev) >= 2:
                P.wait("tensor", ac, t_qev[-2])
            for hc in range(NHC):
                P.run("tensor", lambda t, ec=ec, hc=hc, es4=es4, bank=bank: t.matmul(
                    psb[bank][:, :], wqcolw[ec % 2][:, hc * 128:(hc + 1) * 128], ln1f[hc][:, es4 * 512:(es4 + 1) * 512],
                    start=(hc == 0), stop=(hc == NHC - 1)), pe if hc == NHC - 1 else None)
            t_mm = pe.n
            qo = qout[(ec * 4 + es4) % 4]
            P.wait("scalar", pe, t_mm)
            if len(t_qspill) >= 4:
                P.wait("scalar", qS, t_qspill[-4])
            P.run("scalar", lambda s, qo=qo, bank=bank: s.activation(out=qo[:, :], in_=psb[bank][:, :], func=AF.Silu), ac)
            t_qev.append(ac.n)
            P.wait("sync", ac, t_qev[-1])
            t_qspill.append(P.dma("sync", qQSP, bass.AP(
                d_q, (es4 * NHC + ec) * 128 * 512, [[512, 128], [1, 512]]), qo[:, :]))
        t_qgrp_pe[ec] = pe.n
        if ec + 2 < NHC:
            cast_wq(ec + 2)
    t_midev = [0] * 4
    for es4 in range(4):
        P.wait("tensor", qG, t_wg)
        if es4 >= 1:
            P.wait("tensor", ac, t_midev[es4 - 1])
        for hc in range(NHC):
            P.run("tensor", lambda t, hc=hc, es4=es4: t.matmul(
                psb[2][:, :], wg1[:, hc * 128:(hc + 1) * 128], ln1f[hc][:, es4 * 512:(es4 + 1) * 512],
                start=(hc == 0), stop=(hc == NHC - 1)), pe if hc == NHC - 1 else None)
        t_mm = pe.n
        P.wait("scalar", pe, t_mm)
        P.run("scalar", lambda s, es4=es4: s.activation(out=midb[:, es4 * 512:(es4 + 1) * 512], in_=psb[2][:, :], func=AF.Copy), ac)
        t_midev[es4] = ac.n
    t_mid = ac.n
    t_mid_pe = pe.n
    t_l1sp = [0] * NHC
    for hc in range(NHC):
        P.wait("sync", pe, t_mid_pe)
        t_l1sp[hc] = P.dma("sync", qLNF, bass.AP(
            d_ln1g, hc * 128 * 128, [[128, 128], [NHC * 128 * 128, NTC], [1, 128]]),
            ln1f[hc][:, :].rearrange("p (t c) -> p t c", c=128))
    t_ln1_spill = tokfull(qLNF)
    h_ln1free[0] = t_ln1_spill
    h_ln1free[1] = t_mid_pe

    d_kdbg = dbgout("ktm", [NTC, 128, H], BF16)
    d_vdbg = dbgout("vtm", [NTC, 128, H], BF16)
    d_odbg = dbgout("ointra", [NTC, 128, H], BF16)
    P.wait("vector", pe, t_mid_pe)
    P.wait("vector", qS, t_ln1_spill)
    P.run("vector", lambda v: v.memset(tstate, 0.0), dv)
    t_tzero = dv.n
    P.wait("vector", ac, t_decay)
    ktcr = [ktc, ktcB]
    vtcr = [vtc, vtcB]
    t_qtc = [0] * NTC
    t_l1tc = [0] * NTC
    t_qlastuse = [0] * NTC
    t_scan_dv = [0] * NTC
    t_scan_ac = [0] * NTC
    t_kv_pe = [0] * NTC
    t_kv_ac = [0] * NTC
    t_kev = [0] * NTC
    t_vev = [0] * NTC
    t_kread_dv = [0] * NTC
    t_ospill = [None] * NTC
    t_aspill = [None] * NTC
    sc = {}

    def kv_group(tc, gi):
        isv = gi >= 4
        es4 = gi % 4
        bank = (2 if isv else 0) + es4 % 2
        wsb = wv_sb if isv else wk_sb
        dst = vtcr[tc % 2] if isv else ktcr[tc % 2]
        if gi == 0:
            if tc == 0:
                P.wait("tensor", ac, ac.n)
            P.wait("tensor", qG, t_wk)
            P.wait("tensor", qS, t_l1tc[tc])
        if gi == 4:
            P.wait("tensor", qG, t_wv)
        if es4 >= 2:
            P.wait("tensor", ac, sc[("kvev", tc, gi - 2)])
        elif gi == 0 and tc >= 1:
            P.wait("tensor", ac, t_kv_ac[tc - 1])
        for hc in range(NHC):
            P.run("tensor", lambda t, hc=hc, es4=es4, wsb=wsb: t.matmul(
                psb[(2 if isv else 0) + es4 % 2][:, :] if False else psb[bank][:, :],
                ln1tc[:, hc * 128:(hc + 1) * 128], wsb[hc][:, es4 * 512:(es4 + 1) * 512],
                start=(hc == 0), stop=(hc == NHC - 1)), pe if hc == NHC - 1 else None)
        t_mm = pe.n
        sc[("kvmm", tc, gi)] = t_mm

    def kv_evac(tc, gi):
        isv = gi >= 4
        es4 = gi % 4
        bank = (2 if isv else 0) + es4 % 2
        dst = vtcr[tc % 2] if isv else ktcr[tc % 2]
        P.wait("scalar", pe, sc[("kvmm", tc, gi)])
        if tc >= 2:
            P.wait("scalar", dv, t_kread_dv[tc - 2] if not isv else t_scan_dv[tc - 2])
            P.wait("scalar", pe, t_qlastuse[tc - 2])
        P.run("scalar", lambda s, dst=dst, es4=es4, bank=bank: s.activation(
            out=dst[:, es4 * 512:(es4 + 1) * 512], in_=psb[bank][:, :], func=AF.Silu), ac)
        sc[("kvev", tc, gi)] = ac.n
        if gi == 3:
            t_kev[tc] = ac.n
        if gi == 7:
            t_vev[tc] = ac.n
            t_kv_ac[tc] = ac.n

    def scan_dv_batch(tc, which, hb):
        hs = hb * 8
        kt = ktcr[tc % 2]
        vt = vtcr[tc % 2]
        if which == 0:
            P.wait("vector", ac, t_kev[tc])
            if tc == 0 and hb == 0:
                P.wait("vector", dv, t_tzero)
                P.wait("vector", dv, t_dmask)
            for j in range(8):
                h = hs + j
                P.run("vector", lambda v, j=j, h=h, kt=kt: v.tensor_scalar_mul(
                    out=kdk8[:, j * 128:(j + 1) * 128], in0=kt[:, h * 128:(h + 1) * 128], scalar1=kdecs[:, h:h + 1]),
                    dv if j == 7 else None)
            sc[("D0", hb)] = dv.n
            if hb == 1:
                t_kread_dv[tc] = dv.n
        elif which == 1:
            P.wait("vector", pe, sc[("SA", hb)])
            if t_aspill[tc - 1] is not None and hb == 0:
                P.wait("vector", qS, t_aspill[tc - 1])
            for j in range(8):
                h = hs + j
                bk = 4 + j // 4
                off = (j % 4) * 128
                P.run("vector", lambda v, h=h, bk=bk, off=off: v.tensor_copy(
                    out=atc[:, h * 128:(h + 1) * 128], in_=psb[bk][:, off:off + 128]), dv)
                P.run("vector", lambda v, h=h: v.scalar_tensor_tensor(
                    out=tstate[:, h * 128:(h + 1) * 128], in0=tstate[:, h * 128:(h + 1) * 128], scalar=cds[:, h:h + 1],
                    in1=atc[:, h * 128:(h + 1) * 128], op0=ALU.mult, op1=ALU.add), dv if j == 7 else None)
            sc[("D1", hb)] = dv.n
        elif which == 2:
            P.wait("vector", pe, sc[("SS", hb)])
            for j in range(8):
                h = hs + j
                bk = 4 + j // 4
                off = (j % 4) * 128
                P.run("vector", lambda v, j=j, h=h, bk=bk, off=off: v.tensor_mul(
                    out=sdb8[:, j * 128:(j + 1) * 128], in0=psb[bk][:, off:off + 128], in1=dmsk[:, h * 128:(h + 1) * 128]),
                    dv if j == 7 else None)
            sc[("D2", hb)] = dv.n
        elif which == 3:
            P.wait("vector", pe, sc[("SO", hb)])
            if t_ospill[tc - 1] is not None and hb == 0:
                P.wait("vector", qS, t_ospill[tc - 1])
            for j in range(8):
                h = hs + j
                bk = 4 + j // 4
                off = (j % 4) * 128
                P.run("vector", lambda v, h=h, bk=bk, off=off: v.tensor_copy(
                    out=otc[:, h * 128:(h + 1) * 128], in_=psb[bk][:, off:off + 128]), dv)
            sc[("D3", hb)] = dv.n
            if hb == 1:
                t_scan_dv[tc] = dv.n

    def scan_ac_batch(tc, which, hb):
        if which == 0:
            P.wait("scalar", pe, sc[("ST", hb)])
            for j in range(8):
                P.run("scalar", lambda s, j=j: s.activation(
                    out=kfm8[:, j * 128:(j + 1) * 128], in_=bf(psb[6][:, j * 64:(j + 1) * 64]), func=AF.Copy),
                    ac if j == 7 else None)
            sc[("A0", hb)] = ac.n
        else:
            P.wait("scalar", pe, sc[("SD", hb)])
            for j in range(8):
                P.run("scalar", lambda s, j=j: s.activation(
                    out=sdt8[:, j * 128:(j + 1) * 128], in_=bf(psb[7][:, j * 64:(j + 1) * 64]), func=AF.Copy),
                    ac if j == 7 else None)
            sc[("A1", hb)] = ac.n
            if hb == 1:
                t_scan_ac[tc] = ac.n

    def scan_pe_sub(tc, si):
        hb, st = si // 5, si % 5
        hs = hb * 8
        kt = ktcr[tc % 2]
        vt = vtcr[tc % 2]
        if st == 0:
            scan_dv_batch(tc, 0, hb)
            P.wait("tensor", dv, sc[("D0", hb)])
            if hb == 0:
                if tc >= 1:
                    P.wait("tensor", dv, t_scan_dv[tc - 1])
            else:
                P.wait("tensor", dv, sc[("D3", 0)])
            P.wait("tensor", ac, t_vev[tc])
            for j in range(8):
                h = hs + j
                bk = 4 + j // 4
                off = (j % 4) * 128
                P.run("tensor", lambda t, j=j, h=h, bk=bk, off=off, vt=vt: t.matmul(
                    psb[bk][:, off:off + 128], kdk8[:, j * 128:(j + 1) * 128], vt[:, h * 128:(h + 1) * 128],
                    start=True, stop=True), pe if j == 7 else None)
            sc[("SA", hb)] = pe.n
            scan_dv_batch(tc, 1, hb)
        elif st == 1:
            if hb == 1:
                P.wait("tensor", ac, sc[("A0", 0)])
            elif tc >= 1:
                P.wait("tensor", ac, sc[("A0p", 1)] if ("A0p", 1) in sc else 0)
            for j in range(8):
                h = hs + j
                P.run("tensor", lambda t, j=j, h=h, kt=kt: t.transpose(
                    bf(psb[6][:, j * 64:(j + 1) * 64]), kt[:, h * 128:(h + 1) * 128], ident[:, :]), pe if j == 7 else None)
            sc[("ST", hb)] = pe.n
            scan_ac_batch(tc, 0, hb)
        elif st == 2:
            P.wait("tensor", dv, sc[("D1", hb)])
            P.wait("tensor", ac, sc[("A0", hb)])
            P.wait("tensor", qS, t_qtc[tc])
            for j in range(8):
                h = hs + j
                bk = 4 + j // 4
                off = (j % 4) * 128
                P.run("tensor", lambda t, j=j, h=h, bk=bk, off=off, tc=tc: t.matmul(
                    psb[bk][:, off:off + 128], qtc[tc % 2][:, h * 128:(h + 1) * 128], kfm8[:, j * 128:(j + 1) * 128],
                    start=True, stop=True), pe if j == 7 else None)
            sc[("SS", hb)] = pe.n
            scan_dv_batch(tc, 2, hb)
        elif st == 3:
            P.wait("tensor", dv, sc[("D2", hb)])
            if hb == 1:
                P.wait("tensor", ac, sc[("A1", 0)])
            for j in range(8):
                P.run("tensor", lambda t, j=j: t.transpose(
                    bf(psb[7][:, j * 64:(j + 1) * 64]), sdb8[:, j * 128:(j + 1) * 128], ident[:, :]), pe if j == 7 else None)
            sc[("SD", hb)] = pe.n
            scan_ac_batch(tc, 1, hb)
        else:
            P.wait("tensor", ac, sc[("A1", hb)])
            for j in range(8):
                h = hs + j
                bk = 4 + j // 4
                off = (j % 4) * 128
                P.run("tensor", lambda t, j=j, h=h, bk=bk, off=off, vt=vt: t.matmul(
                    psb[bk][:, off:off + 128], sdt8[:, j * 128:(j + 1) * 128], vt[:, h * 128:(h + 1) * 128],
                    start=True, stop=True), pe if j == 7 else None)
            sc[("SO", hb)] = pe.n
            scan_dv_batch(tc, 3, hb)
            if hb == 1:
                t_qlastuse[tc] = pe.n
                sc[("A0p", 1)] = sc[("A0", 1)]
                P.wait("sync", dv, t_scan_dv[tc])
                t_ospill[tc] = P.dma("sync", qOSP, d_oi[tc], otc)
                t_aspill[tc] = P.dma("sync", qASP, d_a[tc], atc)
                if d_odbg is not None:
                    P.dma("sync", qS, d_odbg[tc], otc)

    for tc in range(NTC):
        if tc >= 1:
            P.wait("sync", pe, t_kv_pe[tc - 1])
        if tc == 0:
            P.wait("sync", pe, t_mid_pe)
            P.wait("sync", qS, t_ln1_spill)
            for tok in t_qspill[-4:]:
                P.wait("sync", qS, tok)
            P.wait("sync", ac, t_wqc[NHC - 1])
        t_l1tc[tc] = P.dma("sync", qL1, ln1tc[:, :].rearrange("p (h c) -> p h c", c=128), bass.AP(
            d_ln1g, tc * NHC * 128 * 128, [[128, 128], [128 * 128, NHC], [1, 128]]))
        if tc >= 2:
            P.wait("sync", pe, t_qlastuse[tc - 2])
        t_qtc[tc] = P.dma("sync", qQT, qtc[tc % 2][:, :], bass.AP(
            d_q, (tc // 4) * NHC * 128 * 512 + (tc % 4) * 128, [[512, 128], [128 * 512, NHC], [1, 128]]))
        for gi in range(8):
            kv_group(tc, gi)
            kv_evac(tc, gi)
            if tc >= 1 and gi < 8:
                si = gi
                scan_pe_sub(tc - 1, si)
        if tc >= 1:
            scan_pe_sub(tc - 1, 8)
            scan_pe_sub(tc - 1, 9)
        t_kv_pe[tc] = sc[("kvmm", tc, 7)]
        if d_kdbg is not None:
            P.wait("sync", ac, t_kev[tc])
            P.dma("sync", qS, d_kdbg[tc], ktcr[tc % 2])
        if d_vdbg is not None:
            P.wait("sync", ac, t_vev[tc])
            P.dma("sync", qS, d_vdbg[tc], vtcr[tc % 2])
    for si in range(10):
        scan_pe_sub(NTC - 1, si)

    t_wo = [0] * NHC
    P.wait("gpsimd", dv, t_scan_dv[NTC - 1])
    P.wait("gpsimd", pe, t_qlastuse[NTC - 1])
    t_wg2l = P.dma("gpsimd", qWG2, wg2[:, :], d_wg2[:, :])
    for ec in range(NHC):
        t_wo[ec] = P.dma("gpsimd", qWO, wo_sb[ec], d_wo[ec * 128:(ec + 1) * 128, :])

    t_tspill = P.dma("sync", qTSPs, d_agt_i[:, :], tstate)
    P.wait("gpsimd", qS, t_tspill)
    P.run("gpsimd", lambda g: g.collective_compute(
        "AllGather", ALU.bypass, replica_groups=[list(range(CORES))],
        ins=[d_agt_i.ap().opt()], outs=[d_agt_o.ap().opt()]), gp)
    t_agt = gp.n

    d_tdbg = dbgout("tst", [CORES * 128, H])
    if d_tdbg is not None:
        P.wait("sync", gp, t_agt)
        P.dma("sync", qS, d_tdbg[:, :], d_agt_o[:, :])
    P.wait("vector", pe, t_kv_pe[NTC - 1])
    P.run("vector", lambda v: v.memset(sinb, 0.0), dv)
    t_sin0 = dv.n
    t_peer = [0] * CORES
    t_sinr = [0] * CORES
    for r in range(CORES):
        P.wait("gpsimd", gp, t_agt)
        if r == 0:
            P.wait("gpsimd", pe, t_kv_pe[NTC - 1])
        if r >= 2:
            P.waitl("gpsimd", dv, t_sinr, r - 2)
        t_peer[r] = P.dma("gpsimd", qPEER, onbr[r % 2], d_agt_o[r * 128:(r + 1) * 128, :])
    for r in range(CORES):
        P.run("vector", lambda v, r=r: v.tensor_scalar_mul(out=coefr[:, :], in0=ld_rep[:, :], scalar1=gapr[:, r:r + 1]), dv)
        tcf = dv.n
        P.wait("scalar", dv, tcf)
        P.run("scalar", lambda s: s.activation(out=coefr[:, :], in_=coefr[:, :], func=AF.Exp, scale=2048.0), ac)
        tce = ac.n
        P.wait("vector", ac, tce)
        P.wait("vector", qG, t_peer[r])
        if r == 0:
            P.wait("vector", dv, t_sin0)
        for h in range(NH):
            P.run("vector", lambda v, r=r, h=h: v.scalar_tensor_tensor(
                out=sinb[:, h * 128:(h + 1) * 128], in0=onbr[r % 2][:, h * 128:(h + 1) * 128], scalar=coefr[:, h:h + 1],
                in1=sinb[:, h * 128:(h + 1) * 128], op0=ALU.mult, op1=ALU.add), dv if h == NH - 1 else None)
        t_sinr[r] = dv.n
    t_sin = dv.n
    P.wait("scalar", dv, t_sin)
    P.run("scalar", lambda s: s.activation(out=sbfb, in_=sinb, func=AF.Copy), ac)
    t_sbf = ac.n

    t_wofold = [0] * NHC
    for ec in range(NHC):
        P.wait("vector", qG, t_wo[NHC - 1])
        P.wait("vector", qS, t_nw)
        P.run("vector", lambda v, ec=ec: v.tensor_scalar_mul(out=wo_sb[ec][:, :], in0=wo_sb[ec][:, :], scalar1=nwcol[:, ec:ec + 1]), dv)
        t_wofold[ec] = dv.n

    d_y1dbg = dbgout("y1", [T, H])
    d_aodbg = dbgout("attnout", [T, H])
    t_ln2s = [0] * NTC
    t_y1sp = [0] * NTC
    t_sbf_tc = t_sbf
    t_supd = [0] * NTC
    t_oint = [0] * NTC
    t_gate = [0] * NTC
    t_trs_pe = [0] * NTC
    t_on = [0] * NTC
    t_gp_res = [0] * NTC
    onb3 = [onbr[0], onbr[1], peerb]
    t_msqL = [0] * NTC
    t_mg = {}
    t_gse = {}
    t_ont = {}
    t_aoev = {}
    t_bdv = [0] * NTC
    t_bac = [0] * NTC
    t_q2 = [0] * NTC
    t_oi2 = [0] * NTC
    t_a2 = [0] * NTC
    t_x2 = [0] * NTC

    def p2_loads(tc):
        if tc >= 2:
            P.wait("gpsimd", pe, t_oint[tc - 2])
        t_q2[tc] = P.dma("gpsimd", qQT, qtc[tc % 2][:, :], bass.AP(
            d_q, (tc // 4) * NHC * 128 * 512 + (tc % 4) * 128, [[512, 128], [128 * 512, NHC], [1, 128]]))
        if tc >= 2:
            P.wait("sync", dv, t_mg[(tc - 2, 3)])
        t_oi2[tc] = P.dma("sync", qOI, oitcr[tc % 2], d_oi[tc])
        if tc >= 2:
            P.wait("sync", dv, t_supd[tc - 2])
        t_a2[tc] = P.dma("sync", qA2, a2tcr[tc % 2], d_a[tc])

    def p2_xload(tc):
        if tc >= 2:
            P.wait("gpsimd", gp, t_gp_res[tc - 2])
        t_x2[tc] = P.dma("gpsimd", qX2, xtc2r[tc % 2], d_x[tc * 128:(tc + 1) * 128, :])

    def emit_A(tc):
        nonlocal t_sbf_tc
        sbf_prev = t_sbf_tc
        P.wait("vector", ac, sbf_prev)
        P.waitt("vector", t_a2[tc])
        for h in range(NH):
            P.run("vector", lambda v, tc=tc, h=h: v.scalar_tensor_tensor(
                out=sinb[:, h * 128:(h + 1) * 128], in0=sinb[:, h * 128:(h + 1) * 128], scalar=cds[:, h:h + 1],
                in1=a2tcr[tc % 2][:, h * 128:(h + 1) * 128], op0=ALU.mult, op1=ALU.add), dv if h == NH - 1 else None)
        t_supd[tc] = dv.n
        P.wait("tensor", ac, sbf_prev)
        P.waitt("tensor", t_q2[tc])
        if tc >= 1:
            P.wait("tensor", dv, t_mg[(tc - 1, 3)])
            P.wait("tensor", ac, t_gse[(tc - 1, 3)])
        for h in range(NH):
            P.run("tensor", lambda t, tc=tc, h=h: t.matmul(
                psb[h // 4][:, (h % 4) * 128:(h % 4 + 1) * 128], qtc[tc % 2][:, h * 128:(h + 1) * 128],
                sbfb[:, h * 128:(h + 1) * 128], start=True, stop=True), pe if h == NH - 1 else None)
        t_oint[tc] = pe.n
        P.wait("vector", pe, t_oint[tc])
        P.waitt("vector", t_oi2[tc])
        for h in range(NH):
            P.run("vector", lambda v, tc=tc, h=h: v.scalar_tensor_tensor(
                out=otc[:, h * 128:(h + 1) * 128], in0=psb[h // 4][:, (h % 4) * 128:(h % 4 + 1) * 128],
                scalar=qdecs[:, h:h + 1], in1=oitcr[tc % 2][:, h * 128:(h + 1) * 128],
                op0=ALU.mult, op1=ALU.add), dv if h == NH - 1 else None)
        t_mg[(tc, 3)] = dv.n
        P.wait("scalar", dv, t_supd[tc])
        P.wait("scalar", pe, t_oint[tc])
        P.run("scalar", lambda s: s.activation(out=sbfb, in_=sinb, func=AF.Copy), ac)
        t_sbf_tc = ac.n
        for pr in range(2):
            if pr == 0:
                P.wait("tensor", dv, t_mg[(tc, 3)])
                if tc == 0:
                    P.wait("tensor", qG, t_wg2l)
            else:
                P.wait("tensor", ac, t_gse[(tc, 1)])
            for es4 in (2 * pr, 2 * pr + 1):
                P.run("tensor", lambda t, tc=tc, es4=es4: t.matmul(
                    psb[2 + es4 % 2][:, :], midb[:, tc * 128:(tc + 1) * 128], wg2[:, es4 * 512:(es4 + 1) * 512],
                    start=True, stop=True), pe if es4 % 2 == 1 else None)
            tp = pe.n
            P.wait("scalar", pe, tp)
            for es4 in (2 * pr, 2 * pr + 1):
                P.run("scalar", lambda s, es4=es4: s.activation(
                    out=gsigb[:, es4 * 512:(es4 + 1) * 512], in_=psb[2 + es4 % 2][:, :], func=AF.Sigmoid), ac)
                t_gse[(tc, es4)] = ac.n
        t_gate[tc] = pe.n
        P.wait("vector", ac, t_gse[(tc, 3)])
        P.run("vector", lambda v: v.tensor_mul(out=otc[:, :], in0=otc[:, :], in1=gsigb[:, :]))
        if tc >= 3:
            P.wait("vector", pe, t_trs_pe[tc - 3])
        P.run("vector", lambda v, tc=tc: v.tensor_mul(out=onb3[tc % 3][:, :], in0=otc[:, :], in1=otc[:, :]))
        P.run("vector", lambda v, tc=tc: v.tensor_reduce(
            out=stat[:, 0:16], in_=onb3[tc % 3][:, :].rearrange("p (g c) -> p g c", c=128), axis=AXX, op=ALU.add), dv)
        t_msqL[tc] = dv.n

    def emit_A2(tc):
        P.wait("scalar", dv, t_msqL[tc])
        P.run("scalar", lambda s: s.activation(out=stat[:, 0:16], in_=stat[:, 0:16], func=AF.Ln, scale=1.0 / 128, bias=cepsA[:, 0:1]))
        P.run("scalar", lambda s: s.activation(out=stat[:, 0:16], in_=stat[:, 0:16], func=AF.Exp, scale=-0.5), ac)
        t_grs = ac.n
        P.wait("vector", ac, t_grs)
        for h in range(NH):
            P.run("vector", lambda v, tc=tc, h=h: v.tensor_scalar_mul(
                out=onb3[tc % 3][:, h * 128:(h + 1) * 128], in0=otc[:, h * 128:(h + 1) * 128],
                scalar1=stat[:, h:h + 1]), dv if h == NH - 1 else None)
        t_on[tc] = dv.n

    def emit_B(tc):
        aof = aofr[tc % 2]
        bln2 = bln2r[tc % 2]
        bstat = bstatr[tc % 2]
        P.wait("tensor", dv, t_on[tc])
        for hf in range(2):
            bank = 6 + hf
            if tc >= 1:
                P.wait("tensor", ac, t_ont[(tc - 1, hf)])
            for k in range(8):
                ec = hf * 8 + k
                P.run("tensor", lambda t, tc=tc, ec=ec, k=k, bank=bank: t.transpose(
                    bf(psb[bank][:, k * 64:(k + 1) * 64]), onb3[tc % 3][:, ec * 128:(ec + 1) * 128], ident[:, :]),
                    pe if k == 7 else None)
            tp = pe.n
            P.wait("scalar", pe, tp)
            P.run("scalar", lambda s, hf=hf, bank=bank: s.activation(
                out=ontbb[:, hf * 1024:(hf + 1) * 1024], in_=bf(psb[bank][:, 0:512]), func=AF.Copy), ac)
            t_ont[(tc, hf)] = ac.n
        t_trs_pe[tc] = pe.n
        P.wait("tensor", ac, t_ont[(tc, 1)])
        if tc == 0:
            P.wait("tensor", dv, t_scan_dv[NTC - 1])
            P.wait("tensor", dv, t_wofold[NHC - 1])
        bsc = bln2.bitcast(F32)
        for pr in range(2):
            if pr == 0 and tc >= 1:
                P.wait("tensor", ac, t_aoev[(tc - 1, 1)])
            elif pr == 1:
                P.wait("tensor", ac, t_aoev[(tc, 0)])
            for es4 in (2 * pr, 2 * pr + 1):
                for ec in range(NHC):
                    P.run("tensor", lambda t, ec=ec, es4=es4: t.matmul(
                        psb[4 + es4 % 2][:, :], ontbb[:, ec * 128:(ec + 1) * 128], wo_sb[ec][:, es4 * 512:(es4 + 1) * 512],
                        start=(ec == 0), stop=(ec == NHC - 1)), pe if (ec == NHC - 1 and es4 % 2 == 1) else None)
            t_mm = pe.n
            P.wait("scalar", pe, t_mm)
            if pr == 0 and tc >= 2:
                P.wait("scalar", dv, t_bdv[tc - 2])
                P.waitt("scalar", t_y1sp[tc - 2])
                P.waitt("scalar", t_ln2s[tc - 2])
            for es4 in (2 * pr, 2 * pr + 1):
                P.run("scalar", lambda s, es4=es4: s.activation(
                    out=aof[:, es4 * 512:(es4 + 1) * 512], in_=psb[4 + es4 % 2][:, :], func=AF.Copy, accum_out=bstat[:, es4:es4 + 1]))
                P.run("scalar", lambda s, es4=es4, bsc=bsc: s.activation(
                    out=bsc[:, 0:512], in_=psb[4 + es4 % 2][:, :], func=AF.Square, accum_out=bstat[:, 4 + es4:5 + es4]), ac)
            t_aoev[(tc, pr)] = ac.n
        if d_aodbg is not None:
            P.wait("sync", ac, t_aoev[(tc, 1)])
            tdbg = P.dma("sync", qS, d_aodbg[tc * 128:(tc + 1) * 128, :], aof)
            P.wait("scalar", qS, tdbg)
        P.run("scalar", lambda s, bsc=bsc: s.activation(out=bsc[:, 512:516], in_=bstat[:, 0:4], func=AF.Copy, scale=1.0 / H, accum_out=bstat[:, 10:11]))
        P.run("scalar", lambda s, bsc=bsc: s.activation(out=bsc[:, 516:520], in_=bstat[:, 4:8], func=AF.Copy, scale=1.0 / H, accum_out=bstat[:, 11:12]))
        P.run("scalar", lambda s: s.activation(out=bstat[:, 12:13], in_=bstat[:, 10:11], func=AF.Square), ac)
        t_mv = ac.n
        P.wait("vector", ac, t_mv)
        P.run("vector", lambda v: v.tensor_copy(out=dumf[:, 0:1], in_=dumf[:, 1:2]))
        P.run("vector", lambda v: v.tensor_scalar(out=bstat[:, 13:14], in0=bstat[:, 11:12], scalar1=LNEPS, scalar2=bstat[:, 12:13], op0=ALU.add, op1=ALU.subtract), dv)
        t_var = dv.n
        P.wait("scalar", dv, t_var)
        P.run("scalar", lambda s: s.activation(out=bstat[:, 14:15], in_=bstat[:, 13:14], func=AF.Ln))
        P.run("scalar", lambda s: s.activation(out=bstat[:, 15:16], in_=bstat[:, 14:15], func=AF.Exp, scale=-0.5), ac)
        t_r2 = ac.n
        P.wait("vector", ac, t_r2)
        P.run("vector", lambda v: v.tensor_scalar(out=bstat[:, 8:9], in0=bstat[:, 10:11], scalar1=bstat[:, 15:16], scalar2=-1.0, op0=ALU.mult, op1=ALU.mult))
        P.run("vector", lambda v: v.tensor_copy(out=dumf[:, 0:1], in_=dumf[:, 1:2]))
        P.run("vector", lambda v: v.tensor_scalar(out=aof[:, :], in0=aof[:, :], scalar1=bstat[:, 15:16], scalar2=bstat[:, 8:9], op0=ALU.mult, op1=ALU.add), dv)
        t_aff = dv.n
        P.wait("gpsimd", dv, t_aff)
        P.waitt("gpsimd", t_x2[tc])
        P.run("gpsimd", lambda g, tc=tc: g.tensor_add(out=aof, in0=aof, in1=xtc2r[tc % 2]), gp)
        t_gp_res[tc] = gp.n
        P.wait("scalar", gp, t_gp_res[tc])
        t_y1sp[tc] = P.dma("scalar", qY1S, d_y1[tc * 128:(tc + 1) * 128, :], aof)
        if d_y1dbg is not None:
            P.dma("sync", qS, d_y1dbg[tc * 128:(tc + 1) * 128, :], aof)
        P.wait("scalar", gp, t_gp_res[tc])
        P.run("scalar", lambda s: s.activation(out=ontbb, in_=aof, func=AF.Copy, scale=1.0 / H, accum_out=bstat[:, 0:1]))
        P.run("scalar", lambda s: s.activation(out=ontbb, in_=aof, func=AF.Square, scale=SQH, accum_out=bstat[:, 1:2]))
        P.run("scalar", lambda s: s.activation(out=bstat[:, 2:3], in_=bstat[:, 0:1], func=AF.Square), ac)
        t_s2 = ac.n
        P.wait("vector", ac, t_s2)
        P.run("vector", lambda v: v.tensor_copy(out=dumf[:, 0:1], in_=dumf[:, 1:2]))
        P.run("vector", lambda v: v.tensor_scalar(out=bstat[:, 3:4], in0=bstat[:, 1:2], scalar1=LNEPS, scalar2=bstat[:, 2:3], op0=ALU.add, op1=ALU.subtract), dv)
        t_v2 = dv.n
        P.wait("scalar", dv, t_v2)
        P.run("scalar", lambda s: s.activation(out=bstat[:, 5:6], in_=bstat[:, 3:4], func=AF.Ln))
        P.run("scalar", lambda s: s.activation(out=bstat[:, 6:7], in_=bstat[:, 5:6], func=AF.Exp, scale=-0.5), ac)
        t_r3 = ac.n
        P.wait("vector", ac, t_r3)
        P.run("vector", lambda v: v.tensor_scalar(out=bstat[:, 7:8], in0=bstat[:, 0:1], scalar1=bstat[:, 6:7], scalar2=-1.0, op0=ALU.mult, op1=ALU.mult))
        P.run("vector", lambda v: v.tensor_copy(out=dumf[:, 0:1], in_=dumf[:, 1:2]))
        P.run("vector", lambda v: v.tensor_scalar(out=bln2, in0=aof, scalar1=bstat[:, 6:7], scalar2=bstat[:, 7:8], op0=ALU.mult, op1=ALU.add), dv)
        t_bdv[tc] = dv.n
        t_bac[tc] = ac.n
        P.wait("sync", dv, t_bdv[tc])
        t_ln2s[tc] = P.dma("sync", qL2Sa if tc < 8 else qL2Sb, d_ln2[tc * 128:(tc + 1) * 128, :], bln2)
        if tc + 2 < NTC:
            p2_xload(tc + 2)

    P.wait("sync", dv, t_scan_dv[NTC - 1])
    P.wait("sync", pe, t_qlastuse[NTC - 1])
    P.wait("sync", qS, tokfull(qOSP))
    P.wait("sync", qS, tokfull(qASP))
    P.wait("sync", qS, t_tspill)
    P.wait("gpsimd", qS, tokfull(qOSP))
    p2_loads(0)
    p2_loads(1)
    p2_xload(0)
    p2_xload(1)
    for tc in range(NTC):
        emit_A(tc)
        if tc + 1 < NTC:
            p2_loads(tc + 1)
        if tc >= 2:
            emit_B(tc - 2)
        emit_A2(tc)
    emit_B(NTC - 2)
    emit_B(NTC - 1)

    t_half_dv = 0
    t_yev_prev_half = 0
    t_out_dv_half = 0
    t_out_fin = 0
    t_gu_last_half = 0
    for half in range(2):
        toff = half * 1024
        t_l2 = [0] * NHC
        P.wait("sync", qS, t_ln2s[half * 8 + 7])
        if half == 0:
            P.wait("sync", dv, t_bdv[NTC - 1])
            P.wait("sync", ac, t_bac[NTC - 1])
        else:
            P.wait("sync", pe, t_gu_last_half)
        for hc in range(NHC):
            t_l2[hc] = P.dma("sync", qL2F, ln2f[hc], d_ln2[toff:toff + 1024, hc * 128:(hc + 1) * 128], transpose=True)
        P.wait("gpsimd", gp, t_aggu)
        t_wgu = {}
        t_gu_pe = {}
        t_ha = {}

        def load_gu(ch):
            r, l = ch // GU_L, ch % GU_L
            if ch >= 2:
                P.wait("gpsimd", pe, t_gu_pe[ch - 2])
            elif half == 0:
                P.wait("gpsimd", dv, t_bdv[NTC - 1])
                P.wait("gpsimd", ac, t_bac[NTC - 1])
            else:
                P.wait("gpsimd", pe, t_gu_last_half)
            t_wgu[ch] = P.dma("gpsimd", qWGU, wgub[ch % 2][:, :], bass.AP(
                d_aggu_o, (r * 128) * (GU_L * 2048) + l * 2048, [[GU_L * 2048, 128], [1, 2048]]))

        load_gu(0)
        load_gu(1)
        first = True
        for ic in range(NIC):
            for j in range(2):
                ch = 2 * ic + j
                quad = (ic % 2) * 4 + j * 2
                P.wait("tensor", qG, t_wgu[ch])
                if first:
                    if half == 0:
                        P.wait("tensor", dv, t_bdv[NTC - 1])
                        P.wait("tensor", ac, t_bac[NTC - 1])
                    else:
                        P.wait("tensor", ac, t_yev_prev_half)
                    for hc in range(NHC):
                        P.wait("tensor", qS, t_l2[hc])
                    first = False
                if ic >= 2 and j == 0:
                    P.wait("tensor", dv, t_ha.get(ic - 2, 0))
                for es2 in range(2):
                    bank = quad + es2
                    for hc in range(NHC):
                        P.run("tensor", lambda t, ch=ch, hc=hc, es2=es2, bank=bank: t.matmul(
                            psb[bank][:, :], wgub[ch % 2][:, hc * 128:(hc + 1) * 128], ln2f[hc][:, es2 * 512:(es2 + 1) * 512],
                            start=(hc == 0), stop=(hc == NHC - 1)), pe if hc == NHC - 1 else None)
                t_gu_pe[ch] = pe.n
                t_gu_last_half = pe.n
                if ch + 2 < 2 * NIC:
                    load_gu(ch + 2)
            quad = (ic % 2) * 4
            P.wait("scalar", pe, t_gu_pe[2 * ic + 1])
            for es2 in range(2):
                P.run("scalar", lambda s, es2=es2, quad=quad: s.activation(out=glb[:, es2 * 512:(es2 + 1) * 512], in_=psb[quad + es2][:, :], func=AF.Silu), ac)
            t_sil = ac.n
            P.wait("vector", ac, t_sil)
            if half == 0 and ic == 0:
                P.wait("vector", gp, t_gp_res[NTC - 1])
                P.wait("vector", ac, t_bac[NTC - 1])
                P.wait("vector", qS, tokfull(qY1S))
                P.wait("vector", qS, tokfull(qL2Sa))
                P.wait("vector", qS, tokfull(qL2Sb))
            for es2 in range(2):
                P.run("vector", lambda v, ic=ic, es2=es2, quad=quad: v.tensor_mul(
                    out=hact[ic][:, es2 * 512:(es2 + 1) * 512], in0=glb[:, es2 * 512:(es2 + 1) * 512], in1=psb[quad + 2 + es2][:, :]), dv)
            t_ha[ic] = dv.n
        P.wait("gpsimd", gp, t_agdn)
        t_dn_pe = {}
        t_yev = {}

        def load_dn(idx):
            r8, icgq = idx // 4, idx % 4
            if idx >= 2:
                P.wait("gpsimd", pe, t_dn_pe[idx - 2])
            elif half == 0:
                P.wait("gpsimd", ac, t_bac[NTC - 1])
            else:
                P.wait("gpsimd", pe, t_dn_last_half)
            return P.dma("gpsimd", qWDN, wdnb[idx % 2][:, :], bass.AP(
                d_agdn_o, (r8 * 128) * (NIC * DN_COLS) + icgq * 11 * DN_COLS, [[NIC * DN_COLS, 128], [1, 11 * DN_COLS]]))

        t_wdn = {0: load_dn(0), 1: load_dn(1)}
        for r8 in range(8):
            for icgq in range(4):
                idx = r8 * 4 + icgq
                P.wait("tensor", qG, t_wdn[idx])
                if idx == 0:
                    P.wait("tensor", dv, t_ha[NIC - 1])
                    if half == 1:
                        P.wait("tensor", ac, t_yev_prev_half)
                if icgq == 0 and r8 > 0:
                    P.wait("tensor", ac, t_yev[r8 - 1])
                for ic2 in range(11):
                    ic = icgq * 11 + ic2
                    for tc8 in range(8):
                        P.run("tensor", lambda t, ic=ic, ic2=ic2, idx=idx, tc8=tc8: t.matmul(
                            psb[tc8][:, 0:256], hact[ic][:, tc8 * 128:(tc8 + 1) * 128], wdnb[idx % 2][:, ic2 * 256:(ic2 + 1) * 256],
                            start=(ic == 0), stop=(ic == NIC - 1), skip_group_check=True), pe if (ic2 == 10 and tc8 == 7) else None)
                t_dn_pe[idx] = pe.n
                if idx + 2 < 32:
                    t_wdn[idx + 2] = load_dn(idx + 2)
            t_dmm = pe.n
            P.wait("scalar", pe, t_dmm)
            if half == 1 and r8 == 0:
                P.wait("scalar", dv, t_out_dv_half)
            for tc8 in range(8):
                P.run("scalar", lambda s, tc8=tc8, r8=r8: s.activation(out=ysb[tc8][:, r8 * 256:(r8 + 1) * 256], in_=psb[tc8][:, 0:256], func=AF.Copy, accum_out=systat[:, tc8:tc8 + 1]))
                P.run("scalar", lambda s, tc8=tc8: s.activation(out=glf[:, 0:256], in_=psb[tc8][:, 0:256], func=AF.Square, accum_out=systat[:, 8 + tc8:9 + tc8]), ac)
            t_yev[r8] = ac.n
            P.wait("vector", ac, t_yev[r8])
            for tc8 in range(8):
                if r8 == 0:
                    P.run("vector", lambda v, tc8=tc8: v.tensor_copy(out=stat[:, tc8:tc8 + 1], in_=systat[:, tc8:tc8 + 1]))
                    P.run("vector", lambda v, tc8=tc8: v.tensor_copy(out=stat[:, 8 + tc8:9 + tc8], in_=systat[:, 8 + tc8:9 + tc8]))
                else:
                    P.run("vector", lambda v, tc8=tc8: v.tensor_add(out=stat[:, tc8:tc8 + 1], in0=stat[:, tc8:tc8 + 1], in1=systat[:, tc8:tc8 + 1]))
                    P.run("vector", lambda v, tc8=tc8: v.tensor_add(out=stat[:, 8 + tc8:9 + tc8], in0=stat[:, 8 + tc8:9 + tc8], in1=systat[:, 8 + tc8:9 + tc8]))
            P.run("vector", lambda v: v.engine_nop(), dv)
        t_dstat = dv.n
        t_yev_prev_half = t_yev[7]
        t_dn_last_half = t_dn_pe[31]
        d_yfdbg = dbgout("yffn", [NTC, 128, H], BF16)
        t_yfdump = [0] * 8
        if d_yfdbg is not None:
            for tc8 in range(8):
                P.wait("sync", ac, t_yev[7])
                t_yfdump[tc8] = P.dma("sync", qS, d_yfdbg[half * 8 + tc8], ysb[tc8][:, :])
        t_out_dv = 0
        for tc8 in range(8):
            tc = half * 8 + tc8
            if tc8 >= 1:
                P.wait("sync", dv, t_out_dv)
            elif half == 0:
                P.wait("sync", ac, t_bac[NTC - 1])
                P.wait("sync", qS, tokfull(qY1S))
            t_y1r = P.dma("sync", qY1R, y1rb, d_y1[tc * 128:(tc + 1) * 128, :])
            P.wait("vector", dv, t_dstat)
            P.run("vector", lambda v, tc8=tc8: v.tensor_scalar_mul(out=systat[:, 16:17], in0=stat[:, tc8:tc8 + 1], scalar1=1.0 / H))
            P.run("vector", lambda v, tc8=tc8: v.tensor_scalar_mul(out=systat[:, 17:18], in0=stat[:, 8 + tc8:9 + tc8], scalar1=1.0 / H))
            P.run("vector", lambda v: v.tensor_mul(out=systat[:, 18:19], in0=systat[:, 16:17], in1=systat[:, 16:17]))
            P.run("vector", lambda v: v.tensor_scalar(out=systat[:, 19:20], in0=systat[:, 17:18], scalar1=LNEPS, scalar2=FEPS * LNEPS, op0=ALU.mult, op1=ALU.add))
            P.run("vector", lambda v: v.tensor_sub(out=systat[:, 18:19], in0=systat[:, 17:18], in1=systat[:, 18:19]))
            P.run("vector", lambda v: v.tensor_copy(out=dumf[:, 0:1], in_=dumf[:, 1:2]))
            P.run("vector", lambda v: v.tensor_add(out=systat[:, 19:20], in0=systat[:, 18:19], in1=systat[:, 19:20]), dv)
            t_arg = dv.n
            P.wait("scalar", dv, t_arg)
            P.run("scalar", lambda s: s.activation(out=systat[:, 20:21], in_=systat[:, 19:20], func=AF.Ln))
            P.run("scalar", lambda s: s.activation(out=systat[:, 20:21], in_=systat[:, 20:21], func=AF.Exp, scale=-0.5), ac)
            t_rr = ac.n
            P.wait("vector", ac, t_rr)
            P.run("vector", lambda v: v.tensor_scalar(out=systat[:, 21:22], in0=systat[:, 16:17], scalar1=systat[:, 20:21], scalar2=-1.0, op0=ALU.mult, op1=ALU.mult))
            P.run("vector", lambda v: v.tensor_copy(out=dumf[:, 0:1], in_=dumf[:, 1:2]))
            if d_yfdbg is not None:
                P.wait("vector", qS, t_yfdump[tc8])
            P.run("vector", lambda v, tc8=tc8: v.tensor_scalar(out=ysb[tc8][:, :], in0=ysb[tc8][:, :], scalar1=systat[:, 20:21], scalar2=systat[:, 21:22], op0=ALU.mult, op1=ALU.add), dv)
            P.wait("vector", qS, t_y1r)
            P.run("vector", lambda v, tc8=tc8: v.tensor_add(out=y1rb, in0=ysb[tc8][:, :], in1=y1rb), dv)
            t_out_dv = dv.n
            P.wait("sync", dv, t_out_dv)
            t_out_fin = P.dma("sync", qS, d_out[tc * 128:(tc + 1) * 128, :], y1rb)
        t_out_dv_half = t_out_dv

    d_qdbg = dbgout("qq", [NTC * NHC * 128, 128], BF16)
    if d_qdbg is not None:
        P.wait("gpsimd", qS, qS.n)
        P.dma("gpsimd", qG, d_qdbg[:, :], d_q[:, :])
    d_ln2dbg = dbgout("ln2x", [T, H], BF16)
    if d_ln2dbg is not None:
        P.wait("gpsimd", qS, qS.n)
        P.dma("gpsimd", qG, d_ln2dbg[:, :], d_ln2[:, :])
    P.wait("gpsimd", qS, qS.n)
    P.wait("gpsimd", qG, qG.n)
    P.wait("gpsimd", qA, qA.n)

    with nc.Block() as block:
        @block.sync
        def _(e):
            for fn in P.ops["sync"]:
                fn(e)

        @block.scalar
        def _(e):
            for fn in P.ops["scalar"]:
                fn(e)

        @block.vector
        def _(e):
            for fn in P.ops["vector"]:
                fn(e)

        @block.tensor
        def _(e):
            for fn in P.ops["tensor"]:
                fn(e)

        @block.gpsimd
        def _(e):
            for fn in P.ops["gpsimd"]:
                fn(e)

    return nc


def _colperm_gu():
    cp = np.empty(2 * II, dtype=np.int64)
    for ic in range(NIC):
        blk = PERM2[ic * 128:(ic + 1) * 128]
        cp[(2 * ic) * 128:(2 * ic + 1) * 128] = blk
        cp[(2 * ic + 1) * 128:(2 * ic + 2) * 128] = II + blk
    return cp


def prep_inputs(L0_attn_x, L0_attn_log_decay, L0_attn_Wq, L0_attn_Wk, L0_attn_Wv,
                L0_attn_Wo, L0_attn_Wg1, L0_attn_Wg2, L0_attn_norm_w,
                L0_ffn_gate_up_packed, L0_ffn_gate_up_scales, L0_ffn_down_packed,
                L0_ffn_down_scales):
    perm = PERM
    cpgu = _colperm_gu()
    wq = np.ascontiguousarray(np.asarray(L0_attn_Wq, dtype=np.float32)[perm, :])
    wk = np.ascontiguousarray(np.asarray(L0_attn_Wk, dtype=np.float32)[perm, :])
    wv = np.ascontiguousarray(np.asarray(L0_attn_Wv, dtype=np.float32)[perm, :])
    wo = np.ascontiguousarray(np.asarray(L0_attn_Wo, dtype=np.float32)[:, perm])
    wg1 = np.ascontiguousarray(np.asarray(L0_attn_Wg1, dtype=np.float32)[perm, :])
    wg2 = np.ascontiguousarray(np.asarray(L0_attn_Wg2, dtype=np.float32))
    nw = np.ascontiguousarray(np.asarray(L0_attn_norm_w, dtype=np.float32).reshape(H, 1))
    ldrow = np.ascontiguousarray(np.asarray(L0_attn_log_decay, dtype=np.float32).reshape(1, NH))

    gu_b = np.asarray(L0_ffn_gate_up_packed, dtype=np.int32).view(np.uint8).reshape(2 * II, 1024)
    gu_bt = gu_b.T[:, cpgu]
    gu_sc = np.asarray(L0_ffn_gate_up_scales)[:, cpgu]

    dn_b = np.asarray(L0_ffn_down_packed, dtype=np.int32).view(np.uint8).reshape(H, 2816)
    dn_bt = dn_b[perm, :].T
    dn_sc = np.asarray(L0_ffn_down_scales)[:, perm]

    grow = (np.arange(128)[None, :] >= 64).astype(np.int64)
    x = np.asarray(L0_attn_x, dtype=np.float32)
    in_maps = []
    for c in range(CORES):
        b, qt = c // 4, c % 4
        xs = np.ascontiguousarray(x[b, qt * T:(qt + 1) * T, :][:, perm])
        sl = gu_bt[:, c * GU_COLS:(c + 1) * GU_COLS]
        pk = np.ascontiguousarray(sl.reshape(8, 128, GU_L, 128).transpose(2, 1, 0, 3).reshape(GU_L, 128, 1024))
        g_idx = (2 * np.arange(8)[:, None] + grow).astype(np.int64)
        scs = gu_sc[:, c * GU_COLS:(c + 1) * GU_COLS]
        sc_lpuc = np.ascontiguousarray(
            scs[g_idx].reshape(8, 128, GU_L, 128).transpose(2, 1, 0, 3).reshape(GU_L, 128, 1024).astype(np.float16))
        dsl = dn_bt[:, c * DN_COLS:(c + 1) * DN_COLS]
        pkd = np.ascontiguousarray(dsl.reshape(DN_U, 128, DN_COLS))
        g2 = (2 * np.arange(DN_U)[:, None] + grow).astype(np.int64)
        dscs = dn_sc[:, c * DN_COLS:(c + 1) * DN_COLS]
        sc_d = np.ascontiguousarray(dscs[g2].astype(np.float16))
        gap = np.full((1, CORES), 1e6, dtype=np.float32)
        for r in range(CORES):
            br, qr = r // 4, r % 4
            if br == b and qr < qt:
                gap[0, r] = float(qt - 1 - qr)
        in_maps.append({
            "x": xs, "wq": wq, "wk": wk, "wv": wv, "wo": wo, "wg1": wg1, "wg2": wg2,
            "nw": nw, "ldec": ldrow, "gap": gap,
            "pkgu": pk, "scgu": sc_lpuc, "pkdn": pkd, "scdn": sc_d,
        })
    return in_maps


def kernel(**inputs):
    in_maps = prep_inputs(**inputs)
    es = ExitStack()
    try:
        nc = build(es)
        res = run_bass_kernel_spmd(nc, in_maps, core_ids=list(range(CORES)))
    finally:
        es.close()
    inv = np.argsort(PERM)
    out = np.empty((2, 8192, H), dtype=np.float32)
    for c in range(CORES):
        b, qt = c // 4, c % 4
        out[b, qt * T:(qt + 1) * T, :] = res.results[c]["out"][:, inv]
    return out


# version 21
# speedup vs baseline: 1.0105x; 1.0089x over previous
import numpy as np
from contextlib import ExitStack

import concourse.bass as bass
import concourse.mybir as mybir
from concourse.bass_utils import run_bass_kernel_spmd

F32 = mybir.dt.float32
BF16 = mybir.dt.bfloat16
F16 = mybir.dt.float16
I8 = mybir.dt.int8
U8 = mybir.dt.uint8
AF = mybir.ActivationFunctionType
ALU = mybir.AluOpType
AXX = mybir.AxisListType.X

H = 2048
T = 2048
NH = 16
NTC = 16
NHC = 16
II = 5632
NIC = 44
CORES = 8
GU_COLS = 11264 // CORES
GU_L = GU_COLS // 128
DN_COLS = 2048 // CORES
DN_U = 2816 // 128
LNEPS = 1e-5
SQH = (1.0 / 2048.0) ** 0.5
GEPS = 1e-5
FEPS = 1e-6


def _perm(nchunks):
    p = np.empty(nchunks * 128, dtype=np.int64)
    for k in range(nchunks):
        u, j = k // 2, k % 2
        p[k * 128:(k + 1) * 128] = 256 * u + 2 * np.arange(128) + j
    return p

PERM = _perm(NHC)
PERM2 = _perm(NIC)


class Sem:
    def __init__(self, h):
        self.h = h
        self.n = 0

    def inc(self, k=1):
        self.n += k
        return self.n


class Lanes:

    def __init__(self, mk, n, name):
        self.sems = [mk(f"{name}{i}") for i in range(n)]
        self.n = n
        self.k = 0

    def nxt(self):
        lane = self.sems[self.k % self.n]
        self.k += 1
        return lane


class Plan:
    ENGINES = ("sync", "scalar", "vector", "tensor", "gpsimd")

    def __init__(self):
        self.ops = {e: [] for e in self.ENGINES}

    def on(self, eng, fn):
        self.ops[eng].append(fn)

    def dma(self, eng, sem, out, in_, **kw):
        if isinstance(sem, Lanes):
            sem = sem.nxt()
        t = sem.inc(16)
        self.on(eng, lambda e, out=out, in_=in_, kw=kw, s=sem.h: e.dma_start(out=out, in_=in_, **kw).then_inc(s, 16))
        return (sem, t)

    def waitt(self, eng, tok):
        sem, t = tok
        self.on(eng, lambda e, s=sem.h, t=t: e.wait_ge(s, t))

    def wait(self, eng, sem, tgt):
        if isinstance(tgt, tuple):
            sem, tgt = tgt
        if tgt and tgt > 0:
            self.on(eng, lambda e, s=sem.h, t=tgt: e.wait_ge(s, t))

    def waitl(self, eng, sem, holder, idx):
        def go(e, s=sem.h, holder=holder, idx=idx):
            t = holder[idx]
            if isinstance(t, tuple):
                return e.wait_ge(t[0].h, t[1])
            if t and t > 0:
                return e.wait_ge(s, t)
            return e.nop()
        self.on(eng, go)

    def run(self, eng, fn, sem=None, k=1):
        if sem is None:
            self.on(eng, fn)
            return None
        t = sem.inc(k)
        self.on(eng, lambda e, fn=fn, s=sem.h, k=k: fn(e).then_inc(s, k))
        return t


def build(es, debug=()):
    nc = bass.Bass("TRN2", target_bir_lowering=False)
    dbg = set(debug)

    d_x = nc.dram_tensor("x", [T, H], F32, kind="ExternalInput")
    d_wq = nc.dram_tensor("wq", [H, H], F32, kind="ExternalInput")
    d_wk = nc.dram_tensor("wk", [H, H], F32, kind="ExternalInput")
    d_wv = nc.dram_tensor("wv", [H, H], F32, kind="ExternalInput")
    d_wo = nc.dram_tensor("wo", [H, H], F32, kind="ExternalInput")
    d_wg1 = nc.dram_tensor("wg1", [H, 128], F32, kind="ExternalInput")
    d_wg2 = nc.dram_tensor("wg2", [128, H], F32, kind="ExternalInput")
    d_nw = nc.dram_tensor("nw", [H, 1], F32, kind="ExternalInput")
    d_ld = nc.dram_tensor("ldec", [1, NH], F32, kind="ExternalInput")
    d_gap = nc.dram_tensor("gap", [1, CORES], F32, kind="ExternalInput")
    d_pkgu = nc.dram_tensor("pkgu", [GU_L, 128, 1024], U8, kind="ExternalInput")
    d_scgu = nc.dram_tensor("scgu", [GU_L, 128, 1024], F16, kind="ExternalInput")
    d_pkdn = nc.dram_tensor("pkdn", [DN_U, 128, DN_COLS], U8, kind="ExternalInput")
    d_scdn = nc.dram_tensor("scdn", [DN_U, 128, DN_COLS], F16, kind="ExternalInput")
    d_out = nc.dram_tensor("out", [T, H], F32, kind="ExternalOutput")

    d_ln1g = nc.dram_tensor("ln1g", [NTC * NHC * 128, 128], BF16)
    d_q = nc.dram_tensor("qd", [NTC * NHC * 128, 128], BF16)
    d_oi = nc.dram_tensor("oid", [NTC, 128, H], BF16)
    d_a = nc.dram_tensor("ad", [NTC, 128, H], BF16)
    d_mid = nc.dram_tensor("midd", [128, H], BF16)
    d_y1 = nc.dram_tensor("y1d", [T, H], F32)
    d_ln2 = nc.dram_tensor("ln2d", [T, H], BF16)
    d_agt_i = nc.dram_tensor("agti", [128, H], BF16)
    d_agt_o = nc.dram_tensor("agto", [CORES * 128, H], BF16, addr_space="Shared")
    d_aggu_i = nc.dram_tensor("aggui", [128, GU_L * 2048], BF16)
    d_aggu_o = nc.dram_tensor("agguo", [CORES * 128, GU_L * 2048], BF16, addr_space="Shared")
    d_agdn_i = nc.dram_tensor("agdni", [128, NIC * DN_COLS], BF16)
    d_agdn_o = nc.dram_tensor("agdno", [CORES * 128, NIC * DN_COLS], BF16, addr_space="Shared")

    d_dbg = {}
    def dbgout(name, shape, dt=F32):
        if name in dbg:
            if name not in d_dbg:
                d_dbg[name] = nc.dram_tensor("dbg_" + name, shape, dt, kind="ExternalOutput")
            return d_dbg[name]
        return None

    sbt = lambda n_, sh, dt: es.enter_context(nc.sbuf_tensor(n_, sh, dt))
    ident = sbt("ident", [128, 128], BF16)
    iorow = sbt("iorow", [128, 128], F32)
    iocol = sbt("iocol", [128, 1], F32)
    ld_rep = sbt("ldrep", [128, NH], F32)
    ldneg = sbt("ldneg", [128, NH], F32)
    qdecs = sbt("qdecs", [128, NH], F32)
    kdecs = sbt("kdecs", [128, NH], F32)
    cds = sbt("cds", [128, NH], F32)
    gapr = sbt("gapr", [128, CORES], F32)
    coefr = sbt("coefr", [128, NH], F32)
    tmask = sbt("tmask", [128, 128], F32)
    dargm = sbt("dargm", [128, 128], F32)
    dtmp = sbt("dtmp", [128, 128], F32)
    stat = sbt("stat", [128, 24], F32)
    bstatr = [sbt("bstat0", [128, 16], F32), sbt("bstat1", [128, 16], F32)]
    systat = sbt("systat", [128, 24], F32)
    nwcol = sbt("nwcol", [128, NHC], F32)
    dmsk = sbt("dmsk", [128, NH * 128], BF16)
    cepsA = sbt("cepsA", [128, 1], F32)
    dumf = sbt("dumf", [128, 2], F32)
    kdk8 = sbt("kdk8", [128, 1024], BF16)
    kfm8 = sbt("kfm8", [128, 1024], BF16)
    sdb8 = sbt("sdb8", [128, 1024], BF16)
    sdt8 = sbt("sdt8", [128, 1024], BF16)
    ktcB = sbt("ktcB", [128, H], BF16)[:, :]
    vtcB = sbt("vtcB", [128, H], BF16)[:, :]

    KB = 1024
    R1, R2, R3 = 0, 64 * KB, 128 * KB
    ARN = 182 * KB
    arena = sbt("arena", [128, ARN], U8)

    def carve(off, nelem, dt):
        sz = {F32: 4, BF16: 2, F16: 2, I8: 1, U8: 1}[dt]
        ap = arena[:, off:off + nelem * sz].bitcast(dt)
        return ap, off + nelem * sz

    def carve_list(off, n, nelem, dt):
        out = []
        for _ in range(n):
            ap, off = carve(off, nelem, dt)
            out.append(ap)
        return out, off

    ln1f, o1 = carve_list(R1, NHC, T, BF16)
    assert o1 <= R2
    wv_sb, _ = carve_list(R1, NHC, H, BF16)
    wo_sb = wv_sb
    ysb, o = carve_list(R1, 8, H, BF16)
    y1rb, o = carve(o, H, F32)
    wdnb, o = carve_list(o, 2, 11 * DN_COLS, BF16)
    glb, o = carve(o, 1024, BF16)
    glf = glb.bitcast(F32)
    hact_r1_base = o
    assert hact_r1_base + 5 * 2 * KB <= R2, hact_r1_base

    o = R2
    gub, o = carve_list(o, GU_L, 1024, U8)
    gscr, o = carve_list(o, GU_L, 1024, F16)
    gdec, o = carve_list(o, 4, 2048, BF16)
    dvi, o = carve_list(o, 3, 1024, I8)
    dvf, o = carve_list(o, 2, 1024, F32)
    assert o <= R3, o
    dnbb = gub[0].tensor if False else None
    dnbb, _ = carve(R2, DN_U * DN_COLS, U8)
    dnsr2, _ = carve(R2 + GU_L * KB, DN_U * DN_COLS, F16)
    dnstg = [gdec[0], gdec[1]]
    wk_sb, _ = carve_list(R2, NHC, H, BF16)
    o = R2
    sinb, o = carve(o, H, F32)
    sbfb, o = carve(o, H, BF16)
    oitcr, o = carve_list(o, 2, H, BF16)
    a2tcr, o = carve_list(o, 2, H, BF16)
    onbr, o = carve_list(o, 2, H, BF16)
    ontbb, o = carve(o, H, BF16)
    aof0, o = carve(o, H, F32)
    bln20, o = carve(o, H, BF16)
    gsigb, o = carve(o, H, BF16)
    peerb, o = carve(o, H, BF16)
    wg2, o = carve(o, H, BF16)
    assert o <= R3, o
    o = R2
    ln2f, o = carve_list(o, NHC, 1024, BF16)
    wgub, o = carve_list(o, 2, 2048, BF16)
    hact_r2_base = o
    assert hact_r2_base + 12 * 2 * KB <= R3

    o = R3
    xb2, o = carve_list(o, 2, H, F32)
    ln1b, o = carve(o, H, BF16)
    o = R3
    wqcolw, o = carve_list(o, 2, H, BF16)
    wg1, o = carve(o, H, BF16)
    qout, o = carve_list(o, 4, 512, BF16)
    wqf32, o = carve_list(o, 2, H, F32)
    o = R3
    ln1tc, o = carve(o, H, BF16)
    ktc, o = carve(o, H, BF16)
    vtc, o = carve(o, H, BF16)
    qtc2, o = carve(o, 2 * H, BF16)
    otc, o = carve(o, H, BF16)
    atc, o = carve(o, H, BF16)
    tstate, o = carve(o, H, F32)
    midb, o = carve(o, H, BF16)
    assert o <= R3 + 48 * KB, o
    xtc2r = [carve(R3 + 28 * KB, H, F32)[0], carve(R3, H, F32)[0]]
    aofr = [aof0, carve(R3 + 40 * KB, H, F32)[0]]
    bln2r = [bln20, carve(R3 + 48 * KB, H, BF16)[0]]
    qtc = [qtc2[:, 0:H], qtc2[:, H:2 * H]]
    hact = []
    for i in range(27):
        apx, _ = carve(R3 + i * 2 * KB, 1024, BF16)
        hact.append(apx)
    for i in range(12):
        apx, _ = carve(hact_r2_base + i * 2 * KB, 1024, BF16)
        hact.append(apx)
    for i in range(5):
        apx, _ = carve(hact_r1_base + i * 2 * KB, 1024, BF16)
        hact.append(apx)

    psb = [es.enter_context(nc.psum_tensor(f"pb{i}", [128, 512], F32)) for i in range(8)]

    mksem = lambda n_: Sem(es.enter_context(nc.semaphore(n_)))
    qS = mksem("qS")
    qG = mksem("qG")
    qA = mksem("qA")
    pe = mksem("pe")
    dv = mksem("dv")
    ac = mksem("ac")
    gp = mksem("gp")
    tokfull = lambda sm: (sm, sm.n)
    qLD = mksem("qLD"); qNW = mksem("qNW")
    qX = Lanes(mksem, 2, "qX")
    qQSP = Lanes(mksem, 4, "qQSP")
    qLNF = mksem("qLNF")
    qL1 = mksem("qL1")
    qQT = Lanes(mksem, 2, "qQT")
    qOSP = mksem("qOSP"); qASP = mksem("qASP"); qTSPs = mksem("qTSPs")
    qOI = Lanes(mksem, 2, "qOI"); qA2 = Lanes(mksem, 2, "qA2"); qX2 = Lanes(mksem, 2, "qX2")
    qY1S = mksem("qY1S"); qL2Sa = mksem("qL2Sa"); qL2Sb = mksem("qL2Sb")
    qL2F = mksem("qL2F"); qY1R = mksem("qY1R"); qPEER = mksem("qPEER")
    qWQ = Lanes(mksem, 2, "qWQ"); qWG1 = mksem("qWG1"); qWG2 = mksem("qWG2")
    qWK = mksem("qWK"); qWV = mksem("qWV"); qWO = mksem("qWO")
    qGUB = mksem("qGUB"); qDNB = mksem("qDNB")
    qGSC = Lanes(mksem, 4, "qGSC"); qDNS = Lanes(mksem, 4, "qDNS")
    qGSP = Lanes(mksem, 4, "qGSP"); qDSP = Lanes(mksem, 2, "qDSP")
    qWGU = Lanes(mksem, 2, "qWGU"); qWDN = Lanes(mksem, 2, "qWDN")

    P = Plan()
    bf = lambda ap: ap.bitcast(BF16)

    for l in range(GU_L):
        P.dma("scalar", qGUB, gub[l], d_pkgu[l])
    for l in range(GU_L):
        P.dma("scalar", qGUB, gscr[l], d_scgu[l])
    t_gu_in = tokfull(qGUB)

    P.run("gpsimd", lambda g: g.iota(iocol[:, :], [[1, 1]], channel_multiplier=1, allow_small_or_imprecise_dtypes=True))
    P.run("gpsimd", lambda g: g.iota(iorow[:, :], [[1, 128]], channel_multiplier=0, allow_small_or_imprecise_dtypes=True), gp)
    t_iota = gp.n
    t_ln1tm = [0] * NTC
    t_qgrp_pe = [0] * NHC

    P.dma("sync", qLD, ld_rep[:, :], bass.AP(d_ld, 0, [[0, 128], [1, NH]]))
    P.dma("sync", qLD, gapr[:, :], bass.AP(d_gap, 0, [[0, 128], [1, CORES]]))
    t_ldrep = tokfull(qLD)
    for ec in range(NHC):
        P.dma("sync", qNW, nwcol[:, ec:ec + 1], d_nw[ec * 128:(ec + 1) * 128, :])
    t_nw = tokfull(qNW)

    P.wait("vector", gp, t_iota)
    P.run("vector", lambda v: v.tensor_scalar(out=ident[:, :], in0=iorow[:, :], scalar1=iocol[:, 0:1], scalar2=None, op0=ALU.is_equal), dv)
    t_ident = dv.n
    P.run("vector", lambda v: v.memset(cepsA[:, :], LNEPS), dv)
    t_ceps = dv.n
    P.wait("vector", qS, t_ldrep)
    P.run("vector", lambda v: v.tensor_scalar_mul(out=ldneg[:, :], in0=ld_rep[:, :], scalar1=-1.0))
    P.run("vector", lambda v: v.tensor_scalar(out=tmask[:, :], in0=iorow[:, :], scalar1=iocol[:, 0:1], scalar2=None, op0=ALU.subtract))
    P.run("vector", lambda v: v.tensor_scalar_min(out=dargm[:, :], in0=tmask[:, :], scalar1=0.0))
    P.run("vector", lambda v: v.tensor_scalar(out=tmask[:, :], in0=tmask[:, :], scalar1=0.0, scalar2=None, op0=ALU.is_le), dv)
    t_dargs = dv.n

    P.wait("scalar", qS, t_ldrep)
    P.wait("scalar", gp, t_iota)
    P.run("scalar", lambda s: s.activation(out=stat[:, 0:1], in_=iocol[:, 0:1], func=AF.Copy, bias=1.0))
    P.run("scalar", lambda s: s.activation(out=stat[:, 1:2], in_=iocol[:, 0:1], func=AF.Copy, scale=-1.0, bias=127.0))
    for h in range(NH):
        P.run("scalar", lambda s, h=h: s.activation(out=qdecs[:, h:h + 1], in_=stat[:, 0:1], func=AF.Exp, scale=ld_rep[:, h:h + 1]))
        P.run("scalar", lambda s, h=h: s.activation(out=kdecs[:, h:h + 1], in_=stat[:, 1:2], func=AF.Exp, scale=ld_rep[:, h:h + 1]))
        P.run("scalar", lambda s, h=h: s.activation(out=cds[:, h:h + 1], in_=ld_rep[:, h:h + 1], func=AF.Exp, scale=128.0))
    P.run("scalar", lambda s: s.nop(), ac)
    t_decay = ac.n
    P.wait("scalar", dv, t_dargs)
    t_dmh = 0
    for h in range(NH):
        if t_dmh:
            P.wait("scalar", dv, t_dmh)
        P.run("scalar", lambda s, h=h: s.activation(out=dtmp[:, :], in_=dargm[:, :], func=AF.Exp, scale=ldneg[:, h:h + 1]), ac)
        th = ac.n
        P.wait("vector", ac, th)
        P.run("vector", lambda v, h=h: v.tensor_mul(out=dmsk[:, h * 128:(h + 1) * 128], in0=dtmp[:, :], in1=tmask[:, :]), dv)
        t_dmh = dv.n
    t_dmask = dv.n

    def dec_ops(e, src3, n8, m8, v8, f1, f2, scl3, dst3, hi):
        if hi:
            e.tensor_scalar(out=m8, in0=src3, scalar1=4, scalar2=7, op0=ALU.logical_shift_right, op1=ALU.bitwise_and)
            e.tensor_scalar(out=n8, in0=src3, scalar1=4, scalar2=8, op0=ALU.logical_shift_right, op1=ALU.bitwise_and)
        else:
            e.tensor_scalar(out=m8, in0=src3, scalar1=7, scalar2=None, op0=ALU.bitwise_and)
            e.tensor_scalar(out=n8, in0=src3, scalar1=8, scalar2=None, op0=ALU.bitwise_and)
        e.tensor_scalar(out=v8, in0=m8, scalar1=2, scalar2=-4, op0=ALU.mult, op1=ALU.add)
        e.tensor_tensor(out=v8, in0=m8, in1=v8, op=ALU.max)
        e.tensor_scalar(out=m8, in0=m8, scalar1=4, scalar2=-16, op0=ALU.mult, op1=ALU.add)
        e.tensor_tensor(out=v8, in0=v8, in1=m8, op=ALU.max)
        e.tensor_scalar(out=f1, in0=n8, scalar1=-0.125, scalar2=0.5, op0=ALU.mult, op1=ALU.add)
        e.tensor_tensor(out=f2, in0=v8, in1=f1, op=ALU.mult)
        return e.tensor_tensor(out=dst3, in0=f2, in1=scl3, op=ALU.mult)

    def r3(ap, c):
        return ap.rearrange("p (u c) -> p u c", c=c)

    t_glo = [0] * GU_L
    t_ghi = [0] * GU_L
    t_gsp = [0] * (GU_L + 1)

    def emit_gu_l(l):
        src = r3(gub[l].bitcast(I8), 128)
        scl3 = r3(gscr[l], 128)
        ob = gdec[l % 4]
        dlo = r3(ob, 256)[:, :, 0:128]
        dhi = r3(ob, 256)[:, :, 128:256]
        P.wait("vector", qA, t_gu_in)
        if l >= 4:
            P.wait("vector", qG, t_gsp[l - 4])
        P.run("vector", lambda v, src=src, scl3=scl3, dlo=dlo: dec_ops(
            v, src, r3(dvi[0], 128), r3(dvi[1], 128), r3(dvi[2], 128), r3(dvf[0], 128), r3(dvf[1], 128), scl3, dlo, False), dv)
        t_glo[l] = dv.n
        P.run("vector", lambda v, src=src, scl3=scl3, dhi=dhi: dec_ops(
            v, src, r3(dvi[0], 128), r3(dvi[1], 128), r3(dvi[2], 128), r3(dvf[0], 128), r3(dvf[1], 128), scl3, dhi, True), dv)
        t_ghi[l] = dv.n

    d_ln1dbg = dbgout("ln1", [T, H], BF16)
    t_x = [0] * NTC
    t_tr_pe = [0] * NTC
    t_sqL = [0] * NTC
    prev2 = [0, 0]
    for tc in range(NTC):
        if tc >= 2:
            P.wait("sync", dv, t_ln1tm[tc - 2])
            P.wait("sync", ac, t_sqL[tc - 2])
        t_x[tc] = P.dma("sync", qX, xb2[tc % 2], d_x[tc * 128:(tc + 1) * 128, :])
        xb = xb2[tc % 2]
        P.wait("vector", qS, t_x[tc])
        P.run("vector", lambda v, xb=xb: v.tensor_reduce(out=stat[:, 2:3], in_=xb, axis=AXX, op=ALU.add), dv)
        P.wait("scalar", qS, t_x[tc])
        if tc >= 1:
            P.wait("scalar", pe, t_tr_pe[tc - 1])
        P.run("scalar", lambda s, xb=xb: s.activation(out=ln1b[:, 0:1024], in_=xb[:, 0:1024], func=AF.Square, accum_out=stat[:, 3:4]))
        P.run("scalar", lambda s, xb=xb: s.activation(out=ln1b[:, 1024:2048], in_=xb[:, 1024:2048], func=AF.Square, accum_out=stat[:, 10:11]), ac)
        t_sq = ac.n
        t_sqL[tc] = t_sq
        P.wait("vector", ac, t_sq)
        P.run("vector", lambda v: v.tensor_add(out=stat[:, 3:4], in0=stat[:, 3:4], in1=stat[:, 10:11]))
        P.run("vector", lambda v: v.tensor_scalar_mul(out=stat[:, 4:5], in0=stat[:, 2:3], scalar1=1.0 / H))
        P.run("vector", lambda v: v.tensor_scalar_mul(out=stat[:, 5:6], in0=stat[:, 3:4], scalar1=1.0 / H))
        P.run("vector", lambda v: v.tensor_mul(out=stat[:, 6:7], in0=stat[:, 4:5], in1=stat[:, 4:5]))
        P.run("vector", lambda v: v.tensor_copy(out=dumf[:, 0:1], in_=dumf[:, 1:2]))
        P.run("vector", lambda v: v.tensor_sub(out=stat[:, 6:7], in0=stat[:, 5:6], in1=stat[:, 6:7]), dv)
        t_var = dv.n
        P.wait("scalar", dv, t_var)
        P.run("scalar", lambda s: s.activation(out=stat[:, 7:8], in_=stat[:, 6:7], func=AF.Ln, bias=cepsA[:, 0:1]))
        P.run("scalar", lambda s: s.activation(out=stat[:, 8:9], in_=stat[:, 7:8], func=AF.Exp, scale=-0.5), ac)
        t_rstd = ac.n
        P.wait("vector", ac, t_rstd)
        P.run("vector", lambda v: v.tensor_scalar(out=stat[:, 9:10], in0=stat[:, 4:5], scalar1=stat[:, 8:9], scalar2=-1.0, op0=ALU.mult, op1=ALU.mult))
        P.run("vector", lambda v: v.tensor_copy(out=dumf[:, 0:1], in_=dumf[:, 1:2]))
        if tc >= 1:
            P.wait("vector", pe, t_tr_pe[tc - 1])
        P.run("vector", lambda v, xb=xb: v.tensor_scalar(out=ln1b, in0=xb, scalar1=stat[:, 8:9], scalar2=stat[:, 9:10], op0=ALU.mult, op1=ALU.add), dv)
        t_ln1tm[tc] = dv.n
        if d_ln1dbg is not None:
            P.wait("sync", dv, t_ln1tm[tc])
            P.dma("sync", qS, d_ln1dbg[tc * 128:(tc + 1) * 128, :], ln1b)
        P.wait("tensor", dv, t_ln1tm[tc])
        if tc == 0:
            P.wait("tensor", dv, t_ident)
        for hc in range(NHC):
            bank = 6 + (hc % 2)
            P.wait("tensor", dv, prev2[hc % 2])
            P.run("tensor", lambda t, hc=hc, bank=bank: t.transpose(bf(psb[bank][:, 0:64]), ln1b[:, hc * 128:(hc + 1) * 128], ident[:, :]), pe)
            tp = pe.n
            P.wait("vector", pe, tp)
            P.run("vector", lambda v, hc=hc, tc=tc, bank=bank: v.tensor_copy(out=ln1f[hc][:, tc * 128:(tc + 1) * 128], in_=bf(psb[bank][:, 0:64])), dv)
            prev2[hc % 2] = dv.n
        t_tr_pe[tc] = pe.n
    t_ln1_all = dv.n
    for l in range(GU_L):
        emit_gu_l(l)

    def dec_ops(e, src3, n8, m8, v8, f1, f2, scl3, dst3, hi):
        if hi:
            e.tensor_scalar(out=m8, in0=src3, scalar1=4, scalar2=7, op0=ALU.logical_shift_right, op1=ALU.bitwise_and)
            e.tensor_scalar(out=n8, in0=src3, scalar1=4, scalar2=8, op0=ALU.logical_shift_right, op1=ALU.bitwise_and)
        else:
            e.tensor_scalar(out=m8, in0=src3, scalar1=7, scalar2=None, op0=ALU.bitwise_and)
            e.tensor_scalar(out=n8, in0=src3, scalar1=8, scalar2=None, op0=ALU.bitwise_and)
        e.tensor_scalar(out=v8, in0=m8, scalar1=2, scalar2=-4, op0=ALU.mult, op1=ALU.add)
        e.tensor_tensor(out=v8, in0=m8, in1=v8, op=ALU.max)
        e.tensor_scalar(out=m8, in0=m8, scalar1=4, scalar2=-16, op0=ALU.mult, op1=ALU.add)
        e.tensor_tensor(out=v8, in0=v8, in1=m8, op=ALU.max)
        e.tensor_scalar(out=f1, in0=n8, scalar1=-0.125, scalar2=0.5, op0=ALU.mult, op1=ALU.add)
        e.tensor_tensor(out=f2, in0=v8, in1=f1, op=ALU.mult)
        return e.tensor_tensor(out=dst3, in0=f2, in1=scl3, op=ALU.mult)

    def r3(ap, c):
        return ap.rearrange("p (u c) -> p u c", c=c)

    for l in range(GU_L):
        P.wait("gpsimd", dv, t_ghi[l])
        t_gsp[l] = P.dma("gpsimd", qGSP, d_aggu_i[:, l * 2048:(l + 1) * 2048], gdec[l % 4])
    P.wait("gpsimd", dv, t_ghi[GU_L - 1])
    t_dnb = P.dma("gpsimd", qDNB, dnbb, bass.AP(d_pkdn, 0, [[DN_COLS, 128], [128 * DN_COLS, DN_U], [1, DN_COLS]]))
    t_dns = P.dma("gpsimd", qDNB, dnsr2, bass.AP(d_scdn, 0, [[DN_COLS, 128], [128 * DN_COLS, DN_U], [1, DN_COLS]]))
    NG = DN_U // 2
    t_dgrp = [0] * NG
    t_dsp = [0] * NG
    P.wait("vector", qG, t_gsp[8])
    P.wait("vector", qG, t_gsp[9])
    P.wait("vector", qG, t_dnb)
    P.wait("vector", qG, t_dns)
    for g_ in range(NG):
        u0 = 2 * g_
        src = r3(dnbb[:, u0 * DN_COLS:(u0 + 2) * DN_COLS].bitcast(I8), DN_COLS)
        scl3 = r3(dnsr2[:, u0 * DN_COLS:(u0 + 2) * DN_COLS], DN_COLS)
        stg = dnstg[g_ % 2][:, 0:1024]
        s4 = stg.rearrange("p (u two c) -> p u two c", two=2, c=DN_COLS)
        dlo = s4[:, :, 0:1, :].squeeze(2)
        dhi = s4[:, :, 1:2, :].squeeze(2)
        if g_ >= 2:
            P.wait("vector", qG, t_dsp[g_ - 2])
        si = [r3(dvi[i][:, 0:512], DN_COLS) for i in range(3)]
        sf = [r3(dvf[i][:, 0:512], DN_COLS) for i in range(2)]
        P.run("vector", lambda v, src=src, scl3=scl3, dlo=dlo, si=si, sf=sf: dec_ops(v, src, si[0], si[1], si[2], sf[0], sf[1], scl3, dlo, False), dv)
        P.run("vector", lambda v, src=src, scl3=scl3, dhi=dhi, si=si, sf=sf: dec_ops(v, src, si[0], si[1], si[2], sf[0], sf[1], scl3, dhi, True), dv)
        t_dgrp[g_] = dv.n
        P.wait("gpsimd", dv, t_dgrp[g_])
        t_dsp[g_] = P.dma("gpsimd", qDSP, d_agdn_i[:, g_ * 1024:(g_ + 1) * 1024], stg)
    t_dec_dv = max(t_ghi[-1], t_dgrp[-1])

    P.wait("gpsimd", dv, t_dec_dv)
    P.wait("gpsimd", qG, t_dsp[NG - 1])
    P.wait("gpsimd", qG, t_dsp[NG - 2])
    t_wk = 0
    for hc in range(NHC):
        t_wk = P.dma("gpsimd", qWK, wk_sb[hc], d_wk[hc * 128:(hc + 1) * 128, :])
    P.waitl("gpsimd", dv, t_ln1tm, NTC - 1)
    for hc in range(NHC):
        P.dma("gpsimd", qWG1, wg1[:, hc * 128:(hc + 1) * 128], d_wg1[hc * 128:(hc + 1) * 128, :])
    t_wg = tokfull(qWG1)
    h_ln1free = [0, 0]
    P.waitl("gpsimd", qS, h_ln1free, 0)
    P.waitl("gpsimd", pe, h_ln1free, 1)
    t_wv = 0
    for hc in range(NHC):
        t_wv = P.dma("gpsimd", qWV, wv_sb[hc], d_wv[hc * 128:(hc + 1) * 128, :])

    for l in range(GU_L - 4, GU_L):
        P.wait("gpsimd", qG, t_gsp[l])
    P.run("gpsimd", lambda g: g.collective_compute(
        "AllGather", ALU.bypass, replica_groups=[list(range(CORES))],
        ins=[d_aggu_i.ap().opt()], outs=[d_aggu_o.ap().opt()]), gp)
    t_aggu = gp.n
    P.wait("gpsimd", qG, t_dsp[NG - 1])
    P.wait("gpsimd", qG, t_dsp[NG - 2])
    P.run("gpsimd", lambda g: g.collective_compute(
        "AllGather", ALU.bypass, replica_groups=[list(range(CORES))],
        ins=[d_agdn_i.ap().opt()], outs=[d_agdn_o.ap().opt()]), gp)
    t_agdn = gp.n

    t_qev = []
    t_qspill = []
    t_wqf = [0] * NHC
    t_wqc = [0] * NHC

    def load_wqf(ec):
        if ec >= 2:
            P.wait("sync", ac, t_wqc[ec - 2])
        else:
            P.wait("sync", dv, t_ln1tm[NTC - 1])
            P.wait("sync", pe, t_tr_pe[NTC - 1])
        t_wqf[ec] = P.dma("sync", qWQ, wqf32[ec % 2][:, :], bass.AP(
            d_wq, ec * 128, [[H, 128], [128 * H, NHC], [1, 128]]))

    def cast_wq(ec):
        P.wait("scalar", qS, t_wqf[ec])
        if ec >= 2:
            P.wait("scalar", pe, t_qgrp_pe[ec - 2])
        if ec == 0:
            P.waitl("scalar", dv, t_ln1tm, NTC - 1)
        P.run("scalar", lambda s, ec=ec: s.activation(out=wqcolw[ec % 2][:, :], in_=wqf32[ec % 2][:, :], func=AF.Copy), ac)
        t_wqc[ec] = ac.n

    load_wqf(0)
    load_wqf(1)
    cast_wq(0)
    cast_wq(1)
    for ec in range(NHC):
        if ec + 2 < NHC:
            load_wqf(ec + 2)
        P.wait("tensor", ac, t_wqc[ec])
        if ec == 0:
            P.wait("tensor", dv, t_ln1_all)
        for es4 in range(4):
            bank = (ec * 4 + es4) % 2
            if len(t_qev) >= 2:
                P.wait("tensor", ac, t_qev[-2])
            for hc in range(NHC):
                P.run("tensor", lambda t, ec=ec, hc=hc, es4=es4, bank=bank: t.matmul(
                    psb[bank][:, :], wqcolw[ec % 2][:, hc * 128:(hc + 1) * 128], ln1f[hc][:, es4 * 512:(es4 + 1) * 512],
                    start=(hc == 0), stop=(hc == NHC - 1)), pe if hc == NHC - 1 else None)
            t_mm = pe.n
            qo = qout[(ec * 4 + es4) % 4]
            P.wait("scalar", pe, t_mm)
            if len(t_qspill) >= 4:
                P.wait("scalar", qS, t_qspill[-4])
            P.run("scalar", lambda s, qo=qo, bank=bank: s.activation(out=qo[:, :], in_=psb[bank][:, :], func=AF.Silu), ac)
            t_qev.append(ac.n)
            P.wait("sync", ac, t_qev[-1])
            t_qspill.append(P.dma("sync", qQSP, bass.AP(
                d_q, (es4 * NHC + ec) * 128 * 512, [[512, 128], [1, 512]]), qo[:, :]))
        t_qgrp_pe[ec] = pe.n
        if ec + 2 < NHC:
            cast_wq(ec + 2)
    t_midev = [0] * 4
    for es4 in range(4):
        P.wait("tensor", qG, t_wg)
        if es4 >= 1:
            P.wait("tensor", ac, t_midev[es4 - 1])
        for hc in range(NHC):
            P.run("tensor", lambda t, hc=hc, es4=es4: t.matmul(
                psb[2][:, :], wg1[:, hc * 128:(hc + 1) * 128], ln1f[hc][:, es4 * 512:(es4 + 1) * 512],
                start=(hc == 0), stop=(hc == NHC - 1)), pe if hc == NHC - 1 else None)
        t_mm = pe.n
        P.wait("scalar", pe, t_mm)
        P.run("scalar", lambda s, es4=es4: s.activation(out=midb[:, es4 * 512:(es4 + 1) * 512], in_=psb[2][:, :], func=AF.Copy), ac)
        t_midev[es4] = ac.n
    t_mid = ac.n
    t_mid_pe = pe.n
    t_l1sp = [0] * NHC
    for hc in range(NHC):
        P.wait("sync", pe, t_mid_pe)
        t_l1sp[hc] = P.dma("sync", qLNF, bass.AP(
            d_ln1g, hc * 128 * 128, [[128, 128], [NHC * 128 * 128, NTC], [1, 128]]),
            ln1f[hc][:, :].rearrange("p (t c) -> p t c", c=128))
    t_ln1_spill = tokfull(qLNF)
    h_ln1free[0] = t_ln1_spill
    h_ln1free[1] = t_mid_pe

    d_kdbg = dbgout("ktm", [NTC, 128, H], BF16)
    d_vdbg = dbgout("vtm", [NTC, 128, H], BF16)
    d_odbg = dbgout("ointra", [NTC, 128, H], BF16)
    P.wait("vector", pe, t_mid_pe)
    P.wait("vector", qS, t_ln1_spill)
    P.run("vector", lambda v: v.memset(tstate, 0.0), dv)
    t_tzero = dv.n
    P.wait("vector", ac, t_decay)
    ktcr = [ktc, ktcB]
    vtcr = [vtc, vtcB]
    t_qtc = [0] * NTC
    t_l1tc = [0] * NTC
    t_qlastuse = [0] * NTC
    t_scan_dv = [0] * NTC
    t_scan_ac = [0] * NTC
    t_kv_pe = [0] * NTC
    t_kv_ac = [0] * NTC
    t_kev = [0] * NTC
    t_vev = [0] * NTC
    t_kread_dv = [0] * NTC
    t_ospill = [None] * NTC
    t_aspill = [None] * NTC
    sc = {}

    def kv_group(tc, gi):
        isv = gi >= 4
        es4 = gi % 4
        bank = (2 if isv else 0) + es4 % 2
        wsb = wv_sb if isv else wk_sb
        dst = vtcr[tc % 2] if isv else ktcr[tc % 2]
        if gi == 0:
            if tc == 0:
                P.wait("tensor", ac, ac.n)
            P.wait("tensor", qG, t_wk)
            P.wait("tensor", qS, t_l1tc[tc])
        if gi == 4:
            P.wait("tensor", qG, t_wv)
        if es4 >= 2:
            P.wait("tensor", ac, sc[("kvev", tc, gi - 2)])
        elif gi == 0 and tc >= 1:
            P.wait("tensor", ac, t_kv_ac[tc - 1])
        for hc in range(NHC):
            P.run("tensor", lambda t, hc=hc, es4=es4, wsb=wsb: t.matmul(
                psb[(2 if isv else 0) + es4 % 2][:, :] if False else psb[bank][:, :],
                ln1tc[:, hc * 128:(hc + 1) * 128], wsb[hc][:, es4 * 512:(es4 + 1) * 512],
                start=(hc == 0), stop=(hc == NHC - 1)), pe if hc == NHC - 1 else None)
        t_mm = pe.n
        sc[("kvmm", tc, gi)] = t_mm

    def kv_evac(tc, gi):
        isv = gi >= 4
        es4 = gi % 4
        bank = (2 if isv else 0) + es4 % 2
        dst = vtcr[tc % 2] if isv else ktcr[tc % 2]
        P.wait("scalar", pe, sc[("kvmm", tc, gi)])
        if tc >= 2:
            P.wait("scalar", dv, t_kread_dv[tc - 2] if not isv else t_scan_dv[tc - 2])
            P.wait("scalar", pe, t_qlastuse[tc - 2])
        P.run("scalar", lambda s, dst=dst, es4=es4, bank=bank: s.activation(
            out=dst[:, es4 * 512:(es4 + 1) * 512], in_=psb[bank][:, :], func=AF.Silu), ac)
        sc[("kvev", tc, gi)] = ac.n
        if gi == 3:
            t_kev[tc] = ac.n
        if gi == 7:
            t_vev[tc] = ac.n
            t_kv_ac[tc] = ac.n

    def scan_dv_batch(tc, which, hb):
        hs = hb * 8
        kt = ktcr[tc % 2]
        vt = vtcr[tc % 2]
        if which == 0:
            P.wait("vector", ac, t_kev[tc])
            if tc == 0 and hb == 0:
                P.wait("vector", dv, t_tzero)
                P.wait("vector", dv, t_dmask)
            for j in range(8):
                h = hs + j
                P.run("vector", lambda v, j=j, h=h, kt=kt: v.tensor_scalar_mul(
                    out=kdk8[:, j * 128:(j + 1) * 128], in0=kt[:, h * 128:(h + 1) * 128], scalar1=kdecs[:, h:h + 1]),
                    dv if j == 7 else None)
            sc[("D0", hb)] = dv.n
            if hb == 1:
                t_kread_dv[tc] = dv.n
        elif which == 1:
            P.wait("vector", pe, sc[("SA", hb)])
            if t_aspill[tc - 1] is not None and hb == 0:
                P.wait("vector", qS, t_aspill[tc - 1])
            for j in range(8):
                h = hs + j
                bk = 4 + j // 4
                off = (j % 4) * 128
                P.run("vector", lambda v, h=h, bk=bk, off=off: v.tensor_copy(
                    out=atc[:, h * 128:(h + 1) * 128], in_=psb[bk][:, off:off + 128]), dv)
                P.run("vector", lambda v, h=h: v.scalar_tensor_tensor(
                    out=tstate[:, h * 128:(h + 1) * 128], in0=tstate[:, h * 128:(h + 1) * 128], scalar=cds[:, h:h + 1],
                    in1=atc[:, h * 128:(h + 1) * 128], op0=ALU.mult, op1=ALU.add), dv if j == 7 else None)
            sc[("D1", hb)] = dv.n
        elif which == 2:
            P.wait("vector", pe, sc[("SS", hb)])
            for j in range(8):
                h = hs + j
                bk = 4 + j // 4
                off = (j % 4) * 128
                P.run("vector", lambda v, j=j, h=h, bk=bk, off=off: v.tensor_mul(
                    out=sdb8[:, j * 128:(j + 1) * 128], in0=psb[bk][:, off:off + 128], in1=dmsk[:, h * 128:(h + 1) * 128]),
                    dv if j == 7 else None)
            sc[("D2", hb)] = dv.n
        elif which == 3:
            P.wait("vector", pe, sc[("SO", hb)])
            if t_ospill[tc - 1] is not None and hb == 0:
                P.wait("vector", qS, t_ospill[tc - 1])
            for j in range(8):
                h = hs + j
                bk = 4 + j // 4
                off = (j % 4) * 128
                P.run("vector", lambda v, h=h, bk=bk, off=off: v.tensor_copy(
                    out=otc[:, h * 128:(h + 1) * 128], in_=psb[bk][:, off:off + 128]), dv)
            sc[("D3", hb)] = dv.n
            if hb == 1:
                t_scan_dv[tc] = dv.n

    def scan_ac_batch(tc, which, hb):
        if which == 0:
            P.wait("scalar", pe, sc[("ST", hb)])
            for j in range(8):
                P.run("scalar", lambda s, j=j: s.activation(
                    out=kfm8[:, j * 128:(j + 1) * 128], in_=bf(psb[6][:, j * 64:(j + 1) * 64]), func=AF.Copy),
                    ac if j == 7 else None)
            sc[("A0", hb)] = ac.n
        else:
            P.wait("scalar", pe, sc[("SD", hb)])
            for j in range(8):
                P.run("scalar", lambda s, j=j: s.activation(
                    out=sdt8[:, j * 128:(j + 1) * 128], in_=bf(psb[7][:, j * 64:(j + 1) * 64]), func=AF.Copy),
                    ac if j == 7 else None)
            sc[("A1", hb)] = ac.n
            if hb == 1:
                t_scan_ac[tc] = ac.n

    def scan_pe_sub(tc, si):
        hb, st = si // 5, si % 5
        hs = hb * 8
        kt = ktcr[tc % 2]
        vt = vtcr[tc % 2]
        if st == 0:
            scan_dv_batch(tc, 0, hb)
            P.wait("tensor", dv, sc[("D0", hb)])
            if hb == 0:
                if tc >= 1:
                    P.wait("tensor", dv, t_scan_dv[tc - 1])
            else:
                P.wait("tensor", dv, sc[("D3", 0)])
            P.wait("tensor", ac, t_vev[tc])
            for j in range(8):
                h = hs + j
                bk = 4 + j // 4
                off = (j % 4) * 128
                P.run("tensor", lambda t, j=j, h=h, bk=bk, off=off, vt=vt: t.matmul(
                    psb[bk][:, off:off + 128], kdk8[:, j * 128:(j + 1) * 128], vt[:, h * 128:(h + 1) * 128],
                    start=True, stop=True), pe if j == 7 else None)
            sc[("SA", hb)] = pe.n
            scan_dv_batch(tc, 1, hb)
        elif st == 1:
            if hb == 1:
                P.wait("tensor", ac, sc[("A0", 0)])
            elif tc >= 1:
                P.wait("tensor", ac, sc[("A0p", 1)] if ("A0p", 1) in sc else 0)
            for j in range(8):
                h = hs + j
                P.run("tensor", lambda t, j=j, h=h, kt=kt: t.transpose(
                    bf(psb[6][:, j * 64:(j + 1) * 64]), kt[:, h * 128:(h + 1) * 128], ident[:, :]), pe if j == 7 else None)
            sc[("ST", hb)] = pe.n
            scan_ac_batch(tc, 0, hb)
        elif st == 2:
            P.wait("tensor", dv, sc[("D1", hb)])
            P.wait("tensor", ac, sc[("A0", hb)])
            P.wait("tensor", qS, t_qtc[tc])
            for j in range(8):
                h = hs + j
                bk = 4 + j // 4
                off = (j % 4) * 128
                P.run("tensor", lambda t, j=j, h=h, bk=bk, off=off, tc=tc: t.matmul(
                    psb[bk][:, off:off + 128], qtc[tc % 2][:, h * 128:(h + 1) * 128], kfm8[:, j * 128:(j + 1) * 128],
                    start=True, stop=True), pe if j == 7 else None)
            sc[("SS", hb)] = pe.n
            scan_dv_batch(tc, 2, hb)
        elif st == 3:
            P.wait("tensor", dv, sc[("D2", hb)])
            if hb == 1:
                P.wait("tensor", ac, sc[("A1", 0)])
            for j in range(8):
                P.run("tensor", lambda t, j=j: t.transpose(
                    bf(psb[7][:, j * 64:(j + 1) * 64]), sdb8[:, j * 128:(j + 1) * 128], ident[:, :]), pe if j == 7 else None)
            sc[("SD", hb)] = pe.n
            scan_ac_batch(tc, 1, hb)
        else:
            P.wait("tensor", ac, sc[("A1", hb)])
            for j in range(8):
                h = hs + j
                bk = 4 + j // 4
                off = (j % 4) * 128
                P.run("tensor", lambda t, j=j, h=h, bk=bk, off=off, vt=vt: t.matmul(
                    psb[bk][:, off:off + 128], sdt8[:, j * 128:(j + 1) * 128], vt[:, h * 128:(h + 1) * 128],
                    start=True, stop=True), pe if j == 7 else None)
            sc[("SO", hb)] = pe.n
            scan_dv_batch(tc, 3, hb)
            if hb == 1:
                t_qlastuse[tc] = pe.n
                sc[("A0p", 1)] = sc[("A0", 1)]
                P.wait("sync", dv, t_scan_dv[tc])
                t_ospill[tc] = P.dma("sync", qOSP, d_oi[tc], otc)
                t_aspill[tc] = P.dma("sync", qASP, d_a[tc], atc)
                if d_odbg is not None:
                    P.dma("sync", qS, d_odbg[tc], otc)

    for tc in range(NTC):
        if tc >= 1:
            P.wait("sync", pe, t_kv_pe[tc - 1])
        if tc == 0:
            P.wait("sync", pe, t_mid_pe)
            P.wait("sync", qS, t_ln1_spill)
            for tok in t_qspill[-4:]:
                P.wait("sync", qS, tok)
            P.wait("sync", ac, t_wqc[NHC - 1])
        t_l1tc[tc] = P.dma("sync", qL1, ln1tc[:, :].rearrange("p (h c) -> p h c", c=128), bass.AP(
            d_ln1g, tc * NHC * 128 * 128, [[128, 128], [128 * 128, NHC], [1, 128]]))
        if tc >= 2:
            P.wait("sync", pe, t_qlastuse[tc - 2])
        t_qtc[tc] = P.dma("sync", qQT, qtc[tc % 2][:, :], bass.AP(
            d_q, (tc // 4) * NHC * 128 * 512 + (tc % 4) * 128, [[512, 128], [128 * 512, NHC], [1, 128]]))
        for gi in range(8):
            kv_group(tc, gi)
            kv_evac(tc, gi)
            if tc >= 1 and gi < 8:
                si = gi
                scan_pe_sub(tc - 1, si)
        if tc >= 1:
            scan_pe_sub(tc - 1, 8)
            scan_pe_sub(tc - 1, 9)
        t_kv_pe[tc] = sc[("kvmm", tc, 7)]
        if d_kdbg is not None:
            P.wait("sync", ac, t_kev[tc])
            P.dma("sync", qS, d_kdbg[tc], ktcr[tc % 2])
        if d_vdbg is not None:
            P.wait("sync", ac, t_vev[tc])
            P.dma("sync", qS, d_vdbg[tc], vtcr[tc % 2])
    for si in range(10):
        scan_pe_sub(NTC - 1, si)

    t_wo = [0] * NHC
    P.wait("gpsimd", dv, t_scan_dv[NTC - 1])
    P.wait("gpsimd", pe, t_qlastuse[NTC - 1])
    t_wg2l = P.dma("gpsimd", qWG2, wg2[:, :], d_wg2[:, :])
    for ec in range(NHC):
        t_wo[ec] = P.dma("gpsimd", qWO, wo_sb[ec], d_wo[ec * 128:(ec + 1) * 128, :])

    P.wait("scalar", dv, t_scan_dv[NTC - 1])
    P.run("scalar", lambda s: s.activation(out=ktcB, in_=tstate, func=AF.Copy), ac)
    t_tcast = ac.n
    P.wait("sync", ac, t_tcast)
    t_tspill = P.dma("sync", qTSPs, d_agt_i[:, :], ktcB)
    P.wait("gpsimd", qS, t_tspill)
    P.run("gpsimd", lambda g: g.collective_compute(
        "AllGather", ALU.bypass, replica_groups=[list(range(CORES))],
        ins=[d_agt_i.ap().opt()], outs=[d_agt_o.ap().opt()]), gp)
    t_agt = gp.n

    d_tdbg = dbgout("tst", [CORES * 128, H], BF16)
    if d_tdbg is not None:
        P.wait("sync", gp, t_agt)
        P.dma("sync", qS, d_tdbg[:, :], d_agt_o[:, :])
    P.wait("vector", pe, t_kv_pe[NTC - 1])
    P.run("vector", lambda v: v.memset(sinb, 0.0), dv)
    t_sin0 = dv.n
    t_peer = [0] * CORES
    t_sinr = [0] * CORES
    for r in range(CORES):
        P.wait("gpsimd", gp, t_agt)
        if r == 0:
            P.wait("gpsimd", pe, t_kv_pe[NTC - 1])
        if r >= 2:
            P.waitl("gpsimd", dv, t_sinr, r - 2)
        t_peer[r] = P.dma("gpsimd", qPEER, onbr[r % 2], d_agt_o[r * 128:(r + 1) * 128, :])
    for r in range(CORES):
        P.run("vector", lambda v, r=r: v.tensor_scalar_mul(out=coefr[:, :], in0=ld_rep[:, :], scalar1=gapr[:, r:r + 1]), dv)
        tcf = dv.n
        P.wait("scalar", dv, tcf)
        P.run("scalar", lambda s: s.activation(out=coefr[:, :], in_=coefr[:, :], func=AF.Exp, scale=2048.0), ac)
        tce = ac.n
        P.wait("vector", ac, tce)
        P.wait("vector", qG, t_peer[r])
        if r == 0:
            P.wait("vector", dv, t_sin0)
        for h in range(NH):
            P.run("vector", lambda v, r=r, h=h: v.scalar_tensor_tensor(
                out=sinb[:, h * 128:(h + 1) * 128], in0=onbr[r % 2][:, h * 128:(h + 1) * 128], scalar=coefr[:, h:h + 1],
                in1=sinb[:, h * 128:(h + 1) * 128], op0=ALU.mult, op1=ALU.add), dv if h == NH - 1 else None)
        t_sinr[r] = dv.n
    t_sin = dv.n
    P.wait("scalar", dv, t_sin)
    P.run("scalar", lambda s: s.activation(out=sbfb, in_=sinb, func=AF.Copy), ac)
    t_sbf = ac.n

    t_wofold = [0] * NHC
    for ec in range(NHC):
        P.wait("vector", qG, t_wo[NHC - 1])
        P.wait("vector", qS, t_nw)
        P.run("vector", lambda v, ec=ec: v.tensor_scalar_mul(out=wo_sb[ec][:, :], in0=wo_sb[ec][:, :], scalar1=nwcol[:, ec:ec + 1]), dv)
        t_wofold[ec] = dv.n

    d_y1dbg = dbgout("y1", [T, H])
    d_aodbg = dbgout("attnout", [T, H])
    t_ln2s = [0] * NTC
    t_y1sp = [0] * NTC
    t_sbf_tc = t_sbf
    t_supd = [0] * NTC
    t_oint = [0] * NTC
    t_gate = [0] * NTC
    t_trs_pe = [0] * NTC
    t_on = [0] * NTC
    t_gp_res = [0] * NTC
    onb3 = [onbr[0], onbr[1], peerb]
    t_msqL = [0] * NTC
    t_mg = {}
    t_gse = {}
    t_ont = {}
    t_aoev = {}
    t_bdv = [0] * NTC
    t_bac = [0] * NTC
    t_q2 = [0] * NTC
    t_oi2 = [0] * NTC
    t_a2 = [0] * NTC
    t_x2 = [0] * NTC

    def p2_loads(tc):
        if tc >= 2:
            P.wait("gpsimd", pe, t_oint[tc - 2])
        t_q2[tc] = P.dma("gpsimd", qQT, qtc[tc % 2][:, :], bass.AP(
            d_q, (tc // 4) * NHC * 128 * 512 + (tc % 4) * 128, [[512, 128], [128 * 512, NHC], [1, 128]]))
        if tc >= 2:
            P.wait("gpsimd", dv, t_mg[(tc - 2, 3)])
        t_oi2[tc] = P.dma("gpsimd", qOI, oitcr[tc % 2], d_oi[tc])
        if tc >= 2:
            P.wait("sync", dv, t_supd[tc - 2])
        t_a2[tc] = P.dma("sync", qA2, a2tcr[tc % 2], d_a[tc])

    def p2_xload(tc):
        if tc >= 2:
            P.wait("scalar", gp, t_gp_res[tc - 2])
        t_x2[tc] = P.dma("scalar", qX2, xtc2r[tc % 2], d_x[tc * 128:(tc + 1) * 128, :])

    def emit_A(tc):
        nonlocal t_sbf_tc
        sbf_prev = t_sbf_tc
        P.wait("vector", ac, sbf_prev)
        P.waitt("vector", t_a2[tc])
        for h in range(NH):
            P.run("vector", lambda v, tc=tc, h=h: v.scalar_tensor_tensor(
                out=sinb[:, h * 128:(h + 1) * 128], in0=sinb[:, h * 128:(h + 1) * 128], scalar=cds[:, h:h + 1],
                in1=a2tcr[tc % 2][:, h * 128:(h + 1) * 128], op0=ALU.mult, op1=ALU.add), dv if h == NH - 1 else None)
        t_supd[tc] = dv.n
        P.wait("tensor", ac, sbf_prev)
        P.waitt("tensor", t_q2[tc])
        if tc >= 1:
            P.wait("tensor", dv, t_mg[(tc - 1, 3)])
            P.wait("tensor", ac, t_gse[(tc - 1, 3)])
        for h in range(NH):
            P.run("tensor", lambda t, tc=tc, h=h: t.matmul(
                psb[h // 4][:, (h % 4) * 128:(h % 4 + 1) * 128], qtc[tc % 2][:, h * 128:(h + 1) * 128],
                sbfb[:, h * 128:(h + 1) * 128], start=True, stop=True), pe if h == NH - 1 else None)
        t_oint[tc] = pe.n
        P.wait("vector", pe, t_oint[tc])
        P.waitt("vector", t_oi2[tc])
        for h in range(NH):
            P.run("vector", lambda v, tc=tc, h=h: v.scalar_tensor_tensor(
                out=otc[:, h * 128:(h + 1) * 128], in0=psb[h // 4][:, (h % 4) * 128:(h % 4 + 1) * 128],
                scalar=qdecs[:, h:h + 1], in1=oitcr[tc % 2][:, h * 128:(h + 1) * 128],
                op0=ALU.mult, op1=ALU.add), dv if h == NH - 1 else None)
        t_mg[(tc, 3)] = dv.n
        P.wait("scalar", dv, t_supd[tc])
        P.wait("scalar", pe, t_oint[tc])
        P.run("scalar", lambda s: s.activation(out=sbfb, in_=sinb, func=AF.Copy), ac)
        t_sbf_tc = ac.n
        for pr in range(2):
            if pr == 0:
                P.wait("tensor", dv, t_mg[(tc, 3)])
                if tc == 0:
                    P.wait("tensor", qG, t_wg2l)
            else:
                P.wait("tensor", ac, t_gse[(tc, 1)])
            for es4 in (2 * pr, 2 * pr + 1):
                P.run("tensor", lambda t, tc=tc, es4=es4: t.matmul(
                    psb[2 + es4 % 2][:, :], midb[:, tc * 128:(tc + 1) * 128], wg2[:, es4 * 512:(es4 + 1) * 512],
                    start=True, stop=True), pe if es4 % 2 == 1 else None)
            tp = pe.n
            P.wait("scalar", pe, tp)
            for es4 in (2 * pr, 2 * pr + 1):
                P.run("scalar", lambda s, es4=es4: s.activation(
                    out=gsigb[:, es4 * 512:(es4 + 1) * 512], in_=psb[2 + es4 % 2][:, :], func=AF.Sigmoid), ac)
                t_gse[(tc, es4)] = ac.n
        t_gate[tc] = pe.n
        P.wait("vector", ac, t_gse[(tc, 3)])
        P.run("vector", lambda v: v.tensor_mul(out=otc[:, :], in0=otc[:, :], in1=gsigb[:, :]))
        if tc >= 3:
            P.wait("vector", pe, t_trs_pe[tc - 3])
        P.run("vector", lambda v, tc=tc: v.tensor_mul(out=onb3[tc % 3][:, :], in0=otc[:, :], in1=otc[:, :]))
        P.run("vector", lambda v, tc=tc: v.tensor_reduce(
            out=stat[:, 0:16], in_=onb3[tc % 3][:, :].rearrange("p (g c) -> p g c", c=128), axis=AXX, op=ALU.add), dv)
        t_msqL[tc] = dv.n

    def emit_A2(tc):
        P.wait("scalar", dv, t_msqL[tc])
        P.run("scalar", lambda s: s.activation(out=stat[:, 0:16], in_=stat[:, 0:16], func=AF.Ln, scale=1.0 / 128, bias=cepsA[:, 0:1]))
        P.run("scalar", lambda s: s.activation(out=stat[:, 0:16], in_=stat[:, 0:16], func=AF.Exp, scale=-0.5), ac)
        t_grs = ac.n
        P.wait("vector", ac, t_grs)
        for h in range(NH):
            P.run("vector", lambda v, tc=tc, h=h: v.tensor_scalar_mul(
                out=onb3[tc % 3][:, h * 128:(h + 1) * 128], in0=otc[:, h * 128:(h + 1) * 128],
                scalar1=stat[:, h:h + 1]), dv if h == NH - 1 else None)
        t_on[tc] = dv.n

    def emit_B(tc):
        aof = aofr[tc % 2]
        bln2 = bln2r[tc % 2]
        bstat = bstatr[tc % 2]
        P.wait("tensor", dv, t_on[tc])
        for hf in range(2):
            bank = 6 + hf
            if tc >= 1:
                P.wait("tensor", ac, t_ont[(tc - 1, hf)])
            for k in range(8):
                ec = hf * 8 + k
                P.run("tensor", lambda t, tc=tc, ec=ec, k=k, bank=bank: t.transpose(
                    bf(psb[bank][:, k * 64:(k + 1) * 64]), onb3[tc % 3][:, ec * 128:(ec + 1) * 128], ident[:, :]),
                    pe if k == 7 else None)
            tp = pe.n
            P.wait("scalar", pe, tp)
            P.run("scalar", lambda s, hf=hf, bank=bank: s.activation(
                out=ontbb[:, hf * 1024:(hf + 1) * 1024], in_=bf(psb[bank][:, 0:512]), func=AF.Copy), ac)
            t_ont[(tc, hf)] = ac.n
        t_trs_pe[tc] = pe.n
        P.wait("tensor", ac, t_ont[(tc, 1)])
        if tc == 0:
            P.wait("tensor", dv, t_scan_dv[NTC - 1])
            P.wait("tensor", dv, t_wofold[NHC - 1])
        bsc = bln2.bitcast(F32)
        for pr in range(2):
            if pr == 0 and tc >= 1:
                P.wait("tensor", ac, t_aoev[(tc - 1, 1)])
            elif pr == 1:
                P.wait("tensor", ac, t_aoev[(tc, 0)])
            for es4 in (2 * pr, 2 * pr + 1):
                for ec in range(NHC):
                    P.run("tensor", lambda t, ec=ec, es4=es4: t.matmul(
                        psb[4 + es4 % 2][:, :], ontbb[:, ec * 128:(ec + 1) * 128], wo_sb[ec][:, es4 * 512:(es4 + 1) * 512],
                        start=(ec == 0), stop=(ec == NHC - 1)), pe if (ec == NHC - 1 and es4 % 2 == 1) else None)
            t_mm = pe.n
            P.wait("scalar", pe, t_mm)
            if pr == 0 and tc >= 2:
                P.wait("scalar", dv, t_bdv[tc - 2])
                P.waitt("scalar", t_y1sp[tc - 2])
                P.waitt("scalar", t_ln2s[tc - 2])
            for es4 in (2 * pr, 2 * pr + 1):
                P.run("scalar", lambda s, es4=es4: s.activation(
                    out=aof[:, es4 * 512:(es4 + 1) * 512], in_=psb[4 + es4 % 2][:, :], func=AF.Copy, accum_out=bstat[:, es4:es4 + 1]))
                P.run("scalar", lambda s, es4=es4, bsc=bsc: s.activation(
                    out=bsc[:, 0:512], in_=psb[4 + es4 % 2][:, :], func=AF.Square, accum_out=bstat[:, 4 + es4:5 + es4]), ac)
            t_aoev[(tc, pr)] = ac.n
        if d_aodbg is not None:
            P.wait("sync", ac, t_aoev[(tc, 1)])
            tdbg = P.dma("sync", qS, d_aodbg[tc * 128:(tc + 1) * 128, :], aof)
            P.wait("scalar", qS, tdbg)
        P.run("scalar", lambda s, bsc=bsc: s.activation(out=bsc[:, 512:516], in_=bstat[:, 0:4], func=AF.Copy, scale=1.0 / H, accum_out=bstat[:, 10:11]))
        P.run("scalar", lambda s, bsc=bsc: s.activation(out=bsc[:, 516:520], in_=bstat[:, 4:8], func=AF.Copy, scale=1.0 / H, accum_out=bstat[:, 11:12]))
        P.run("scalar", lambda s: s.activation(out=bstat[:, 12:13], in_=bstat[:, 10:11], func=AF.Square), ac)
        t_mv = ac.n
        P.wait("vector", ac, t_mv)
        P.run("vector", lambda v: v.tensor_copy(out=dumf[:, 0:1], in_=dumf[:, 1:2]))
        P.run("vector", lambda v: v.tensor_scalar(out=bstat[:, 13:14], in0=bstat[:, 11:12], scalar1=LNEPS, scalar2=bstat[:, 12:13], op0=ALU.add, op1=ALU.subtract), dv)
        t_var = dv.n
        P.wait("scalar", dv, t_var)
        P.run("scalar", lambda s: s.activation(out=bstat[:, 14:15], in_=bstat[:, 13:14], func=AF.Ln))
        P.run("scalar", lambda s: s.activation(out=bstat[:, 15:16], in_=bstat[:, 14:15], func=AF.Exp, scale=-0.5), ac)
        t_r2 = ac.n
        P.wait("vector", ac, t_r2)
        P.run("vector", lambda v: v.tensor_scalar(out=bstat[:, 8:9], in0=bstat[:, 10:11], scalar1=bstat[:, 15:16], scalar2=-1.0, op0=ALU.mult, op1=ALU.mult))
        P.run("vector", lambda v: v.tensor_copy(out=dumf[:, 0:1], in_=dumf[:, 1:2]))
        P.run("vector", lambda v: v.tensor_scalar(out=aof[:, :], in0=aof[:, :], scalar1=bstat[:, 15:16], scalar2=bstat[:, 8:9], op0=ALU.mult, op1=ALU.add), dv)
        t_aff = dv.n
        P.wait("gpsimd", dv, t_aff)
        P.waitt("gpsimd", t_x2[tc])
        P.run("gpsimd", lambda g, tc=tc: g.tensor_add(out=aof, in0=aof, in1=xtc2r[tc % 2]), gp)
        t_gp_res[tc] = gp.n
        P.wait("scalar", gp, t_gp_res[tc])
        t_y1sp[tc] = P.dma("scalar", qY1S, d_y1[tc * 128:(tc + 1) * 128, :], aof)
        if d_y1dbg is not None:
            P.dma("sync", qS, d_y1dbg[tc * 128:(tc + 1) * 128, :], aof)
        P.wait("scalar", gp, t_gp_res[tc])
        P.run("scalar", lambda s: s.activation(out=ontbb, in_=aof, func=AF.Copy, scale=1.0 / H, accum_out=bstat[:, 0:1]))
        P.run("scalar", lambda s: s.activation(out=ontbb, in_=aof, func=AF.Square, scale=SQH, accum_out=bstat[:, 1:2]))
        P.run("scalar", lambda s: s.activation(out=bstat[:, 2:3], in_=bstat[:, 0:1], func=AF.Square), ac)
        t_s2 = ac.n
        P.wait("vector", ac, t_s2)
        P.run("vector", lambda v: v.tensor_copy(out=dumf[:, 0:1], in_=dumf[:, 1:2]))
        P.run("vector", lambda v: v.tensor_scalar(out=bstat[:, 3:4], in0=bstat[:, 1:2], scalar1=LNEPS, scalar2=bstat[:, 2:3], op0=ALU.add, op1=ALU.subtract), dv)
        t_v2 = dv.n
        P.wait("scalar", dv, t_v2)
        P.run("scalar", lambda s: s.activation(out=bstat[:, 5:6], in_=bstat[:, 3:4], func=AF.Ln))
        P.run("scalar", lambda s: s.activation(out=bstat[:, 6:7], in_=bstat[:, 5:6], func=AF.Exp, scale=-0.5), ac)
        t_r3 = ac.n
        P.wait("vector", ac, t_r3)
        P.run("vector", lambda v: v.tensor_scalar(out=bstat[:, 7:8], in0=bstat[:, 0:1], scalar1=bstat[:, 6:7], scalar2=-1.0, op0=ALU.mult, op1=ALU.mult))
        P.run("vector", lambda v: v.tensor_copy(out=dumf[:, 0:1], in_=dumf[:, 1:2]))
        P.run("vector", lambda v: v.tensor_scalar(out=bln2, in0=aof, scalar1=bstat[:, 6:7], scalar2=bstat[:, 7:8], op0=ALU.mult, op1=ALU.add), dv)
        t_bdv[tc] = dv.n
        t_bac[tc] = ac.n
        P.wait("sync", dv, t_bdv[tc])
        t_ln2s[tc] = P.dma("sync", qL2Sa if tc < 8 else qL2Sb, d_ln2[tc * 128:(tc + 1) * 128, :], bln2)
        if tc + 2 < NTC:
            p2_xload(tc + 2)

    P.wait("sync", dv, t_scan_dv[NTC - 1])
    P.wait("sync", pe, t_qlastuse[NTC - 1])
    P.wait("sync", qS, tokfull(qOSP))
    P.wait("sync", qS, tokfull(qASP))
    P.wait("sync", qS, t_tspill)
    P.wait("scalar", ac, t_tcast)
    P.wait("gpsimd", qS, tokfull(qOSP))
    p2_loads(0)
    p2_loads(1)
    p2_xload(0)
    p2_xload(1)
    for tc in range(NTC):
        emit_A(tc)
        if tc + 1 < NTC:
            p2_loads(tc + 1)
        if tc >= 2:
            emit_B(tc - 2)
        emit_A2(tc)
    emit_B(NTC - 2)
    emit_B(NTC - 1)

    t_half_dv = 0
    t_yev_prev_half = 0
    t_out_dv_half = 0
    t_out_fin = 0
    t_gu_last_half = 0
    for half in range(2):
        toff = half * 1024
        t_l2 = [0] * NHC
        P.wait("sync", qS, t_ln2s[half * 8 + 7])
        if half == 0:
            P.wait("sync", dv, t_bdv[NTC - 1])
            P.wait("sync", ac, t_bac[NTC - 1])
        else:
            P.wait("sync", pe, t_gu_last_half)
        for hc in range(NHC):
            t_l2[hc] = P.dma("sync", qL2F, ln2f[hc], d_ln2[toff:toff + 1024, hc * 128:(hc + 1) * 128], transpose=True)
        P.wait("gpsimd", gp, t_aggu)
        t_wgu = {}
        t_gu_pe = {}
        t_ha = {}

        def load_gu(ch):
            r, l = ch // GU_L, ch % GU_L
            if ch >= 2:
                P.wait("gpsimd", pe, t_gu_pe[ch - 2])
            elif half == 0:
                P.wait("gpsimd", dv, t_bdv[NTC - 1])
                P.wait("gpsimd", ac, t_bac[NTC - 1])
            else:
                P.wait("gpsimd", pe, t_gu_last_half)
            t_wgu[ch] = P.dma("gpsimd", qWGU, wgub[ch % 2][:, :], bass.AP(
                d_aggu_o, (r * 128) * (GU_L * 2048) + l * 2048, [[GU_L * 2048, 128], [1, 2048]]))

        load_gu(0)
        load_gu(1)
        first = True
        for ic in range(NIC):
            for j in range(2):
                ch = 2 * ic + j
                quad = (ic % 2) * 4 + j * 2
                P.wait("tensor", qG, t_wgu[ch])
                if first:
                    if half == 0:
                        P.wait("tensor", dv, t_bdv[NTC - 1])
                        P.wait("tensor", ac, t_bac[NTC - 1])
                    else:
                        P.wait("tensor", ac, t_yev_prev_half)
                    for hc in range(NHC):
                        P.wait("tensor", qS, t_l2[hc])
                    first = False
                if ic >= 2 and j == 0:
                    P.wait("tensor", dv, t_ha.get(ic - 2, 0))
                for es2 in range(2):
                    bank = quad + es2
                    for hc in range(NHC):
                        P.run("tensor", lambda t, ch=ch, hc=hc, es2=es2, bank=bank: t.matmul(
                            psb[bank][:, :], wgub[ch % 2][:, hc * 128:(hc + 1) * 128], ln2f[hc][:, es2 * 512:(es2 + 1) * 512],
                            start=(hc == 0), stop=(hc == NHC - 1)), pe if hc == NHC - 1 else None)
                t_gu_pe[ch] = pe.n
                t_gu_last_half = pe.n
                if ch + 2 < 2 * NIC:
                    load_gu(ch + 2)
            quad = (ic % 2) * 4
            P.wait("scalar", pe, t_gu_pe[2 * ic + 1])
            for es2 in range(2):
                P.run("scalar", lambda s, es2=es2, quad=quad: s.activation(out=glb[:, es2 * 512:(es2 + 1) * 512], in_=psb[quad + es2][:, :], func=AF.Silu), ac)
            t_sil = ac.n
            P.wait("vector", ac, t_sil)
            if half == 0 and ic == 0:
                P.wait("vector", gp, t_gp_res[NTC - 1])
                P.wait("vector", ac, t_bac[NTC - 1])
                P.wait("vector", qS, tokfull(qY1S))
                P.wait("vector", qS, tokfull(qL2Sa))
                P.wait("vector", qS, tokfull(qL2Sb))
            for es2 in range(2):
                P.run("vector", lambda v, ic=ic, es2=es2, quad=quad: v.tensor_mul(
                    out=hact[ic][:, es2 * 512:(es2 + 1) * 512], in0=glb[:, es2 * 512:(es2 + 1) * 512], in1=psb[quad + 2 + es2][:, :]), dv)
            t_ha[ic] = dv.n
        P.wait("gpsimd", gp, t_agdn)
        t_dn_pe = {}
        t_yev = {}

        def load_dn(idx):
            r8, icgq = idx // 4, idx % 4
            if idx >= 2:
                P.wait("gpsimd", pe, t_dn_pe[idx - 2])
            elif half == 0:
                P.wait("gpsimd", ac, t_bac[NTC - 1])
            else:
                P.wait("gpsimd", pe, t_dn_last_half)
            return P.dma("gpsimd", qWDN, wdnb[idx % 2][:, :], bass.AP(
                d_agdn_o, (r8 * 128) * (NIC * DN_COLS) + icgq * 11 * DN_COLS, [[NIC * DN_COLS, 128], [1, 11 * DN_COLS]]))

        t_wdn = {0: load_dn(0), 1: load_dn(1)}
        for r8 in range(8):
            for icgq in range(4):
                idx = r8 * 4 + icgq
                P.wait("tensor", qG, t_wdn[idx])
                if idx == 0:
                    P.wait("tensor", dv, t_ha[NIC - 1])
                    if half == 1:
                        P.wait("tensor", ac, t_yev_prev_half)
                if icgq == 0 and r8 > 0:
                    P.wait("tensor", ac, t_yev[r8 - 1])
                for ic2 in range(11):
                    ic = icgq * 11 + ic2
                    for tc8 in range(8):
                        P.run("tensor", lambda t, ic=ic, ic2=ic2, idx=idx, tc8=tc8: t.matmul(
                            psb[tc8][:, 0:256], hact[ic][:, tc8 * 128:(tc8 + 1) * 128], wdnb[idx % 2][:, ic2 * 256:(ic2 + 1) * 256],
                            start=(ic == 0), stop=(ic == NIC - 1), skip_group_check=True), pe if (ic2 == 10 and tc8 == 7) else None)
                t_dn_pe[idx] = pe.n
                if idx + 2 < 32:
                    t_wdn[idx + 2] = load_dn(idx + 2)
            t_dmm = pe.n
            P.wait("scalar", pe, t_dmm)
            if half == 1 and r8 == 0:
                P.wait("scalar", dv, t_out_dv_half)
            for tc8 in range(8):
                P.run("scalar", lambda s, tc8=tc8, r8=r8: s.activation(out=ysb[tc8][:, r8 * 256:(r8 + 1) * 256], in_=psb[tc8][:, 0:256], func=AF.Copy, accum_out=systat[:, tc8:tc8 + 1]))
                P.run("scalar", lambda s, tc8=tc8: s.activation(out=glf[:, 0:256], in_=psb[tc8][:, 0:256], func=AF.Square, accum_out=systat[:, 8 + tc8:9 + tc8]), ac)
            t_yev[r8] = ac.n
            P.wait("vector", ac, t_yev[r8])
            for tc8 in range(8):
                if r8 == 0:
                    P.run("vector", lambda v, tc8=tc8: v.tensor_copy(out=stat[:, tc8:tc8 + 1], in_=systat[:, tc8:tc8 + 1]))
                    P.run("vector", lambda v, tc8=tc8: v.tensor_copy(out=stat[:, 8 + tc8:9 + tc8], in_=systat[:, 8 + tc8:9 + tc8]))
                else:
                    P.run("vector", lambda v, tc8=tc8: v.tensor_add(out=stat[:, tc8:tc8 + 1], in0=stat[:, tc8:tc8 + 1], in1=systat[:, tc8:tc8 + 1]))
                    P.run("vector", lambda v, tc8=tc8: v.tensor_add(out=stat[:, 8 + tc8:9 + tc8], in0=stat[:, 8 + tc8:9 + tc8], in1=systat[:, 8 + tc8:9 + tc8]))
            P.run("vector", lambda v: v.engine_nop(), dv)
        t_dstat = dv.n
        t_yev_prev_half = t_yev[7]
        t_dn_last_half = t_dn_pe[31]
        d_yfdbg = dbgout("yffn", [NTC, 128, H], BF16)
        t_yfdump = [0] * 8
        if d_yfdbg is not None:
            for tc8 in range(8):
                P.wait("sync", ac, t_yev[7])
                t_yfdump[tc8] = P.dma("sync", qS, d_yfdbg[half * 8 + tc8], ysb[tc8][:, :])
        t_out_dv = 0
        for tc8 in range(8):
            tc = half * 8 + tc8
            if tc8 >= 1:
                P.wait("sync", dv, t_out_dv)
            elif half == 0:
                P.wait("sync", ac, t_bac[NTC - 1])
                P.wait("sync", qS, tokfull(qY1S))
            t_y1r = P.dma("sync", qY1R, y1rb, d_y1[tc * 128:(tc + 1) * 128, :])
            P.wait("vector", dv, t_dstat)
            P.run("vector", lambda v, tc8=tc8: v.tensor_scalar_mul(out=systat[:, 16:17], in0=stat[:, tc8:tc8 + 1], scalar1=1.0 / H))
            P.run("vector", lambda v, tc8=tc8: v.tensor_scalar_mul(out=systat[:, 17:18], in0=stat[:, 8 + tc8:9 + tc8], scalar1=1.0 / H))
            P.run("vector", lambda v: v.tensor_mul(out=systat[:, 18:19], in0=systat[:, 16:17], in1=systat[:, 16:17]))
            P.run("vector", lambda v: v.tensor_scalar(out=systat[:, 19:20], in0=systat[:, 17:18], scalar1=LNEPS, scalar2=FEPS * LNEPS, op0=ALU.mult, op1=ALU.add))
            P.run("vector", lambda v: v.tensor_sub(out=systat[:, 18:19], in0=systat[:, 17:18], in1=systat[:, 18:19]))
            P.run("vector", lambda v: v.tensor_copy(out=dumf[:, 0:1], in_=dumf[:, 1:2]))
            P.run("vector", lambda v: v.tensor_add(out=systat[:, 19:20], in0=systat[:, 18:19], in1=systat[:, 19:20]), dv)
            t_arg = dv.n
            P.wait("scalar", dv, t_arg)
            P.run("scalar", lambda s: s.activation(out=systat[:, 20:21], in_=systat[:, 19:20], func=AF.Ln))
            P.run("scalar", lambda s: s.activation(out=systat[:, 20:21], in_=systat[:, 20:21], func=AF.Exp, scale=-0.5), ac)
            t_rr = ac.n
            P.wait("vector", ac, t_rr)
            P.run("vector", lambda v: v.tensor_scalar(out=systat[:, 21:22], in0=systat[:, 16:17], scalar1=systat[:, 20:21], scalar2=-1.0, op0=ALU.mult, op1=ALU.mult))
            P.run("vector", lambda v: v.tensor_copy(out=dumf[:, 0:1], in_=dumf[:, 1:2]))
            if d_yfdbg is not None:
                P.wait("vector", qS, t_yfdump[tc8])
            P.run("vector", lambda v, tc8=tc8: v.tensor_scalar(out=ysb[tc8][:, :], in0=ysb[tc8][:, :], scalar1=systat[:, 20:21], scalar2=systat[:, 21:22], op0=ALU.mult, op1=ALU.add), dv)
            P.wait("vector", qS, t_y1r)
            P.run("vector", lambda v, tc8=tc8: v.tensor_add(out=y1rb, in0=ysb[tc8][:, :], in1=y1rb), dv)
            t_out_dv = dv.n
            P.wait("sync", dv, t_out_dv)
            t_out_fin = P.dma("sync", qS, d_out[tc * 128:(tc + 1) * 128, :], y1rb)
        t_out_dv_half = t_out_dv

    d_qdbg = dbgout("qq", [NTC * NHC * 128, 128], BF16)
    if d_qdbg is not None:
        P.wait("gpsimd", qS, qS.n)
        P.dma("gpsimd", qG, d_qdbg[:, :], d_q[:, :])
    d_ln2dbg = dbgout("ln2x", [T, H], BF16)
    if d_ln2dbg is not None:
        P.wait("gpsimd", qS, qS.n)
        P.dma("gpsimd", qG, d_ln2dbg[:, :], d_ln2[:, :])
    P.wait("gpsimd", qS, qS.n)
    P.wait("gpsimd", qG, qG.n)
    P.wait("gpsimd", qA, qA.n)

    with nc.Block() as block:
        @block.sync
        def _(e):
            for fn in P.ops["sync"]:
                fn(e)

        @block.scalar
        def _(e):
            for fn in P.ops["scalar"]:
                fn(e)

        @block.vector
        def _(e):
            for fn in P.ops["vector"]:
                fn(e)

        @block.tensor
        def _(e):
            for fn in P.ops["tensor"]:
                fn(e)

        @block.gpsimd
        def _(e):
            for fn in P.ops["gpsimd"]:
                fn(e)

    return nc


def _colperm_gu():
    cp = np.empty(2 * II, dtype=np.int64)
    for ic in range(NIC):
        blk = PERM2[ic * 128:(ic + 1) * 128]
        cp[(2 * ic) * 128:(2 * ic + 1) * 128] = blk
        cp[(2 * ic + 1) * 128:(2 * ic + 2) * 128] = II + blk
    return cp


def prep_inputs(L0_attn_x, L0_attn_log_decay, L0_attn_Wq, L0_attn_Wk, L0_attn_Wv,
                L0_attn_Wo, L0_attn_Wg1, L0_attn_Wg2, L0_attn_norm_w,
                L0_ffn_gate_up_packed, L0_ffn_gate_up_scales, L0_ffn_down_packed,
                L0_ffn_down_scales):
    perm = PERM
    cpgu = _colperm_gu()
    wq = np.ascontiguousarray(np.asarray(L0_attn_Wq, dtype=np.float32)[perm, :])
    wk = np.ascontiguousarray(np.asarray(L0_attn_Wk, dtype=np.float32)[perm, :])
    wv = np.ascontiguousarray(np.asarray(L0_attn_Wv, dtype=np.float32)[perm, :])
    wo = np.ascontiguousarray(np.asarray(L0_attn_Wo, dtype=np.float32)[:, perm])
    wg1 = np.ascontiguousarray(np.asarray(L0_attn_Wg1, dtype=np.float32)[perm, :])
    wg2 = np.ascontiguousarray(np.asarray(L0_attn_Wg2, dtype=np.float32))
    nw = np.ascontiguousarray(np.asarray(L0_attn_norm_w, dtype=np.float32).reshape(H, 1))
    ldrow = np.ascontiguousarray(np.asarray(L0_attn_log_decay, dtype=np.float32).reshape(1, NH))

    gu_b = np.asarray(L0_ffn_gate_up_packed, dtype=np.int32).view(np.uint8).reshape(2 * II, 1024)
    gu_bt = gu_b.T[:, cpgu]
    gu_sc = np.asarray(L0_ffn_gate_up_scales)[:, cpgu]

    dn_b = np.asarray(L0_ffn_down_packed, dtype=np.int32).view(np.uint8).reshape(H, 2816)
    dn_bt = dn_b[perm, :].T
    dn_sc = np.asarray(L0_ffn_down_scales)[:, perm]

    grow = (np.arange(128)[None, :] >= 64).astype(np.int64)
    x = np.asarray(L0_attn_x, dtype=np.float32)
    in_maps = []
    for c in range(CORES):
        b, qt = c // 4, c % 4
        xs = np.ascontiguousarray(x[b, qt * T:(qt + 1) * T, :][:, perm])
        sl = gu_bt[:, c * GU_COLS:(c + 1) * GU_COLS]
        pk = np.ascontiguousarray(sl.reshape(8, 128, GU_L, 128).transpose(2, 1, 0, 3).reshape(GU_L, 128, 1024))
        g_idx = (2 * np.arange(8)[:, None] + grow).astype(np.int64)
        scs = gu_sc[:, c * GU_COLS:(c + 1) * GU_COLS]
        sc_lpuc = np.ascontiguousarray(
            scs[g_idx].reshape(8, 128, GU_L, 128).transpose(2, 1, 0, 3).reshape(GU_L, 128, 1024).astype(np.float16))
        dsl = dn_bt[:, c * DN_COLS:(c + 1) * DN_COLS]
        pkd = np.ascontiguousarray(dsl.reshape(DN_U, 128, DN_COLS))
        g2 = (2 * np.arange(DN_U)[:, None] + grow).astype(np.int64)
        dscs = dn_sc[:, c * DN_COLS:(c + 1) * DN_COLS]
        sc_d = np.ascontiguousarray(dscs[g2].astype(np.float16))
        gap = np.full((1, CORES), 1e6, dtype=np.float32)
        for r in range(CORES):
            br, qr = r // 4, r % 4
            if br == b and qr < qt:
                gap[0, r] = float(qt - 1 - qr)
        in_maps.append({
            "x": xs, "wq": wq, "wk": wk, "wv": wv, "wo": wo, "wg1": wg1, "wg2": wg2,
            "nw": nw, "ldec": ldrow, "gap": gap,
            "pkgu": pk, "scgu": sc_lpuc, "pkdn": pkd, "scdn": sc_d,
        })
    return in_maps


def kernel(**inputs):
    in_maps = prep_inputs(**inputs)
    es = ExitStack()
    try:
        nc = build(es)
        res = run_bass_kernel_spmd(nc, in_maps, core_ids=list(range(CORES)))
    finally:
        es.close()
    inv = np.argsort(PERM)
    out = np.empty((2, 8192, H), dtype=np.float32)
    for c in range(CORES):
        b, qt = c // 4, c % 4
        out[b, qt * T:(qt + 1) * T, :] = res.results[c]["out"][:, inv]
    return out
